# version 03
# baseline (speedup 1.0000x reference)
import math
from contextlib import ExitStack
import numpy as np
import ml_dtypes
import concourse.bass as bass
import concourse.mybir as mybir
from concourse.ap import AP as BAP
from concourse.bass_utils import run_bass_kernel_spmd

F32 = mybir.dt.float32
BF16 = mybir.dt.bfloat16
I32 = mybir.dt.int32
AF = mybir.ActivationFunctionType
ALU = mybir.AluOpType
AX = mybir.AxisListType

T = 4096
D = 1024
TB = 512
NTB = T // TB
NKD = D // 128


QUEUES = ("sync", "tensor", "vector", "scalar", "gpsimd")
PARANOID = True
PARANOID_ENGINES = {"vector", "gpsimd", "scalar"}
DMA_SLOTS = 8
_DT_SZ = {F32: 4, BF16: 2, I32: 4}


def _ap_box(ap):
    t = ap.tensor
    name = t.name
    space = str(ap.space)
    esz = _DT_SZ.get(ap.dtype, 4)
    dims = list(ap.ap)
    off = int(ap.offset)
    if "SB" not in space and "PSUM" not in space and "Psum" not in space:
        lo = hi = 0
        for step, count in dims:
            d = int(step) * (int(count) - 1)
            if d >= 0:
                hi += d
            else:
                lo += d
        return (name, "D", 0, 0, (off + lo) * esz, (off + hi + 1) * esz)
    if "PSUM" in space or "Psum" in space:
        return (name, "S", 0, 127, 0, 1 << 30)
    tshape = list(t.shape)
    row_elems_t = 1
    for s in tshape[1:]:
        row_elems_t *= int(s)
    row_bytes = row_elems_t * _DT_SZ.get(t.dtype, 4)
    if row_bytes % esz:
        return (name, "S", 0, 127, 0, max(row_bytes, 4))
    row_elems = row_bytes // esz
    try:
        base_p = int(t.base_partition)
    except Exception:
        base_p = 0
    p_lo = p_hi = b_lo = b_hi = 0
    ok = True
    for step, count in dims:
        step = int(step)
        count = int(count)
        d = step * (count - 1)
        if step != 0 and step % row_elems == 0:
            dp = (step // row_elems) * (count - 1)
            if dp >= 0:
                p_hi += dp
            else:
                p_lo += dp
        elif abs(d) < row_elems:
            if d >= 0:
                b_hi += d
            else:
                b_lo += d
        else:
            ok = False
            break
    if not ok:
        return (name, "S", 0, 127, 0, row_bytes)
    p0 = off // row_elems
    c0 = off % row_elems
    return (name, "S", base_p + p0 + p_lo, base_p + p0 + p_hi,
            (c0 + b_lo) * esz, (c0 + b_hi) * esz + esz)


def _overlap(a, b):
    if a[0] != b[0]:
        return False
    if a[1] == "S" and (a[3] < b[2] or b[3] < a[2]):
        return False
    return not (a[5] <= b[4] or b[5] <= a[4])


class _Op:
    __slots__ = ("idx", "queue", "emit", "reads", "writes", "dma", "inc",
                 "waits", "scalar_reads", "tag", "semkey")

    def __init__(self, idx, queue, emit, reads, writes, dma, scalar_reads, tag):
        self.idx = idx
        self.queue = queue
        self.emit = emit
        self.reads = reads
        self.writes = writes
        self.dma = dma
        self.inc = False
        self.waits = []
        self.scalar_reads = scalar_reads
        self.tag = tag


class Px:

    def __init__(self, nc):
        self.nc = nc
        self.ops = []
        self.edges = []
        self._hist = {}
        self.n_bar = 0
        self._epoch_ops = 0

    def _boxes(self, aps):
        return [_ap_box(a) for a in aps if isinstance(a, BAP)]

    def _rec(self, queue, emit, reads=(), writes=(), dma=False,
             scalar_reads=(), tag=""):
        if self._epoch_ops >= 3000:
            self.barrier()
        self._epoch_ops += 1
        idx = len(self.ops)
        op = _Op(idx, queue, emit, self._boxes(reads), self._boxes(writes),
                 dma, self._boxes(scalar_reads), tag)
        dep = set()
        for box in op.reads + op.scalar_reads:
            for (j, b2, w2) in self._hist.get(box[0], ()):
                if w2 and _overlap(box, b2):
                    dep.add(j)
        for box in op.writes:
            for (j, b2, w2) in self._hist.get(box[0], ()):
                if _overlap(box, b2):
                    dep.add(j)
        for j in dep:
            p = self.ops[j]
            same = p.queue == queue and not p.dma and not dma
            if same and not PARANOID and queue not in PARANOID_ENGINES:
                continue
            self.edges.append((j, idx, same))
        for box in op.scalar_reads:
            for (j, b2, w2) in self._hist.get(box[0], ()):
                if w2 and _overlap(box, b2) and self.ops[j].queue == queue \
                        and not self.ops[j].dma:
                    self.edges.append((j, idx, True))
        for box in op.writes:
            self._hist.setdefault(box[0], []).append((idx, box, True))
        for box in op.reads + op.scalar_reads:
            self._hist.setdefault(box[0], []).append((idx, box, False))
        self.ops.append(op)
        return op

    def barrier(self):
        idx = len(self.ops)
        self.ops.append(_Op(idx, "BARRIER", None, [], [], False, [], "bar"))
        self._hist = {}
        self.n_bar += 1
        self._epoch_ops = 0

    def dma(self, q, out, in_, tag="", **kw):
        return self._rec(q, lambda e: e.dma_start(out=out, in_=in_, **kw),
                         reads=[in_], writes=[out], dma=True, tag=tag)

    def mm(self, out, lhsT, rhs, start=True, stop=True, tile_position=None,
           tag=""):
        return self._rec(
            "tensor",
            lambda e: e.matmul(out, lhsT, rhs, start=start, stop=stop,
                               tile_position=tile_position),
            reads=[lhsT, rhs], writes=[out], tag=tag)

    def transpose(self, out, in_, ident, tag=""):
        return self._rec("tensor", lambda e: e.transpose(out, in_, ident),
                         reads=[in_, ident], writes=[out], tag=tag)

    def act(self, out, in_, func, bias=0.0, scale=1.0, accum_out=None, tag=""):
        sreads = [b for b in (bias, scale) if isinstance(b, BAP)]
        writes = [out] + ([accum_out] if accum_out is not None else [])
        return self._rec(
            "scalar",
            lambda e: e.activation(out, in_, func, bias=bias, scale=scale,
                                   accum_out=accum_out),
            reads=[in_], writes=writes, scalar_reads=sreads, tag=tag)

    def tt(self, eng, out, in0, in1, op, tag=""):
        return self._rec(eng, lambda e: e.tensor_tensor(out, in0, in1, op),
                         reads=[in0, in1], writes=[out], tag=tag)

    def ts(self, eng, out, in0, s1, s2=None, op0=ALU.mult, op1=ALU.bypass,
           tag=""):
        sreads = [s for s in (s1, s2) if isinstance(s, BAP)]
        return self._rec(
            eng, lambda e: e.tensor_scalar(out, in0, s1, s2, op0, op1),
            reads=[in0], writes=[out], scalar_reads=sreads, tag=tag)

    def stt(self, eng, out, in0, scalar, in1, op0, op1, tag=""):
        sreads = [scalar] if isinstance(scalar, BAP) else []
        return self._rec(
            eng,
            lambda e: e.scalar_tensor_tensor(out, in0, scalar, in1, op0, op1),
            reads=[in0, in1], writes=[out], scalar_reads=sreads, tag=tag)

    def red(self, eng, out, in_, op, tag=""):
        return self._rec(eng, lambda e: e.tensor_reduce(out, in_, AX.X, op),
                         reads=[in_], writes=[out], tag=tag)

    def scan(self, out, d0, d1, init, op0=ALU.mult, op1=ALU.add, tag=""):
        sreads = [init] if isinstance(init, BAP) else []
        return self._rec(
            "vector",
            lambda e: e.tensor_tensor_scan(out, d0, d1, init, op0, op1),
            reads=[d0, d1], writes=[out], scalar_reads=sreads, tag=tag)

    def copy(self, eng, out, in_, tag=""):
        if eng == "scalar":
            return self._rec(eng, lambda e: e.copy(out, in_),
                             reads=[in_], writes=[out], tag=tag)
        return self._rec(eng, lambda e: e.tensor_copy(out=out, in_=in_),
                         reads=[in_], writes=[out], tag=tag)

    def memset(self, eng, ap_, v, tag=""):
        return self._rec(eng, lambda e: e.memset(ap_, v),
                         reads=[], writes=[ap_], tag=tag)

    def recip(self, out, in_, tag=""):
        return self._rec("vector", lambda e: e.reciprocal(out, in_),
                         reads=[in_], writes=[out], tag=tag)

    def iota(self, out, pattern, base=None, channel_multiplier=None, tag=""):
        return self._rec(
            "gpsimd",
            lambda e: e.iota(out, pattern, base=base,
                             channel_multiplier=channel_multiplier,
                             allow_small_or_imprecise_dtypes=True),
            reads=[], writes=[out], tag=tag)

    def finalize(self):
        last_c = {}
        forced = []
        for op in self.ops:
            if op.queue == "BARRIER":
                forced.extend(last_c.values())
                last_c = {}
            elif not op.dma:
                last_c[op.queue] = op.idx
        forced.extend(last_c.values())
        for j in forced:
            self.ops[j].inc = True
        for (j, _i, _f) in self.edges:
            self.ops[j].inc = True
        for op in self.ops:
            if op.dma:
                op.inc = True
        inc_count = {}
        dma_idx = {}
        op_inc_val = {}
        q_ops = {q: [] for q in QUEUES}
        for op in self.ops:
            if op.queue == "BARRIER":
                snap = dict(inc_count)
                for q in QUEUES:
                    q_ops[q].append(("BAR", snap))
                continue
            if op.dma:
                i = dma_idx.get(op.queue, 0)
                dma_idx[op.queue] = i + 1
                s = i % DMA_SLOTS
                j = i // DMA_SLOTS
                key = ("d", op.queue, s)
                op.semkey = key
                inc_count[key] = 16 * (j + 1)
                op_inc_val[op.idx] = (key, 16 * (j + 1))
                if j > 0:
                    op.waits.append((key, 16 * j, True))
            else:
                key = ("c", op.queue)
                if op.inc:
                    inc_count[key] = inc_count.get(key, 0) + 1
                    op_inc_val[op.idx] = (key, inc_count[key])
            q_ops[op.queue].append(("OP", op))
        for (j, i, f) in self.edges:
            key, val = op_inc_val[j]
            self.ops[i].waits.append((key, val, f))
        used_keys = set(inc_count.keys())
        sems = {}
        self._sem_guards = []
        for k in sorted(used_keys, key=str):
            gcm = self.nc.semaphore("_".join(str(p) for p in k))
            sems[k] = gcm.__enter__()
            self._sem_guards.append(gcm)
        return sems, q_ops

    def emit_queue(self, qname, eng, sems, q_ops):
        waited = {}

        def _w(key, val):
            if val <= waited.get(key, 0):
                return
            eng.wait_ge(sems[key], val)
            waited[key] = val

        for kind, payload in q_ops[qname]:
            if kind == "BAR":
                for key, val in payload.items():
                    if key == ("c", qname):
                        continue
                    _w(key, val)
                continue
            op = payload
            for w in op.waits:
                key, val = w[0], w[1]
                forced = w[2] if len(w) > 2 else False
                if key == ("c", qname) and not op.dma and not forced:
                    continue
                _w(key, val)
            inst = op.emit(eng)
            if op.inc:
                if op.dma:
                    inst.then_inc(sems[op.semkey], 16)
                else:
                    inst.then_inc(sems[("c", op.queue)], 1)


def emit_program(nc, px):
    sems, q_ops = px.finalize()
    with nc.Block() as block:
        @block.sync
        def _(e):
            px.emit_queue("sync", e, sems, q_ops)

        @block.tensor
        def _(e):
            px.emit_queue("tensor", e, sems, q_ops)

        @block.vector
        def _(e):
            px.emit_queue("vector", e, sems, q_ops)

        @block.scalar
        def _(e):
            px.emit_queue("scalar", e, sems, q_ops)

        @block.gpsimd
        def _(e):
            px.emit_queue("gpsimd", e, sems, q_ops)



class Ctx:
    pass


def make_psum_pool(nc):
    pool = []
    guards = []
    for i in range(8):
        gcm = nc.psum_tensor(f"pbank{i}", [128, TB], F32)
        pool.append(gcm.__enter__())
        guards.append(gcm)
    return pool, guards



def build_l0f_consts():
    HD, W, n = 64, 256, T
    half = np.arange(0, HD, 2, dtype=np.float64)
    inv_freq = 1.0 / 10000.0 ** (half / HD)
    t = np.arange(n, dtype=np.float64)
    freqs = t[:, None] * inv_freq[None, :]
    freqs = np.concatenate([freqs, freqs], -1)
    base_scale = (half + 0.4 * HD) / (1.4 * HD)
    power = (t - n // 2) / (W // 2)
    scale = base_scale[None, :] ** power[:, None]
    scale = np.concatenate([scale, scale], -1)
    cos = np.cos(freqs); sin = np.sin(freqs)
    rot = np.stack([
        np.tile((cos * scale).T, (2, 1)),
        np.tile((sin * scale).T, (2, 1)),
        np.tile((cos / scale).T, (2, 1)),
        np.tile((sin / scale).T, (2, 1)),
    ]).astype(np.float32)
    R = np.zeros((128, 128), np.float32)
    for blk in range(2):
        b = blk * 64
        for i in range(32):
            R[b + 32 + i, b + i] = -1.0
            R[b + i, b + 32 + i] = 1.0
    msk = np.zeros((2, 2, 128, 512), np.float32)
    p = np.arange(128)[:, None]; c = np.arange(512)[None, :]
    for qs in range(2):
        valid_n = (c >= qs * 128 + p) & (c <= qs * 128 + p + 256)
        msk[0, qs] = np.where(valid_n, 0.0, -1e30)
        valid_0 = (c >= 256) & (c <= 256 + qs * 128 + p)
        msk[1, qs] = np.where(valid_0, 0.0, -1e30)
    return rot, R, msk


def build_consts(offsets):
    c = {}
    c["ident"] = np.eye(128, dtype=np.float32)
    indP = np.zeros((8, 128, 16), np.float32)
    indT = np.zeros((8, 16, 128), np.float32)
    for pg in range(8):
        for p in range(128):
            h = 2 * pg + p // 64
            indP[pg, p, h] = 1.0
            indT[pg, h, p] = 1.0
    c["indP"] = indP
    c["indT"] = indT
    c["ones1"] = np.ones((1, 128), np.float32)
    c["ones_col"] = np.ones((128, 1), np.float32)
    rot, R, msk = build_l0f_consts()
    c["rot"] = rot
    c["R"] = R
    c["l0f_mask"] = msk
    selBC = np.zeros((96, 32 * 128), np.float32)
    for n in range(16):
        selBC[64 + n, (2 * n) * 128:(2 * n + 1) * 128] = 1.0
        selBC[80 + n, (2 * n + 1) * 128:(2 * n + 2) * 128] = 1.0
    c["selBC"] = selBC
    sel8 = np.zeros((8, 8 * 128), np.float32)
    for e in range(8):
        sel8[e, e * 128:(e + 1) * 128] = 1.0
    c["sel8"] = sel8
    return c



def stage_layernorm(px, nc, g, src_f32, dst_bf, dst_f32, name, eps=1e-5):
    with (
        nc.sbuf_tensor(name + "_x", [128, NKD * TB], F32) as xt,
        nc.sbuf_tensor(name + "_xb", [128, NKD * TB], BF16) as xbt,
        nc.sbuf_tensor(name + "_sq", [128, NKD * TB], BF16) as sqt,
        nc.sbuf_tensor(name + "_row", [1, 6 * TB], F32) as row,
        nc.sbuf_tensor(name + "_o", [128, NKD * TB], F32) as ot,
        nc.sbuf_tensor(name + "_ob", [128, NKD * TB], BF16) as otb,
    ):
        SUM, SQ, VAR, RSTD = 0, TB, 2 * TB, 3 * TB
        for tb in range(NTB):
            cs = slice(tb * TB, tb * TB + TB)
            for k in range(NKD):
                sl = slice(k * TB, (k + 1) * TB)
                px.dma("sync", xt[:, sl], src_f32[k * 128:(k + 1) * 128, cs])
                px.dma("gpsimd", xbt[:, sl], src_f32[k * 128:(k + 1) * 128, cs])
                px.act(sqt[:, sl], xbt[:, sl], AF.Square)
            ps_s = g.pool[0]
            ps_q = g.pool[1]
            for k in range(NKD):
                sl = slice(k * TB, (k + 1) * TB)
                px.mm(ps_s[0:1, :], g.ones_col_bf[:, 0:1], xbt[:, sl],
                      start=(k == 0), stop=(k == NKD - 1))
            for k in range(NKD):
                sl = slice(k * TB, (k + 1) * TB)
                px.mm(ps_q[0:1, :], g.ones_col_bf[:, 0:1], sqt[:, sl],
                      start=(k == 0), stop=(k == NKD - 1))
            px.act(row[:, SUM:SUM + TB], ps_s[0:1, :], AF.Copy, scale=1.0 / D)
            px.act(row[:, SQ:SQ + TB], ps_q[0:1, :], AF.Copy, scale=1.0 / D)
            px.tt("vector", row[:, VAR:VAR + TB], row[:, SUM:SUM + TB],
                  row[:, SUM:SUM + TB], ALU.mult)
            px.tt("vector", row[:, VAR:VAR + TB], row[:, SQ:SQ + TB],
                  row[:, VAR:VAR + TB], ALU.subtract)
            px.act(row[:, RSTD:RSTD + TB], row[:, VAR:VAR + TB], AF.Ln,
                   bias=eps)
            px.act(row[:, RSTD:RSTD + TB], row[:, RSTD:RSTD + TB], AF.Exp,
                   scale=-0.5)
            if getattr(g, "ln_debug", None) is not None:
                px.dma("sync", g.ln_debug[tb, 0][None, :], row[0:1, SUM:SUM + TB])
                px.dma("sync", g.ln_debug[tb, 1][None, :], row[0:1, SQ:SQ + TB])
                px.dma("sync", g.ln_debug[tb, 2][None, :], row[0:1, VAR:VAR + TB])
                px.dma("sync", g.ln_debug[tb, 3][None, :], row[0:1, RSTD:RSTD + TB])
            ps_m = g.pool[2]
            ps_r = g.pool[3]
            px.mm(ps_m[:, :], g.ones1_f32[:, :], row[:, SUM:SUM + TB])
            px.mm(ps_r[:, :], g.ones1_f32[:, :], row[:, RSTD:RSTD + TB])
            for k in range(NKD):
                sl = slice(k * TB, (k + 1) * TB)
                px.tt("vector", ot[:, sl], xt[:, sl], ps_m[:, :], ALU.subtract)
                px.tt("vector", ot[:, sl], ot[:, sl], ps_r[:, :], ALU.mult)
                px.copy("scalar", otb[:, sl], ot[:, sl])
                if dst_f32 is not None:
                    px.dma("sync", dst_f32[k * 128:(k + 1) * 128, cs],
                           ot[:, sl])
                px.dma("gpsimd", dst_bf[k * 128:(k + 1) * 128, cs],
                       otb[:, sl])
        px.barrier()



def kchunks(K):
    out = []
    k0 = 0
    while k0 < K:
        out.append((k0, min(128, K - k0)))
        k0 += 128
    return out


def load_w(px, sb, dram_w, K, M, tag="w"):
    for i, (k0, kk) in enumerate(kchunks(K)):
        px.dma("gpsimd", sb[:kk, i * M:i * M + M],
               dram_w[k0:k0 + kk, :], tag=tag)


def load_vec_cols(px, sb, dram_v, K, col0=0, tag="v"):
    for i, (k0, kk) in enumerate(kchunks(K)):
        px.dma("sync", sb[:kk, col0 + i:col0 + i + 1],
               dram_v[k0:k0 + kk][:, None], tag=tag)



def stage_l0a_proj(px, nc, g):
    with (
        nc.sbuf_tensor("ap_wqkv", [128, 8 * 3072], BF16) as wqkv,
        nc.sbuf_tensor("ap_wg", [128, 8 * 1024], BF16) as wg,
        nc.sbuf_tensor("ap_cols", [128, 16], F32) as cols,
        nc.sbuf_tensor("ap_xn", [128, 8 * TB], BF16) as xn,
        nc.sbuf_tensor("ap_xa", [128, 8 * TB], BF16) as xa,
        nc.sbuf_tensor("ap_sq", [128, 8 * TB], BF16) as sq,
        nc.sbuf_tensor("ap_row", [1, 2 * TB], F32) as row,
        nc.sbuf_tensor("ap_tmp", [128, TB], F32) as tmp,
        nc.sbuf_tensor("ap_ob", [128, 4 * TB], BF16) as ob,
    ):
        load_w(px, wqkv, g.w_qkv, D, 3072)
        load_w(px, wg, g.w_gate, D, D)
        load_vec_cols(px, cols, g.norm1_scale, D, 0)
        load_vec_cols(px, cols, g.b_gate, D, 8)
        for tb in range(NTB):
            cs = slice(tb * TB, (tb + 1) * TB)
            for k in range(NKD):
                sl = slice(k * TB, (k + 1) * TB)
                px.dma("sync", xn[:, sl], g.xn_bf[k * 128:(k + 1) * 128, cs])
                px.act(sq[:, sl], xn[:, sl], AF.Square)
            ps = g.pool[6]
            for k in range(NKD):
                px.mm(ps[0:1, :], g.ones_col_bf[:, 0:1],
                      sq[:, k * TB:(k + 1) * TB],
                      start=(k == 0), stop=(k == NKD - 1))
            px.act(row[:, 0:TB], ps[0:1, :], AF.Ln, bias=1e-6, scale=1.0 / D)
            px.act(row[:, 0:TB], row[:, 0:TB], AF.Exp, scale=-0.5)
            ps_r = g.pool[7]
            px.mm(ps_r[:, :], g.ones1_f32[:, :], row[:, 0:TB])
            for k in range(NKD):
                sl = slice(k * TB, (k + 1) * TB)
                px.tt("vector", tmp[:, :], xn[:, sl], ps_r[:, :], ALU.mult)
                px.ts("vector", xa[:, sl], tmp[:, :], cols[:, k:k + 1], None,
                      ALU.mult)
            for m in range(24):
                bank = g.pool[m % 4]
                for k in range(NKD):
                    px.mm(bank[:, :], wqkv[:, k * 3072 + m * 128:
                                           k * 3072 + (m + 1) * 128],
                          xa[:, k * TB:(k + 1) * TB],
                          start=(k == 0), stop=(k == NKD - 1))
                dst = (g.q_s, g.k_s, g.v_s)[m // 8]
                mm = m % 8
                o = ob[:, (m % 4) * TB:(m % 4 + 1) * TB]
                px.act(o, bank[:, :], AF.Copy)
                px.dma("gpsimd", dst[mm * 128:(mm + 1) * 128, cs], o)
            for m in range(NKD):
                bank = g.pool[4 + m % 2]
                for k in range(NKD):
                    px.mm(bank[:, :], wg[:, k * D + m * 128:
                                         k * D + (m + 1) * 128],
                          xa[:, k * TB:(k + 1) * TB],
                          start=(k == 0), stop=(k == NKD - 1))
                o = ob[:, (m % 4) * TB:(m % 4 + 1) * TB]
                px.act(o, bank[:, :], AF.Sigmoid, bias=cols[:, 8 + m:9 + m])
                px.dma("gpsimd", g.gate_s[m * 128:(m + 1) * 128, cs], o)
        px.barrier()



def stage_l0a_attn(px, nc, g):
    offs = g.offsets
    max_seg = getattr(g, "aa_max_seg", NTB)
    do_sm = getattr(g, "aa_do_sm", True)
    do_av = getattr(g, "aa_do_av", True)
    NO = len(offs)
    WIN = 2048
    assert max(offs) + TB <= WIN and min(offs) >= 0
    NOC = NO * 16
    with (
        nc.sbuf_tensor("aa_q", [128, 8 * TB], BF16) as q,
        nc.sbuf_tensor("aa_kw", [128, 8 * WIN], BF16) as kw,
        nc.sbuf_tensor("aa_vw", [128, 8 * WIN], BF16) as vw,
        nc.sbuf_tensor("aa_gt", [128, 8 * TB], BF16) as gt,
        nc.sbuf_tensor("aa_P", [128, 4 * TB], BF16) as Pb,
        nc.sbuf_tensor("aa_sc", [16, TB], BF16) as scb,
        nc.sbuf_tensor("aa_S", [128, 4 * NOC], F32) as Sb,
        nc.sbuf_tensor("aa_al", [128, 4 * NOC], BF16) as al,
        nc.sbuf_tensor("aa_m16", [128, 128], F32) as m16,
        nc.sbuf_tensor("aa_pb", [128, NOC], F32) as pbb,
        nc.sbuf_tensor("aa_pbrow", [1, NOC], F32) as pbrow,
        nc.sbuf_tensor("aa_vm", [128, g.n_vm * NO if g.n_vm else 1], F32) as vm,
        nc.sbuf_tensor("aa_aT", [16, TB], BF16) as aT,
        nc.sbuf_tensor("aa_tmp", [128, TB], F32) as tmpb,
        nc.sbuf_tensor("aa_acc", [128, 8 * TB], F32) as acc,
        nc.sbuf_tensor("aa_ob", [128, 2 * TB], BF16) as ob,
    ):
        px.dma("sync", pbrow[:, :], g.pb_row[:, :])
        if g.n_vm:
            for i in range(g.n_vm):
                px.dma("sync", vm[:, i * NO:(i + 1) * NO], g.c_vmask[i])
        px.mm(g.pool[0][:, :], g.ones1_f32[:, :], pbrow[:, 0:512])
        px.mm(g.pool[1][:, :192], g.ones1_f32[:, :], pbrow[:, 512:NOC])
        px.copy("vector", pbb[:, 0:512], g.pool[0][:, :])
        px.copy("vector", pbb[:, 512:NOC], g.pool[1][:, :192])
        for s in range(max_seg):
            w0 = s * TB - (WIN - TB)
            nz = max(0, -w0)
            for pg in range(8):
                r = slice(pg * 128, (pg + 1) * 128)
                px.dma("sync", q[:, pg * TB:(pg + 1) * TB],
                       g.q_s[r, s * TB:(s + 1) * TB])
                px.dma("sync", gt[:, pg * TB:(pg + 1) * TB],
                       g.gate_s[r, s * TB:(s + 1) * TB])
                if nz:
                    px.memset("gpsimd", kw[:, pg * WIN:pg * WIN + nz], 0)
                    px.memset("gpsimd", vw[:, pg * WIN:pg * WIN + nz], 0)
                px.dma("sync", kw[:, pg * WIN + nz:(pg + 1) * WIN],
                       g.k_s[r, w0 + nz:s * TB + TB])
                px.dma("sync", vw[:, pg * WIN + nz:(pg + 1) * WIN],
                       g.v_s[r, w0 + nz:s * TB + TB])
                px.memset("vector", acc[:, pg * TB:(pg + 1) * TB], 0)
            for oi, o in enumerate(offs):
                sh = WIN - TB - o
                ps_sc = g.pool[oi % 2]
                for pg in range(8):
                    P = Pb[:, (pg % 4) * TB:(pg % 4 + 1) * TB]
                    px.tt("vector" if pg % 2 == 0 else "gpsimd", P,
                          q[:, pg * TB:(pg + 1) * TB],
                          kw[:, pg * WIN + sh:pg * WIN + sh + TB], ALU.mult)
                    px.mm(ps_sc[0:16, :], g.indP_bf[:, pg * 16:(pg + 1) * 16],
                          P, start=(pg == 0), stop=(pg == 7))
                px.ts("vector", scb[:, :], ps_sc[0:16, :], 0.125, None,
                      ALU.mult)
                pT = g.pool[2 + oi % 2][:, :].bitcast(BF16)
                for ts in range(4 if getattr(g, "aa_do_tr", True) else 0):
                    px.transpose(pT[:, ts * 16:(ts + 1) * 16],
                                 scb[:, ts * 128:(ts + 1) * 128],
                                 g.ident_bf[0:16, 0:16])
                for ts in range(4 if getattr(g, "aa_do_ev", True) and getattr(g, "aa_do_tr", True) else 0):
                    if getattr(g, "aa_ev_contig", False):
                        dst = Sb[:, ts * NOC + oi * 16:ts * NOC + oi * 16 + 16]
                        src = pT[:, ts * 16:(ts + 1) * 16]
                    else:
                        dst = Sb[:, ts * NOC:(ts + 1) * NOC].rearrange(
                            "p (h o) -> p h o", o=NO)[:, :, oi:oi + 1]
                        src = pT[:, ts * 16:(ts + 1) * 16].unsqueeze(2)
                    px.copy("vector", dst, src)
            for ts in range(4 if do_sm else 0):
                S3 = Sb[:, ts * NOC:(ts + 1) * NOC].rearrange(
                    "p (h o) -> p h o", o=NO)
                pb3 = pbb[:, :].rearrange("p (h o) -> p h o", o=NO)
                px.tt("vector", S3, S3, pb3, ALU.add)
                gm = s * 4 + ts
                if gm < g.n_vm:
                    vm3 = vm[:, gm * NO:(gm + 1) * NO].unsqueeze(1) \
                        .broadcast_to([128, 16, NO])
                    px.tt("vector", S3, S3, vm3, ALU.add)
                mx = m16[:, ts * 16:(ts + 1) * 16]
                px.red("vector", mx, S3, ALU.max)
                mx3 = mx.unsqueeze(2).broadcast_to([128, 16, NO])
                px.tt("vector", S3, S3, mx3, ALU.subtract)
                px.act(Sb[:, ts * NOC:(ts + 1) * NOC],
                       Sb[:, ts * NOC:(ts + 1) * NOC], AF.Exp)
                sm = m16[:, 64 + ts * 16:64 + (ts + 1) * 16]
                px.red("vector", sm, S3, ALU.add)
                px.recip(sm, sm)
                sm3 = sm.unsqueeze(2).broadcast_to([128, 16, NO])
                px.tt("vector", al[:, ts * NOC:(ts + 1) * NOC].rearrange(
                    "p (h o) -> p h o", o=NO), S3, sm3, ALU.mult)
            for oi, o in enumerate(offs if do_av else []):
                sh = WIN - TB - o
                pT = g.pool[4 + oi % 2][:, :].bitcast(BF16)
                for ts in range(4):
                    src = al[:, ts * NOC:(ts + 1) * NOC].rearrange(
                        "p (h o) -> p h o", o=NO)[:, :, oi:oi + 1].squeeze(2)
                    px.transpose(pT[0:16, ts * 128:(ts + 1) * 128], src,
                                 g.ident_bf[:, :])
                    px.copy("vector", aT[:, ts * 128:(ts + 1) * 128],
                            pT[0:16, ts * 128:(ts + 1) * 128])
                for pg in range(8):
                    ps_ab = g.pool[6 + pg % 2]
                    px.mm(ps_ab[:, :], g.indT_bf[:, pg * 128:(pg + 1) * 128],
                          aT[:, :], start=True, stop=True)
                    px.tt("vector", tmpb[:, :],
                          vw[:, pg * WIN + sh:pg * WIN + sh + TB],
                          ps_ab[:, :], ALU.mult)
                    px.tt("gpsimd", acc[:, pg * TB:(pg + 1) * TB],
                          acc[:, pg * TB:(pg + 1) * TB], tmpb[:, :], ALU.add)
            for pg in range(8):
                o = ob[:, (pg % 2) * TB:(pg % 2 + 1) * TB]
                px.tt("vector", o, acc[:, pg * TB:(pg + 1) * TB],
                      gt[:, pg * TB:(pg + 1) * TB], ALU.mult)
                px.dma("gpsimd", g.attng_s[pg * 128:(pg + 1) * 128,
                                           s * TB:(s + 1) * TB], o)
        px.barrier()



def stage_l0a_out(px, nc, g):
    with (
        nc.sbuf_tensor("ao_wout", [128, 8 * D], BF16) as wout,
        nc.sbuf_tensor("ao_cols", [128, 8], F32) as cols,
        nc.sbuf_tensor("ao_ag", [128, 8 * TB], BF16) as ag,
        nc.sbuf_tensor("ao_xb", [128, 8 * TB], F32) as xb,
        nc.sbuf_tensor("ao_xn2", [128, 8 * TB], BF16) as xn2,
        nc.sbuf_tensor("ao_row", [1, TB], F32) as row,
        nc.sbuf_tensor("ao_tmp", [128, TB], F32) as tmpb,
    ):
        load_w(px, wout, g.w_out, D, D)
        load_vec_cols(px, cols, g.norm2_scale, D, 0)
        for tb in range(NTB):
            cs = slice(tb * TB, (tb + 1) * TB)
            for k in range(NKD):
                sl = slice(k * TB, (k + 1) * TB)
                px.dma("sync", ag[:, sl], g.attng_s[k * 128:(k + 1) * 128, cs])
            for m in range(NKD):
                bank = g.pool[m % 3]
                for k in range(NKD):
                    px.mm(bank[:, :],
                          wout[:, k * D + m * 128:k * D + (m + 1) * 128],
                          ag[:, k * TB:(k + 1) * TB],
                          start=(k == 0), stop=(k == NKD - 1))
                sl = slice(m * TB, (m + 1) * TB)
                px.dma("sync", xb[:, sl], g.xn_f32[m * 128:(m + 1) * 128, cs])
                px.tt("vector", xb[:, sl], xb[:, sl], bank[:, :], ALU.add)
                px.dma("gpsimd", g.xb_f32[m * 128:(m + 1) * 128, cs],
                       xb[:, sl])
            for k in range(NKD):
                sl = slice(k * TB, (k + 1) * TB)
                px.act(ag[:, sl], xb[:, sl], AF.Square)
            ps = g.pool[6]
            for k in range(NKD):
                px.mm(ps[0:1, :], g.ones_col_bf[:, 0:1],
                      ag[:, k * TB:(k + 1) * TB],
                      start=(k == 0), stop=(k == NKD - 1))
            px.act(row[:, :], ps[0:1, :], AF.Ln, bias=1e-6, scale=1.0 / D)
            px.act(row[:, :], row[:, :], AF.Exp, scale=-0.5)
            ps_r = g.pool[7]
            px.mm(ps_r[:, :], g.ones1_f32[:, :], row[:, :])
            for k in range(NKD):
                sl = slice(k * TB, (k + 1) * TB)
                px.tt("vector", tmpb[:, :], xb[:, sl], ps_r[:, :], ALU.mult)
                px.ts("vector", xn2[:, sl], tmpb[:, :], cols[:, k:k + 1],
                      None, ALU.mult)
                px.dma("gpsimd", g.xn2_bf[k * 128:(k + 1) * 128, cs],
                       xn2[:, sl])
        px.barrier()


def stage_l0a_ffn(px, nc, g):
    FFN = 2816
    NM_F = FFN // 128
    with (
        nc.sbuf_tensor("af_wgate", [128, 8 * FFN], BF16) as wgate,
        nc.sbuf_tensor("af_wup", [128, 8 * FFN], BF16) as wup,
        nc.sbuf_tensor("af_wdown", [128, NM_F * D], BF16) as wdown,
        nc.sbuf_tensor("af_xn2", [128, 8 * TB], BF16) as xn2,
        nc.sbuf_tensor("af_h", [128, NM_F * TB], BF16) as hb,
        nc.sbuf_tensor("af_gs", [128, TB], BF16) as gs,
        nc.sbuf_tensor("af_xbt", [128, TB], F32) as xbt,
        nc.sbuf_tensor("af_o", [128, TB], F32) as outb,
    ):
        load_w(px, wgate, g.w_ffn_gate, D, FFN)
        load_w(px, wup, g.w_ffn_up, D, FFN)
        load_w(px, wdown, g.w_ffn_down, FFN, D)
        for tb in range(NTB):
            cs = slice(tb * TB, (tb + 1) * TB)
            for k in range(NKD):
                sl = slice(k * TB, (k + 1) * TB)
                px.dma("sync", xn2[:, sl], g.xn2_bf[k * 128:(k + 1) * 128, cs])
            for m in range(NM_F):
                bg = g.pool[2 * (m % 2)]
                bu = g.pool[2 * (m % 2) + 1]
                for k in range(NKD):
                    px.mm(bg[:, :],
                          wgate[:, k * FFN + m * 128:k * FFN + (m + 1) * 128],
                          xn2[:, k * TB:(k + 1) * TB],
                          start=(k == 0), stop=(k == NKD - 1))
                for k in range(NKD):
                    px.mm(bu[:, :],
                          wup[:, k * FFN + m * 128:k * FFN + (m + 1) * 128],
                          xn2[:, k * TB:(k + 1) * TB],
                          start=(k == 0), stop=(k == NKD - 1))
                px.act(gs[:, :], bg[:, :], AF.Silu)
                px.tt("vector", hb[:, m * TB:(m + 1) * TB], gs[:, :],
                      bu[:, :], ALU.mult)
            for m in range(NKD):
                bank = g.pool[4 + m % 2]
                for k in range(NM_F):
                    px.mm(bank[:, :],
                          wdown[:, k * D + m * 128:k * D + (m + 1) * 128],
                          hb[:, k * TB:(k + 1) * TB],
                          start=(k == 0), stop=(k == NM_F - 1))
                px.dma("sync", xbt[:, :], g.xb_f32[m * 128:(m + 1) * 128, cs])
                px.tt("vector", outb[:, :], xbt[:, :], bank[:, :], ALU.add)
                px.dma("sync", g.s0_f32[m * 128:(m + 1) * 128, cs],
                       outb[:, :])
        px.barrier()



def _shift_load(px, g, dst_bf, src_f32, pg, tb):
    cs0 = tb * TB
    r = slice(pg * 128, (pg + 1) * 128)
    if pg < 4:
        px.dma("gpsimd", dst_bf, src_f32[r, cs0:cs0 + TB])
    elif tb == 0:
        px.memset("vector", dst_bf[:, 0:1], 0)
        px.dma("gpsimd", dst_bf[:, 1:TB], src_f32[r, 0:TB - 1])
    else:
        px.dma("gpsimd", dst_bf, src_f32[r, cs0 - 1:cs0 + TB - 1])


def stage_l0f_attn(px, nc, g, li, src, dst):
    specs = [
        ("fa_wqkv", [128, 8 * 3072], BF16), ("fa_wo", [128, 8 * D], BF16),
        ("fa_qkv", [128, 24 * TB], BF16), ("fa_sq", [128, TB], BF16),
        ("fa_xs", [128, 8 * TB], BF16), ("fa_rr", [16, 2 * TB], F32),
        ("fa_rot", [128, TB], F32), ("fa_eff", [128, 4 * TB], BF16),
        ("fa_qsks", [128, 2], F32), ("fa_qcur", [128, 8 * TB], BF16),
        ("fa_kring", [128, 2 * 8 * TB], BF16), ("fa_vtm", [128, 2 * 4 * 1024], BF16),
        ("fa_msk", [128, 4 * TB], F32), ("fa_sm", [128, 2 * TB], F32),
        ("fa_mx", [128, 4], F32), ("fa_al", [128, 2 * TB], BF16),
        ("fa_aT", [128, 4 * 256], BF16), ("fa_ao", [128, 8 * TB], BF16),
        ("fa_xl", [128, TB], F32), ("fa_o", [128, TB], F32),
        ("fa_t1", [128, TB], F32), ("fa_t2", [128, TB], F32),
    ]
    with ExitStack() as _st:
        (wqkv, wo, qkvb, sqb, xsb, rr, rotb, eff, qsks, qcur, kring, vtm,
         msk, smb, mxc, alb, aTb, aob, xlb, outb, t1, t2) = [
            _st.enter_context(nc.sbuf_tensor(f"{n}_{li}", sh, dt))
            for (n, sh, dt) in specs]
        load_w(px, wqkv, g.f_Wqkv[li], D, 3072)
        load_w(px, wo, g.f_Wo[li], D, D)
        px.dma("sync", qsks[0:64, 0:1], g.f_qscale[li][:, None])
        px.dma("sync", qsks[64:128, 0:1], g.f_qscale[li][:, None])
        px.dma("sync", qsks[0:64, 1:2], g.f_kscale[li][:, None])
        px.dma("sync", qsks[64:128, 1:2], g.f_kscale[li][:, None])
        for kind in range(2):
            for qs_ in range(2):
                px.dma("sync",
                       msk[:, (kind * 2 + qs_) * TB:(kind * 2 + qs_ + 1) * TB],
                       g.c_l0f_mask[kind, qs_])
        px.memset("vector", kring[:, :], 0)
        px.memset("gpsimd", vtm[:, :], 0)
        for tb in range(NTB):
            slot = tb % 2
            cs = slice(tb * TB, (tb + 1) * TB)
            for j in range(4):
                px.dma("sync", rotb[:, :], g.c_rot[j][:, cs])
                px.copy("vector", eff[:, j * TB:(j + 1) * TB], rotb[:, :])
            for pg in range(8):
                _shift_load(px, g, xsb[:, pg * TB:(pg + 1) * TB], src, pg, tb)
            for m in range(24):
                bank = g.pool[m % 2]
                for k in range(NKD):
                    px.mm(bank[:, :],
                          wqkv[:, k * 3072 + m * 128:k * 3072 + (m + 1) * 128],
                          xsb[:, k * TB:(k + 1) * TB],
                          start=(k == 0), stop=(k == NKD - 1))
                px.act(qkvb[:, m * TB:(m + 1) * TB], bank[:, :], AF.Copy)
                if m < 16:
                    px.act(sqb[:, :], bank[:, :], AF.Square)
                    ss = g.pool[2 + m // 8]
                    px.mm(ss[0:16, :], g.indP_bf[:, (m % 8) * 16:(m % 8 + 1) * 16],
                          sqb[:, :], start=(m % 8 == 0), stop=(m % 8 == 7))
            for j in range(2):
                px.act(rr[:, j * TB:(j + 1) * TB], g.pool[2 + j][0:16, :], AF.Ln)
                px.act(rr[:, j * TB:(j + 1) * TB], rr[:, j * TB:(j + 1) * TB],
                       AF.Exp, scale=-0.5)
                px.ts("vector", rr[:, j * TB:(j + 1) * TB],
                      rr[:, j * TB:(j + 1) * TB], 1e12, None, ALU.min)
            for m in range(16):
                pg = m % 8
                isq = m < 8
                rsl = rr[:, 0:TB] if isq else rr[:, TB:2 * TB]
                rb = g.pool[4]
                px.mm(rb[:, :], g.indT_f32[:, pg * 128:(pg + 1) * 128], rsl,
                      start=True, stop=True)
                qn = xsb[:, pg * TB:(pg + 1) * TB]
                px.tt("vector", qn, qkvb[:, m * TB:(m + 1) * TB], rb[:, :],
                      ALU.mult)
                px.ts("vector", qn, qn, qsks[:, 0:1] if isq else qsks[:, 1:2],
                      None, ALU.mult)
                rh = g.pool[5]
                px.mm(rh[:, :], g.c_R_bf[:, :], qn, start=True, stop=True)
                co = eff[:, 0 * TB:1 * TB] if isq else eff[:, 2 * TB:3 * TB]
                si = eff[:, 1 * TB:2 * TB] if isq else eff[:, 3 * TB:4 * TB]
                px.tt("gpsimd", t1[:, :], qn, co, ALU.mult)
                px.tt("vector", t2[:, :], rh[:, :], si, ALU.mult)
                dstq = qcur[:, pg * TB:(pg + 1) * TB] if isq else \
                    kring[:, (slot * 8 + pg) * TB:(slot * 8 + pg + 1) * TB]
                px.tt("vector", dstq, t1[:, :], t2[:, :], ALU.add)
            for pg in range(8):
                m = 16 + pg
                for ts_ in range(4):
                    pT = g.pool[6][:, :].bitcast(BF16)
                    px.transpose(pT[:, 0:128],
                                 qkvb[:, m * TB + ts_ * 128:m * TB + (ts_ + 1) * 128],
                                 g.ident_bf[:, :])
                    px.copy("vector", vtm[:, slot * 4096 + ts_ * 1024 + pg * 128:
                                          slot * 4096 + ts_ * 1024 + (pg + 1) * 128],
                            pT[:, 0:128])
            for w2 in range(2):
                w = tb * 2 + w2
                kind = 1 if w == 0 else 0
                for h in range(16):
                    pg = h // 2
                    qrow = slice((h % 2) * 64, (h % 2) * 64 + 64)
                    for qs_ in range(2):
                        sc = g.pool[qs_]
                        lhs = qcur[qrow, pg * TB + w2 * 256 + qs_ * 128:
                                   pg * TB + w2 * 256 + (qs_ + 1) * 128]
                        for jh in range(2):
                            j0 = (w - 1) * 256 + jh * 256
                            if j0 < 0:
                                px.memset("vector", sc[:, jh * 256:(jh + 1) * 256], 0)
                                continue
                            ksl = (j0 // 512) % 2
                            kof = j0 % 512
                            px.mm(sc[:, jh * 256:(jh + 1) * 256], lhs,
                                  kring[qrow, (ksl * 8 + pg) * TB + kof:
                                        (ksl * 8 + pg) * TB + kof + 256],
                                  start=True, stop=True)
                        mx = mxc[:, qs_:qs_ + 1]
                        px.red("vector", mx, sc[:, :], ALU.max)
                        px.stt("vector", smb[:, qs_ * TB:(qs_ + 1) * TB],
                               sc[:, :], mx, msk[:, (kind * 2 + qs_) * TB:
                                                 (kind * 2 + qs_ + 1) * TB],
                               ALU.subtract, ALU.add)
                        px.act(smb[:, qs_ * TB:(qs_ + 1) * TB],
                               smb[:, qs_ * TB:(qs_ + 1) * TB], AF.Exp,
                               scale=8.0, accum_out=mxc[:, 2 + qs_:3 + qs_])
                        px.recip(mxc[:, 2 + qs_:3 + qs_], mxc[:, 2 + qs_:3 + qs_])
                        px.ts("vector", alb[:, qs_ * TB:(qs_ + 1) * TB],
                              smb[:, qs_ * TB:(qs_ + 1) * TB],
                              mxc[:, 2 + qs_:3 + qs_], None, ALU.mult)
                        pT = g.pool[2 + qs_][:, :].bitcast(BF16)
                        for jc in range(4):
                            px.transpose(pT[:, jc * 128:(jc + 1) * 128],
                                         alb[:, qs_ * TB + jc * 128:
                                             qs_ * TB + (jc + 1) * 128],
                                         g.ident_bf[:, :])
                        for jc in range(4):
                            px.copy("vector",
                                    aTb[:, jc * 256 + qs_ * 128:
                                        jc * 256 + (qs_ + 1) * 128],
                                    pT[:, jc * 128:(jc + 1) * 128])
                    av = g.pool[4 + (h % 2)]
                    nj = 0
                    for jc in range(4):
                        j0 = (w - 1) * 256 + jc * 128
                        if j0 < 0:
                            continue
                        vsl = (j0 // 512) % 2
                        vtt = (j0 % 512) // 128
                        px.mm(av[0:64, 0:256],
                              vtm[:, vsl * 4096 + vtt * 1024 + h * 64:
                                  vsl * 4096 + vtt * 1024 + (h + 1) * 64],
                              aTb[:, jc * 256:(jc + 1) * 256],
                              start=(nj == 0), stop=(jc == 3))
                        nj += 1
                    px.act(aob[qrow, pg * TB + w2 * 256:pg * TB + w2 * 256 + 256],
                           av[0:64, 0:256], AF.Copy)
            for m in range(NKD):
                bank = g.pool[6 + m % 2]
                for k in range(NKD):
                    px.mm(bank[:, :], wo[:, k * D + m * 128:k * D + (m + 1) * 128],
                          aob[:, k * TB:(k + 1) * TB],
                          start=(k == 0), stop=(k == NKD - 1))
                px.dma("sync", xlb[:, :], src[m * 128:(m + 1) * 128, cs])
                px.tt("vector", outb[:, :], xlb[:, :], bank[:, :], ALU.add)
                px.dma("sync", dst[m * 128:(m + 1) * 128, cs], outb[:, :])
        px.barrier()


def stage_l0f_ffn(px, nc, g, li, src, dst):
    DI = 2730
    NC1 = len(kchunks(DI))
    with (
        nc.sbuf_tensor(f"ff{li}_w1", [128, 8 * 5460], BF16) as w1,
        nc.sbuf_tensor(f"ff{li}_w2", [128, NC1 * D], BF16) as w2,
        nc.sbuf_tensor(f"ff{li}_cols", [128, 16 + 2 * NC1 + 8], F32) as cols,
        nc.sbuf_tensor(f"ff{li}_xs", [128, 8 * TB], F32) as xsf,
        nc.sbuf_tensor(f"ff{li}_xsb", [128, 8 * TB], BF16) as xsb,
        nc.sbuf_tensor(f"ff{li}_xn", [128, 8 * TB], BF16) as xnb,
        nc.sbuf_tensor(f"ff{li}_h", [128, NC1 * TB], BF16) as hb,
        nc.sbuf_tensor(f"ff{li}_gg", [128, TB], BF16) as gg,
        nc.sbuf_tensor(f"ff{li}_row", [1, 4 * TB], F32) as row,
        nc.sbuf_tensor(f"ff{li}_o", [128, TB], F32) as outb,
    ):
        load_w(px, w1, g.f_W1[li], D, 5460)
        load_w(px, w2, g.f_W2[li], DI, D)
        load_vec_cols(px, cols, g.f_lng[li], D, 0)
        load_vec_cols(px, cols, g.f_lnb[li], D, 8)
        for i, (k0, kk) in enumerate(kchunks(DI)):
            px.dma("sync", cols[0:kk, 16 + i:17 + i], g.f_b1[li][k0:k0 + kk][:, None])
            px.dma("sync", cols[0:kk, 16 + NC1 + i:17 + NC1 + i],
                   g.f_b1[li][DI + k0:DI + k0 + kk][:, None])
        load_vec_cols(px, cols, g.f_b2[li], D, 16 + 2 * NC1)
        SUM, SQ, VAR, RSTD = 0, TB, 2 * TB, 3 * TB
        for tb in range(NTB):
            cs = slice(tb * TB, (tb + 1) * TB)
            for pg in range(8):
                sl = slice(pg * TB, (pg + 1) * TB)
                _shift_load(px, g, xsb[:, sl], src, pg, tb)
                cs0 = tb * TB
                r = slice(pg * 128, (pg + 1) * 128)
                if pg < 4:
                    px.dma("sync", xsf[:, sl], src[r, cs0:cs0 + TB])
                elif tb == 0:
                    px.memset("gpsimd", xsf[:, sl][:, 0:1], 0)
                    px.dma("sync", xsf[:, sl][:, 1:TB], src[r, 0:TB - 1])
                else:
                    px.dma("sync", xsf[:, sl], src[r, cs0 - 1:cs0 + TB - 1])
                px.act(xnb[:, sl], xsb[:, sl], AF.Square)
            ps_s = g.pool[6]
            ps_q = g.pool[7]
            for k in range(NKD):
                px.mm(ps_s[0:1, :], g.ones_col_bf[:, 0:1],
                      xsb[:, k * TB:(k + 1) * TB],
                      start=(k == 0), stop=(k == NKD - 1))
            for k in range(NKD):
                px.mm(ps_q[0:1, :], g.ones_col_bf[:, 0:1],
                      xnb[:, k * TB:(k + 1) * TB],
                      start=(k == 0), stop=(k == NKD - 1))
            px.act(row[:, SUM:SUM + TB], ps_s[0:1, :], AF.Copy, scale=1.0 / D)
            px.act(row[:, SQ:SQ + TB], ps_q[0:1, :], AF.Copy, scale=1.0 / D)
            px.tt("vector", row[:, VAR:VAR + TB], row[:, SUM:SUM + TB],
                  row[:, SUM:SUM + TB], ALU.mult)
            px.tt("vector", row[:, VAR:VAR + TB], row[:, SQ:SQ + TB],
                  row[:, VAR:VAR + TB], ALU.subtract)
            px.act(row[:, RSTD:RSTD + TB], row[:, VAR:VAR + TB], AF.Ln,
                   bias=1e-5)
            px.act(row[:, RSTD:RSTD + TB], row[:, RSTD:RSTD + TB], AF.Exp,
                   scale=-0.5)
            ps_m = g.pool[0]
            ps_r = g.pool[1]
            px.mm(ps_m[:, :], g.ones1_f32[:, :], row[:, SUM:SUM + TB])
            px.mm(ps_r[:, :], g.ones1_f32[:, :], row[:, RSTD:RSTD + TB])
            for k in range(NKD):
                sl = slice(k * TB, (k + 1) * TB)
                px.tt("vector", xsf[:, sl], xsf[:, sl], ps_m[:, :], ALU.subtract)
                px.tt("vector", xsf[:, sl], xsf[:, sl], ps_r[:, :], ALU.mult)
                px.ts("vector", xnb[:, sl], xsf[:, sl], cols[:, k:k + 1],
                      cols[:, 8 + k:9 + k], ALU.mult, ALU.add)
            for i, (k0, kk) in enumerate(kchunks(DI)):
                ba = g.pool[2 * (i % 2)]
                bgt = g.pool[2 * (i % 2) + 1]
                for k in range(NKD):
                    px.mm(ba[0:kk, :],
                          w1[:, k * 5460 + k0:k * 5460 + k0 + kk],
                          xnb[:, k * TB:(k + 1) * TB],
                          start=(k == 0), stop=(k == NKD - 1))
                for k in range(NKD):
                    px.mm(bgt[0:kk, :],
                          w1[:, k * 5460 + DI + k0:k * 5460 + DI + k0 + kk],
                          xnb[:, k * TB:(k + 1) * TB],
                          start=(k == 0), stop=(k == NKD - 1))
                px.act(gg[0:kk, :], bgt[0:kk, :], AF.Gelu,
                       bias=cols[0:kk, 16 + NC1 + i:17 + NC1 + i])
                px.stt("vector", hb[0:kk, i * TB:(i + 1) * TB], ba[0:kk, :],
                       cols[0:kk, 16 + i:17 + i], gg[0:kk, :],
                       ALU.add, ALU.mult)
            for m in range(NKD):
                bank = g.pool[4 + m % 2]
                for i, (k0, kk) in enumerate(kchunks(DI)):
                    px.mm(bank[:, :],
                          w2[0:kk, i * D + m * 128:i * D + (m + 1) * 128],
                          hb[0:kk, i * TB:(i + 1) * TB],
                          start=(i == 0), stop=(i == NC1 - 1))
                px.dma("sync", outb[:, :], src[m * 128:(m + 1) * 128, cs])
                px.stt("vector", outb[:, :], bank[:, :],
                       cols[:, 16 + 2 * NC1 + m:17 + 2 * NC1 + m],
                       outb[:, :], ALU.add, ALU.add)
                px.dma("sync", dst[m * 128:(m + 1) * 128, cs], outb[:, :])
        px.barrier()



def stage_combine_ln(px, nc, g, xsrc, ssrc, gsrc, x1_dst, xnb_dst, xnf_dst,
                     name):
    with (
        nc.sbuf_tensor(name + "_x", [128, NKD * TB], F32) as xt,
        nc.sbuf_tensor(name + "_s", [128, TB], F32) as st_,
        nc.sbuf_tensor(name + "_g", [128, TB], F32) as gt_,
        nc.sbuf_tensor(name + "_xb", [128, NKD * TB], BF16) as xbt,
        nc.sbuf_tensor(name + "_sq", [128, NKD * TB], BF16) as sqt,
        nc.sbuf_tensor(name + "_row", [1, 6 * TB], F32) as row,
        nc.sbuf_tensor(name + "_o", [128, TB], F32) as ot,
        nc.sbuf_tensor(name + "_ob", [128, TB], BF16) as otb,
    ):
        SUM, SQ, VAR, RSTD = 0, TB, 2 * TB, 3 * TB
        for tb in range(NTB):
            cs = slice(tb * TB, (tb + 1) * TB)
            for k in range(NKD):
                sl = slice(k * TB, (k + 1) * TB)
                r = slice(k * 128, (k + 1) * 128)
                px.dma("sync", xt[:, sl], xsrc[r, cs])
                px.dma("sync", st_[:, :], ssrc[r, cs])
                px.dma("sync", gt_[:, :], gsrc[r, cs])
                px.act(ot[:, :], gt_[:, :], AF.Silu)
                px.tt("vector", ot[:, :], ot[:, :], st_[:, :], ALU.mult)
                px.tt("vector", xt[:, sl], xt[:, sl], ot[:, :], ALU.add)
                px.dma("gpsimd", x1_dst[r, cs], xt[:, sl])
                px.copy("scalar", xbt[:, sl], xt[:, sl])
                px.act(sqt[:, sl], xbt[:, sl], AF.Square)
            ps_s = g.pool[0]
            ps_q = g.pool[1]
            for k in range(NKD):
                px.mm(ps_s[0:1, :], g.ones_col_bf[:, 0:1],
                      xbt[:, k * TB:(k + 1) * TB],
                      start=(k == 0), stop=(k == NKD - 1))
            for k in range(NKD):
                px.mm(ps_q[0:1, :], g.ones_col_bf[:, 0:1],
                      sqt[:, k * TB:(k + 1) * TB],
                      start=(k == 0), stop=(k == NKD - 1))
            px.act(row[:, SUM:SUM + TB], ps_s[0:1, :], AF.Copy, scale=1.0 / D)
            px.act(row[:, SQ:SQ + TB], ps_q[0:1, :], AF.Copy, scale=1.0 / D)
            px.tt("vector", row[:, VAR:VAR + TB], row[:, SUM:SUM + TB],
                  row[:, SUM:SUM + TB], ALU.mult)
            px.tt("vector", row[:, VAR:VAR + TB], row[:, SQ:SQ + TB],
                  row[:, VAR:VAR + TB], ALU.subtract)
            px.act(row[:, RSTD:RSTD + TB], row[:, VAR:VAR + TB], AF.Ln,
                   bias=1e-5)
            px.act(row[:, RSTD:RSTD + TB], row[:, RSTD:RSTD + TB], AF.Exp,
                   scale=-0.5)
            ps_m = g.pool[2]
            ps_r = g.pool[3]
            px.mm(ps_m[:, :], g.ones1_f32[:, :], row[:, SUM:SUM + TB])
            px.mm(ps_r[:, :], g.ones1_f32[:, :], row[:, RSTD:RSTD + TB])
            for k in range(NKD):
                sl = slice(k * TB, (k + 1) * TB)
                r = slice(k * 128, (k + 1) * 128)
                px.tt("vector", ot[:, :], xt[:, sl], ps_m[:, :], ALU.subtract)
                px.tt("vector", ot[:, :], ot[:, :], ps_r[:, :], ALU.mult)
                px.copy("scalar", otb[:, :], ot[:, :])
                if xnf_dst is not None:
                    px.dma("sync", xnf_dst[r, cs], ot[:, :])
                px.dma("gpsimd", xnb_dst[r, cs], otb[:, :])
        px.barrier()



def stage_m_proj(px, nc, g):
    with (
        nc.sbuf_tensor("mp_w", [128, 8 * 4096], BF16) as w,
        nc.sbuf_tensor("mp_x", [128, 8 * TB], BF16) as xt,
        nc.sbuf_tensor("mp_o", [128, 2 * TB], F32) as ot,
    ):
        load_w(px, w, g.m_inproj, D, 4096)
        for tb in range(NTB):
            cs = slice(tb * TB, (tb + 1) * TB)
            for k in range(NKD):
                px.dma("sync", xt[:, k * TB:(k + 1) * TB],
                       g.xn1_bf[k * 128:(k + 1) * 128, cs])
            for m in range(32):
                bank = g.pool[m % 4]
                for k in range(NKD):
                    px.mm(bank[:, :],
                          w[:, k * 4096 + m * 128:k * 4096 + (m + 1) * 128],
                          xt[:, k * TB:(k + 1) * TB],
                          start=(k == 0), stop=(k == NKD - 1))
                o = ot[:, (m % 2) * TB:(m % 2 + 1) * TB]
                px.act(o, bank[:, :], AF.Copy)
                dst = g.xc_f32 if m < 16 else g.z_f32
                mm = m % 16
                px.dma("sync", dst[mm * 128:(mm + 1) * 128, cs], o)
        px.barrier()


def stage_m_conv(px, nc, g):
    with (
        nc.sbuf_tensor("mc_w", [128, 16 * 5], F32) as wc,
        nc.sbuf_tensor("mc_x", [128, TB + 6], F32) as xt,
        nc.sbuf_tensor("mc_a", [128, TB], F32) as acc,
        nc.sbuf_tensor("mc_ob", [128, TB], BF16) as ob,
    ):
        for pg in range(16):
            r = slice(pg * 128, (pg + 1) * 128)
            px.dma("sync", wc[:, pg * 5:pg * 5 + 4], g.m_convw[r, :])
            px.dma("sync", wc[:, pg * 5 + 4:pg * 5 + 5], g.m_convb[r][:, None])
        for d in range(2):
            for pg in range(16):
                r = slice(pg * 128, (pg + 1) * 128)
                for tb in range(NTB):
                    cs0 = tb * TB
                    lo = cs0 - 3 if d == 0 else cs0
                    hi = cs0 + TB if d == 0 else cs0 + TB + 3
                    lo_c = max(0, lo)
                    hi_c = min(T, hi)
                    if lo < 0:
                        px.memset("vector", xt[:, 0:lo_c - lo], 0)
                    if hi > T:
                        px.memset("vector", xt[:, TB + 3 - (hi - hi_c):TB + 3], 0)
                    px.dma("sync", xt[:, lo_c - lo:lo_c - lo + hi_c - lo_c],
                           g.xc_f32[r, lo_c:hi_c])
                    for i in range(4):
                        off = i if d == 0 else 3 - i
                        xs_ = xt[:, off:off + TB]
                        if i == 0:
                            px.ts("vector", acc[:, :], xs_,
                                  wc[:, pg * 5:pg * 5 + 1], None, ALU.mult)
                        else:
                            px.stt("vector", acc[:, :], xs_,
                                   wc[:, pg * 5 + i:pg * 5 + i + 1], acc[:, :],
                                   ALU.mult, ALU.add)
                    px.act(acc[:, :], acc[:, :], AF.Silu,
                           bias=wc[:, pg * 5 + 4:pg * 5 + 5])
                    px.copy("vector", ob[:, :], acc[:, :])
                    px.dma("sync", g.xcs_f32[d][r, cs0:cs0 + TB], acc[:, :])
                    px.dma("gpsimd", g.xcs_bf[d][r, cs0:cs0 + TB], ob[:, :])
        px.barrier()


def stage_m_scan(px, nc, g):
    DT_RANK, DS = 64, 16
    specs = [
        ("ms_xw", [128, 16 * 96], BF16), ("ms_dtw", [64, 2048], BF16),
        ("ms_cols", [128, 16 * 2], F32), ("ms_A", [128, 256], F32),
        ("ms_selBC", [96, 32 * 128], F32), ("ms_xcb", [128, 16 * TB], BF16),
        ("ms_xcf", [128, 16 * TB], F32), ("ms_dblf", [96, TB], F32),
        ("ms_dtb", [64, TB], BF16), ("ms_delta", [128, 16 * TB], F32),
        ("ms_du", [128, 16 * TB], F32), ("ms_Bs", [128, TB], F32),
        ("ms_Cs", [128, TB], F32), ("ms_a", [128, TB], F32),
        ("ms_b", [128, TB], F32), ("ms_h", [128, TB], F32),
        ("ms_y", [128, 16 * TB], F32), ("ms_carry", [128, 256], F32),
        ("ms_z", [128, TB], F32), ("ms_t", [128, TB], F32),
        ("ms_yc", [128, TB], F32),
    ]
    with ExitStack() as _st:
        (xw, dtw, cols, Asb, selBC, xcb, xcf, dblf, dtb, dlt, du, Bs, Cs,
         ab, bb_, hbuf, yb, carry, zb, tb_, ycb) = [
            _st.enter_context(nc.sbuf_tensor(n, sh, dt)) for (n, sh, dt) in specs]
        load_w(px, xw, g.m_xprojw, 2048, 96)
        px.dma("gpsimd", dtw[:, :], g.m_dtw[:, :])
        for pg in range(16):
            r = slice(pg * 128, (pg + 1) * 128)
            px.dma("sync", cols[:, pg:pg + 1], g.m_dtb[r][:, None])
            px.dma("sync", cols[:, 16 + pg:17 + pg], g.m_D[r][:, None])
            px.dma("sync", Asb[:, pg * 16:(pg + 1) * 16], g.m_Alog[r, :])
        px.act(Asb[:, :], Asb[:, :], AF.Exp)
        px.ts("vector", Asb[:, :], Asb[:, :], -1.0, None, ALU.mult)
        px.dma("sync", selBC[:, :], g.c_selBC[:, :])
        for d in range(getattr(g, "ms_dirs", 2)):
            px.memset("vector", carry[:, :], 0)
            crange = range(NTB) if d == 0 else range(NTB - 1, -1, -1)
            crange = list(crange)[:getattr(g, "ms_chunks", NTB)]
            for c in crange:
                cs = slice(c * TB, (c + 1) * TB)
                for pg in range(16):
                    px.dma("sync", xcb[:, pg * TB:(pg + 1) * TB],
                           g.xcs_bf[d][pg * 128:(pg + 1) * 128, cs])
                    px.dma("sync", xcf[:, pg * TB:(pg + 1) * TB],
                           g.xcs_f32[d][pg * 128:(pg + 1) * 128, cs])
                bank = g.pool[0]
                for k in range(16):
                    px.mm(bank[0:96, :], xw[:, k * 96:(k + 1) * 96],
                          xcb[:, k * TB:(k + 1) * TB],
                          start=(k == 0), stop=(k == 15))
                px.act(dblf[:, :], bank[0:96, :], AF.Copy)
                px.copy("scalar", dtb[:, :], bank[0:64, :])
                if getattr(g, "ms_upto", 5) <= 1:
                    continue
                for m in range(16):
                    bank2 = g.pool[1 + m % 2]
                    px.mm(bank2[:, :], dtw[:, m * 128:(m + 1) * 128],
                          dtb[:, :], start=True, stop=True)
                    px.ts("vector", yb[:, m * TB:(m + 1) * TB], bank2[:, :],
                          cols[:, m:m + 1], None, ALU.add)
                ZW = 16 * TB
                px.act(dlt[:, :], yb[:, :ZW], AF.Abs)
                px.act(dlt[:, :], dlt[:, :], AF.Exp, scale=-1.0)
                px.act(dlt[:, :], dlt[:, :], AF.Ln, bias=1.0)
                px.ts("vector", du[:, :], yb[:, :ZW], 0.0, None, ALU.max)
                px.tt("vector", dlt[:, :], dlt[:, :], du[:, :], ALU.add)
                px.tt("gpsimd", du[:, :], dlt[:, :], xcf[:, :], ALU.mult)
                for n in range(DS):
                    bB = g.pool[4]
                    bC = g.pool[5]
                    px.mm(bB[:, :], selBC[:, (2 * n) * 128:(2 * n + 1) * 128],
                          dblf[:, :], start=True, stop=True)
                    px.mm(bC[:, :], selBC[:, (2 * n + 1) * 128:(2 * n + 2) * 128],
                          dblf[:, :], start=True, stop=True)
                    px.act(Bs[:, :], bB[:, :], AF.Copy)
                    px.act(Cs[:, :], bC[:, :], AF.Copy)
                    for gd in range(16):
                        dsl = dlt[:, gd * TB:(gd + 1) * TB]
                        px.act(ab[:, :], dsl, AF.Exp,
                               scale=Asb[:, gd * 16 + n:gd * 16 + n + 1])
                        px.tt("gpsimd", bb_[:, :], du[:, gd * TB:(gd + 1) * TB],
                              Bs[:, :], ALU.mult)
                        cc = carry[:, gd * 16 + n:gd * 16 + n + 1]
                        if getattr(g, "ms_upto", 5) <= 3:
                            continue
                        if d == 0:
                            px.scan(hbuf[:, :], ab[:, :], bb_[:, :], cc)
                            px.copy("vector", cc, hbuf[:, TB - 1:TB])
                        else:
                            ra = BAP(ab, TB - 1, [[TB, 128], [-1, TB]])
                            rb2 = BAP(bb_, TB - 1, [[TB, 128], [-1, TB]])
                            rh = BAP(hbuf, TB - 1, [[TB, 128], [-1, TB]])
                            px.scan(rh, ra, rb2, cc)
                            px.copy("vector", cc, hbuf[:, 0:1])
                        ysl = yb[:, gd * TB:(gd + 1) * TB]
                        if getattr(g, "ms_upto", 5) <= 4:
                            continue
                        if n == 0:
                            px.tt("vector", ysl, hbuf[:, :], Cs[:, :], ALU.mult)
                        else:
                            px.tt("vector", tb_[:, :], hbuf[:, :], Cs[:, :],
                                  ALU.mult)
                            px.tt("gpsimd", ysl, ysl, tb_[:, :], ALU.add)
                for gd in range(16 if getattr(g, "ms_upto", 5) >= 5 else 0):
                    r = slice(gd * 128, (gd + 1) * 128)
                    px.stt("vector", tb_[:, :], xcf[:, gd * TB:(gd + 1) * TB],
                           cols[:, 16 + gd:17 + gd], yb[:, gd * TB:(gd + 1) * TB],
                           ALU.mult, ALU.add)
                    px.dma("sync", zb[:, :], g.z_f32[r, cs])
                    px.act(zb[:, :], zb[:, :], AF.Silu)
                    px.tt("vector", ycb[:, :], tb_[:, :], zb[:, :], ALU.mult)
                    if d == 0:
                        px.dma("gpsimd", g.ycomb_f32[r, cs], ycb[:, :])
                    else:
                        px.dma("gpsimd", g.ycomb_f32[r, cs], ycb[:, :],
                               accum_op=ALU.add)
        px.barrier()


def stage_m_out(px, nc, g):
    with (
        nc.sbuf_tensor("mo_w", [128, 16 * D], BF16) as w,
        nc.sbuf_tensor("mo_y", [128, 16 * TB], BF16) as yt,
        nc.sbuf_tensor("mo_m", [128, 8 * TB], F32) as mt,
        nc.sbuf_tensor("mo_sq", [128, 8 * TB], BF16) as sqt,
        nc.sbuf_tensor("mo_row", [1, 4 * TB], F32) as row,
        nc.sbuf_tensor("mo_cols", [128, 16], F32) as cols,
        nc.sbuf_tensor("mo_ob", [128, TB], BF16) as ob,
    ):
        load_w(px, w, g.m_outproj, 2048, D)
        load_vec_cols(px, cols, g.m_lng, D, 0)
        load_vec_cols(px, cols, g.m_lnb, D, 8)
        SUM, SQ, VAR, RSTD = 0, TB, 2 * TB, 3 * TB
        for tb in range(NTB):
            cs = slice(tb * TB, (tb + 1) * TB)
            for k in range(16):
                px.dma("gpsimd", yt[:, k * TB:(k + 1) * TB],
                       g.ycomb_f32[k * 128:(k + 1) * 128, cs])
            for m in range(NKD):
                bank = g.pool[m % 4]
                for k in range(16):
                    px.mm(bank[:, :], w[:, k * D + m * 128:k * D + (m + 1) * 128],
                          yt[:, k * TB:(k + 1) * TB],
                          start=(k == 0), stop=(k == 15))
                sl = slice(m * TB, (m + 1) * TB)
                px.act(mt[:, sl], bank[:, :], AF.Copy)
                px.act(sqt[:, sl], bank[:, :], AF.Square)
            ps_s = g.pool[4]
            ps_q = g.pool[5]
            for k in range(NKD):
                px.mm(ps_s[0:1, :], g.ones_col_bf[:, 0:1],
                      sqt[:, k * TB:(k + 1) * TB],
                      start=(k == 0), stop=(k == NKD - 1))
            for k in range(NKD):
                px.copy("scalar", sqt[:, k * TB:(k + 1) * TB],
                        mt[:, k * TB:(k + 1) * TB])
            for k in range(NKD):
                px.mm(ps_q[0:1, :], g.ones_col_bf[:, 0:1],
                      sqt[:, k * TB:(k + 1) * TB],
                      start=(k == 0), stop=(k == NKD - 1))
            px.act(row[:, SUM:SUM + TB], ps_q[0:1, :], AF.Copy, scale=1.0 / D)
            px.act(row[:, SQ:SQ + TB], ps_s[0:1, :], AF.Copy, scale=1.0 / D)
            px.tt("vector", row[:, VAR:VAR + TB], row[:, SUM:SUM + TB],
                  row[:, SUM:SUM + TB], ALU.mult)
            px.tt("vector", row[:, VAR:VAR + TB], row[:, SQ:SQ + TB],
                  row[:, VAR:VAR + TB], ALU.subtract)
            px.act(row[:, RSTD:RSTD + TB], row[:, VAR:VAR + TB], AF.Ln,
                   bias=1e-5)
            px.act(row[:, RSTD:RSTD + TB], row[:, RSTD:RSTD + TB], AF.Exp,
                   scale=-0.5)
            ps_m = g.pool[6]
            ps_r = g.pool[7]
            px.mm(ps_m[:, :], g.ones1_f32[:, :], row[:, SUM:SUM + TB])
            px.mm(ps_r[:, :], g.ones1_f32[:, :], row[:, RSTD:RSTD + TB])
            for k in range(NKD):
                sl = slice(k * TB, (k + 1) * TB)
                px.tt("vector", mt[:, sl], mt[:, sl], ps_m[:, :], ALU.subtract)
                px.tt("vector", mt[:, sl], mt[:, sl], ps_r[:, :], ALU.mult)
                px.ts("vector", ob[:, :], mt[:, sl], cols[:, k:k + 1],
                      cols[:, 8 + k:9 + k], ALU.mult, ALU.add)
                px.dma("gpsimd", g.mh_bf[k * 128:(k + 1) * 128, cs], ob[:, :])
        px.barrier()


def stage_m_ffn(px, nc, g):
    with (
        nc.sbuf_tensor("mf_w1", [128, 8 * 4096], BF16) as w1,
        nc.sbuf_tensor("mf_w2", [128, 32 * D], BF16) as w2,
        nc.sbuf_tensor("mf_cols", [128, 32 + 8], F32) as cols,
        nc.sbuf_tensor("mf_x", [128, 8 * TB], BF16) as xt,
        nc.sbuf_tensor("mf_h", [128, 32 * TB], BF16) as hb,
        nc.sbuf_tensor("mf_o", [128, TB], F32) as ob,
    ):
        load_w(px, w1, g.m_ffw1, D, 4096)
        load_w(px, w2, g.m_ffw2, 4096, D)
        load_vec_cols(px, cols, g.m_ffb1, 4096, 0)
        load_vec_cols(px, cols, g.m_ffb2, D, 32)
        for tb in range(NTB):
            cs = slice(tb * TB, (tb + 1) * TB)
            for k in range(NKD):
                px.dma("sync", xt[:, k * TB:(k + 1) * TB],
                       g.mh_bf[k * 128:(k + 1) * 128, cs])
            for m in range(32):
                bank = g.pool[m % 4]
                for k in range(NKD):
                    px.mm(bank[:, :],
                          w1[:, k * 4096 + m * 128:k * 4096 + (m + 1) * 128],
                          xt[:, k * TB:(k + 1) * TB],
                          start=(k == 0), stop=(k == NKD - 1))
                px.act(hb[:, m * TB:(m + 1) * TB], bank[:, :], AF.Gelu,
                       bias=cols[:, m:m + 1])
            for m in range(NKD):
                bank = g.pool[4 + m % 2]
                for k in range(32):
                    px.mm(bank[:, :], w2[:, k * D + m * 128:k * D + (m + 1) * 128],
                          hb[:, k * TB:(k + 1) * TB],
                          start=(k == 0), stop=(k == 31))
                px.ts("vector", ob[:, :], bank[:, :], cols[:, 32 + m:33 + m],
                      None, ALU.add)
                px.dma("sync", g.s1_f32[m * 128:(m + 1) * 128, cs], ob[:, :])
        px.barrier()



def stage_moe_route(px, nc, g):
    with (
        nc.sbuf_tensor("mr_gw", [128, 8 * 8], BF16) as gw,
        nc.sbuf_tensor("mr_se", [128, 8], BF16) as se,
        nc.sbuf_tensor("mr_x", [128, 8 * TB], BF16) as xt,
        nc.sbuf_tensor("mr_lg", [8, TB], BF16) as lg,
        nc.sbuf_tensor("mr_tm", [128, 8], F32) as tm,
        nc.sbuf_tensor("mr_m1", [128, 4], F32) as m1,
        nc.sbuf_tensor("mr_sc", [128, 8], F32) as sc,
        nc.sbuf_tensor("mr_wt", [128, 8], BF16) as wtb,
        nc.sbuf_tensor("mr_wfm", [8, TB], F32) as wfm,
        nc.sbuf_tensor("mr_sw", [1, TB], F32) as swr,
    ):
        load_w(px, gw, g.e_gatew, D, 8)
        load_w(px, se, g.e_segatew, D, 1)
        for tb in range(NTB):
            cs = slice(tb * TB, (tb + 1) * TB)
            for k in range(NKD):
                px.dma("sync", xt[:, k * TB:(k + 1) * TB],
                       g.xn1_bf[k * 128:(k + 1) * 128, cs])
            blog = g.pool[0]
            for k in range(NKD):
                px.mm(blog[0:8, :], gw[:, k * 8:(k + 1) * 8],
                      xt[:, k * TB:(k + 1) * TB],
                      start=(k == 0), stop=(k == NKD - 1))
            bse = g.pool[1]
            for k in range(NKD):
                px.mm(bse[0:1, :], se[:, k:k + 1], xt[:, k * TB:(k + 1) * TB],
                      start=(k == 0), stop=(k == NKD - 1))
            px.act(swr[:, :], bse[0:1, :], AF.Sigmoid)
            px.dma("sync", g.sw_f32[0:1, cs], swr[:, :])
            px.copy("vector", lg[:, :], blog[0:8, :])
            for ts_ in range(4):
                pT = g.pool[2][:, :].bitcast(BF16)
                px.transpose(pT[:, 0:8], lg[:, ts_ * 128:(ts_ + 1) * 128],
                             g.ident_bf[0:8, 0:8])
                px.copy("vector", tm[:, 0:8], pT[:, 0:8])
                px.red("vector", m1[:, 0:1], tm[:, 0:8], ALU.max)
                px.ts("vector", tm[:, 0:8], tm[:, 0:8], m1[:, 0:1], None,
                      ALU.subtract)
                px.act(tm[:, 0:8], tm[:, 0:8], AF.Exp, accum_out=m1[:, 1:2])
                px.recip(m1[:, 1:2], m1[:, 1:2])
                px.ts("vector", sc[:, :], tm[:, 0:8], m1[:, 1:2], None,
                      ALU.mult)
                px.red("vector", m1[:, 2:3], sc[:, :], ALU.max)
                px.ts("vector", tm[:, 0:8], sc[:, :], m1[:, 2:3], None,
                      ALU.is_ge)
                px.stt("vector", tm[:, 0:8], tm[:, 0:8], -1e30, sc[:, :],
                       ALU.mult, ALU.add)
                px.red("vector", m1[:, 3:4], tm[:, 0:8], ALU.max)
                px.ts("vector", tm[:, 0:8], sc[:, :], m1[:, 3:4], None,
                      ALU.is_ge)
                px.tt("vector", sc[:, :], sc[:, :], tm[:, 0:8], ALU.mult)
                px.copy("vector", wtb[:, :], sc[:, :])
                pT2 = g.pool[3][:, :].bitcast(BF16)
                px.transpose(pT2[0:8, 0:128], wtb[:, :], g.ident_bf[:, :])
                px.copy("vector", wfm[:, ts_ * 128:(ts_ + 1) * 128],
                        pT2[0:8, 0:128])
            px.dma("sync", g.w_fm_f32[:, cs], wfm[:, :])
        px.barrier()


def stage_moe_expert(px, nc, g, e):
    I = 1408
    NCI = I // 128
    with (
        nc.sbuf_tensor(f"me{e}_wgu", [128, 8 * 2816], BF16) as wgu,
        nc.sbuf_tensor(f"me{e}_wd", [128, NCI * D], BF16) as wd,
        nc.sbuf_tensor(f"me{e}_x", [128, 8 * TB], BF16) as xt,
        nc.sbuf_tensor(f"me{e}_w", [8, TB], F32) as wfm,
        nc.sbuf_tensor(f"me{e}_h", [128, NCI * TB], BF16) as hb,
        nc.sbuf_tensor(f"me{e}_gs", [128, TB], BF16) as gs,
        nc.sbuf_tensor(f"me{e}_t", [128, TB], F32) as tmp,
        nc.sbuf_tensor(f"me{e}_o", [128, TB], F32) as ob,
    ):
        load_w(px, wgu, g.e_gup[e], D, 2816)
        load_w(px, wd, g.e_down[e], I, D)
        for tb in range(NTB):
            cs = slice(tb * TB, (tb + 1) * TB)
            for k in range(NKD):
                px.dma("sync", xt[:, k * TB:(k + 1) * TB],
                       g.xn1_bf[k * 128:(k + 1) * 128, cs])
            px.dma("sync", wfm[:, :], g.w_fm_f32[:, cs])
            web = g.pool[6]
            px.mm(web[:, :], g.c_sel8[:, e * 128:(e + 1) * 128], wfm[:, :],
                  start=True, stop=True)
            for i in range(NCI):
                bg = g.pool[2 * (i % 2)]
                bu = g.pool[2 * (i % 2) + 1]
                for k in range(NKD):
                    px.mm(bg[:, :],
                          wgu[:, k * 2816 + i * 128:k * 2816 + (i + 1) * 128],
                          xt[:, k * TB:(k + 1) * TB],
                          start=(k == 0), stop=(k == NKD - 1))
                for k in range(NKD):
                    px.mm(bu[:, :],
                          wgu[:, k * 2816 + I + i * 128:k * 2816 + I + (i + 1) * 128],
                          xt[:, k * TB:(k + 1) * TB],
                          start=(k == 0), stop=(k == NKD - 1))
                px.act(gs[:, :], bg[:, :], AF.Silu)
                px.tt("vector", tmp[:, :], gs[:, :], bu[:, :], ALU.mult)
                px.tt("vector", hb[:, i * TB:(i + 1) * TB], tmp[:, :],
                      web[:, :], ALU.mult)
            for m in range(NKD):
                bank = g.pool[4 + m % 2]
                for i in range(NCI):
                    px.mm(bank[:, :], wd[:, i * D + m * 128:i * D + (m + 1) * 128],
                          hb[:, i * TB:(i + 1) * TB],
                          start=(i == 0), stop=(i == NCI - 1))
                px.act(ob[:, :], bank[:, :], AF.Copy)
                if e == 0:
                    px.dma("gpsimd", g.moe_acc[m * 128:(m + 1) * 128, cs],
                           ob[:, :])
                else:
                    px.dma("gpsimd", g.moe_acc[m * 128:(m + 1) * 128, cs],
                           ob[:, :], accum_op=ALU.add)
        px.barrier()


def stage_moe_shared(px, nc, g):
    SI = 2816
    NCS = SI // 128
    with (
        nc.sbuf_tensor("msh_wg", [128, 8 * SI], BF16) as wgt,
        nc.sbuf_tensor("msh_wu", [128, 8 * SI], BF16) as wut,
        nc.sbuf_tensor("msh_wd", [128, NCS * D], BF16) as wdt,
        nc.sbuf_tensor("msh_x", [128, 8 * TB], BF16) as xt,
        nc.sbuf_tensor("msh_h", [128, NCS * TB], BF16) as hb,
        nc.sbuf_tensor("msh_gs", [128, TB], BF16) as gs,
        nc.sbuf_tensor("msh_sw", [1, TB], F32) as swr,
        nc.sbuf_tensor("msh_t", [128, TB], F32) as tmp,
        nc.sbuf_tensor("msh_x1", [128, TB], F32) as x1t,
        nc.sbuf_tensor("msh_s1", [128, TB], F32) as s1t,
        nc.sbuf_tensor("msh_o", [128, TB], F32) as ob,
    ):
        load_w(px, wgt, g.e_shgate, D, SI)
        load_w(px, wut, g.e_shup, D, SI)
        load_w(px, wdt, g.e_shdown, SI, D)
        for tb in range(NTB):
            cs = slice(tb * TB, (tb + 1) * TB)
            for k in range(NKD):
                px.dma("sync", xt[:, k * TB:(k + 1) * TB],
                       g.xn1_bf[k * 128:(k + 1) * 128, cs])
            px.dma("sync", swr[:, :], g.sw_f32[0:1, cs])
            bsw = g.pool[6]
            px.mm(bsw[:, :], g.ones1_f32[:, :], swr[:, :], start=True,
                  stop=True)
            for i in range(NCS):
                bg = g.pool[2 * (i % 2)]
                bu = g.pool[2 * (i % 2) + 1]
                for k in range(NKD):
                    px.mm(bg[:, :],
                          wgt[:, k * SI + i * 128:k * SI + (i + 1) * 128],
                          xt[:, k * TB:(k + 1) * TB],
                          start=(k == 0), stop=(k == NKD - 1))
                for k in range(NKD):
                    px.mm(bu[:, :],
                          wut[:, k * SI + i * 128:k * SI + (i + 1) * 128],
                          xt[:, k * TB:(k + 1) * TB],
                          start=(k == 0), stop=(k == NKD - 1))
                px.act(gs[:, :], bg[:, :], AF.Silu)
                px.tt("vector", tmp[:, :], gs[:, :], bu[:, :], ALU.mult)
                px.tt("vector", hb[:, i * TB:(i + 1) * TB], tmp[:, :],
                      bsw[:, :], ALU.mult)
            for m in range(NKD):
                bank = g.pool[4 + m % 2]
                for i in range(NCS):
                    px.mm(bank[:, :],
                          wdt[:, i * D + m * 128:i * D + (m + 1) * 128],
                          hb[:, i * TB:(i + 1) * TB],
                          start=(i == 0), stop=(i == NCS - 1))
                r = slice(m * 128, (m + 1) * 128)
                px.dma("sync", tmp[:, :], g.moe_acc[r, cs])
                px.tt("vector", tmp[:, :], tmp[:, :], bank[:, :], ALU.add)
                px.act(tmp[:, :], tmp[:, :], AF.Silu)
                px.dma("sync", s1t[:, :], g.s1_f32[r, cs])
                px.dma("sync", x1t[:, :], g.x1_f32[r, cs])
                px.tt("vector", tmp[:, :], tmp[:, :], s1t[:, :], ALU.mult)
                px.tt("vector", ob[:, :], x1t[:, :], tmp[:, :], ALU.add)
                px.dma("sync", g.out_fm[r, cs], ob[:, :])
        px.barrier()




def build_program(meta):
    nc = bass.Bass("TRN2", target_bir_lowering=False)
    g = Ctx()
    g.pool, g._pool_guards = make_psum_pool(nc)

    def ein(name, shape, dt=F32):
        return nc.dram_tensor(name, list(shape), dt, kind="ExternalInput").ap()

    def scratch(name, shape, dt=F32):
        return nc.dram_tensor(name, list(shape), dt, kind="Internal").ap()

    def eout(name, shape, dt=F32):
        return nc.dram_tensor(name, list(shape), dt, kind="ExternalOutput").ap()

    g.x_fm = ein("x_fm", [D, T])
    g.c_ident = ein("c_ident", [128, 128])
    g.c_indP = ein("c_indP", [8, 128, 16])
    g.c_indT = ein("c_indT", [8, 16, 128])
    g.c_ones1 = ein("c_ones1", [1, 128])
    g.c_ones_col = ein("c_ones_col", [128, 1])

    g._const_guards = []

    def sres(name, shape, dt):
        gcm = nc.sbuf_tensor(name, list(shape), dt)
        t = gcm.__enter__()
        g._const_guards.append(gcm)
        return t

    g.ident_bf = sres("ident_bf", [128, 128], BF16)
    g.indP_bf = sres("indP_bf", [128, 8 * 16], BF16)
    g.indT_bf = sres("indT_bf", [16, 8 * 128], BF16)
    g.indT_f32 = sres("indT_f32", [16, 8 * 128], F32)
    g.ones1_f32 = sres("ones1_f32", [1, 128], F32)
    g.ones1_bf = sres("ones1_bf", [1, 128], BF16)
    g.ones_col_bf = sres("ones_col_bf", [128, 1], BF16)

    px = Px(nc)

    ACT_CONSTS = [1e-5, 1e-6, 1.0 / D, -0.5, 0.5, -1.0, 2.0, 8.0,
                  1.0 / 64, 1e-12, -1e30]
    g.constcol = sres("constcol", [128, len(ACT_CONSTS)], F32)
    for i, v in enumerate(ACT_CONSTS):
        px.memset("vector", g.constcol[:, i:i + 1], v)
        nc.const_aps.register(F32, v, g.constcol[:, i:i + 1])
    px.dma("gpsimd", g.ident_bf[:, :], g.c_ident[:, :])
    for pg in range(8):
        px.dma("gpsimd", g.indP_bf[:, pg * 16:(pg + 1) * 16], g.c_indP[pg])
        px.dma("gpsimd", g.indT_bf[:, pg * 128:(pg + 1) * 128], g.c_indT[pg])
        px.dma("sync", g.indT_f32[:, pg * 128:(pg + 1) * 128], g.c_indT[pg])
    px.dma("sync", g.ones1_f32[:, :], g.c_ones1[:, :])
    px.dma("gpsimd", g.ones1_bf[:, :], g.c_ones1[:, :])
    px.dma("gpsimd", g.ones_col_bf[:, :], g.c_ones_col[:, :])
    px.barrier()

    stages = meta.get("stages", ["ln0"])
    g.offsets = meta.get("offsets", [])
    g.n_vm = meta.get("n_vm", 0)

    g.xn_bf = scratch("xn_bf", [D, T], BF16)
    g.xn_f32 = scratch("xn_f32", [D, T])

    if "l0a_proj" in stages or "l0a_attn" in stages or "l0a_out" in stages:
        g.q_s = scratch("q_s", [D, T], BF16)
        g.k_s = scratch("k_s", [D, T], BF16)
        g.v_s = scratch("v_s", [D, T], BF16)
        g.gate_s = scratch("gate_s", [D, T], BF16)
        g.attng_s = scratch("attng_s", [D, T], BF16)
        g.s0_f32 = scratch("s0_f32", [D, T])
        g.xb_f32 = scratch("xb_f32", [D, T])
        g.xn2_bf = scratch("xn2_bf", [D, T], BF16)
    if "l0a_proj" in stages:
        g.w_qkv = ein("L0_attn_w_qkv", [D, 3 * D])
        g.w_gate = ein("L0_attn_w_gate", [D, D])
        g.norm1_scale = ein("L0_attn_norm1_scale", [D])
        g.b_gate = ein("L0_attn_b_gate", [D])
    if "l0a_attn" in stages:
        g.pb_row = ein("pb_row", [1, len(g.offsets) * 16])
        if g.n_vm:
            g.c_vmask = ein("c_vmask", [g.n_vm, 128, len(g.offsets)])
    if "l0a_out" in stages:
        g.w_out = ein("L0_attn_w_out", [D, D])
        g.norm2_scale = ein("L0_attn_norm2_scale", [D])
    if "l0a_ffn" in stages:
        g.w_ffn_gate = ein("L0_attn_w_ffn_gate", [D, 2816])
        g.w_ffn_up = ein("L0_attn_w_ffn_up", [D, 2816])
        g.w_ffn_down = ein("L0_attn_w_ffn_down", [2816, D])

    need_l0f = any(st.startswith("l0f") for st in stages)
    if need_l0f:
        g.f_Wqkv = ein("L0_ffn_Wqkv", [6, D, 3072])
        g.f_Wo = ein("L0_ffn_Wo", [6, D, D])
        g.f_qscale = ein("L0_ffn_q_scale", [6, 64])
        g.f_kscale = ein("L0_ffn_k_scale", [6, 64])
        g.f_lng = ein("L0_ffn_ln_g", [6, D])
        g.f_lnb = ein("L0_ffn_ln_b", [6, D])
        g.f_W1 = ein("L0_ffn_W1", [6, D, 5460])
        g.f_b1 = ein("L0_ffn_b1", [6, 5460])
        g.f_W2 = ein("L0_ffn_W2", [6, 2730, D])
        g.f_b2 = ein("L0_ffn_b2", [6, D])
        g.c_rot = ein("c_rot", [4, 128, T])
        g.c_R = ein("c_R", [128, 128])
        g.c_l0f_mask = ein("c_l0f_mask", [2, 2, 128, 512])
        g.c_R_bf = sres("c_R_bf", [128, 128], BF16)
        px.dma("gpsimd", g.c_R_bf[:, :], g.c_R[:, :])
        g.xl0 = scratch("xl0_f32", [D, T])
        g.xl1 = scratch("xl1_f32", [D, T])

    need_l1 = any(st.startswith(("comb", "m_", "moe")) for st in stages)
    if need_l1:
        g.x1_f32 = scratch("x1_f32", [D, T])
        g.xn1_bf = scratch("xn1_bf", [D, T], BF16)
        g.xn1_f32 = scratch("xn1_f32", [D, T])
        g.m_inproj = ein("L1_attn_in_proj_w", [D, 4096])
        g.m_convw = ein("L1_attn_conv_w", [2048, 4])
        g.m_convb = ein("L1_attn_conv_b", [2048])
        g.m_xprojw = ein("L1_attn_x_proj_w", [2048, 96])
        g.m_dtw = ein("L1_attn_dt_proj_w", [64, 2048])
        g.m_dtb = ein("L1_attn_dt_proj_b", [2048])
        g.m_Alog = ein("L1_attn_A_log", [2048, 16])
        g.m_D = ein("L1_attn_D", [2048])
        g.m_outproj = ein("L1_attn_out_proj_w", [2048, D])
        g.m_lng = ein("L1_attn_ln_g", [D])
        g.m_lnb = ein("L1_attn_ln_b", [D])
        g.m_ffw1 = ein("L1_attn_ff_w1", [D, 4096])
        g.m_ffb1 = ein("L1_attn_ff_b1", [4096])
        g.m_ffw2 = ein("L1_attn_ff_w2", [4096, D])
        g.m_ffb2 = ein("L1_attn_ff_b2", [D])
        g.c_selBC = ein("c_selBC", [96, 32 * 128])
        g.xc_f32 = scratch("xc_f32", [2048, T])
        g.z_f32 = scratch("z_f32", [2048, T])
        g.xcs_f32 = scratch("xcs_f32", [2, 2048, T])
        g.xcs_bf = scratch("xcs_bf", [2, 2048, T], BF16)
        g.ycomb_f32 = scratch("ycomb_f32", [2048, T])
        g.mh_bf = scratch("mh_bf", [D, T], BF16)
        g.s1_f32 = scratch("s1_f32", [D, T])
        g.e_gatew = ein("t_gate_w", [D, 8])
        g.e_segatew = ein("t_shared_expert_gate_w", [D, 1])
        g.e_gup = ein("t_experts_gate_up", [8, D, 2816])
        g.e_down = ein("t_experts_down", [8, 1408, D])
        g.e_shgate = ein("t_shared_gate_w", [D, 2816])
        g.e_shup = ein("t_shared_up_w", [D, 2816])
        g.e_shdown = ein("t_shared_down_w", [2816, D])
        g.c_sel8_d = ein("c_sel8", [8, 8 * 128])
        g.c_sel8 = sres("c_sel8_sb", [8, 8 * 128], F32)
        px.dma("sync", g.c_sel8[:, :], g.c_sel8_d[:, :])
        g.w_fm_f32 = scratch("w_fm_f32", [8, T])
        g.sw_f32 = scratch("sw_f32", [1, T])
        g.moe_acc = scratch("moe_acc", [D, T])
        g.out_fm = eout("out_fm", [D, T])

    for st in stages:
        if st == "ln0":
            stage_layernorm(px, nc, g, g.x_fm, g.xn_bf, g.xn_f32, "ln0")
        elif st == "l0a_proj":
            stage_l0a_proj(px, nc, g)
        elif st == "l0a_attn":
            stage_l0a_attn(px, nc, g)
        elif st == "l0a_out":
            stage_l0a_out(px, nc, g)
        elif st == "l0a_ffn":
            stage_l0a_ffn(px, nc, g)
        elif st.startswith("l0f") and st.endswith("a"):
            li = int(st[3:-1])
            src = g.xn_f32 if li == 0 else g.xl0
            stage_l0f_attn(px, nc, g, li, src, g.xl1)
        elif st.startswith("l0f") and st.endswith("f"):
            li = int(st[3:-1])
            stage_l0f_ffn(px, nc, g, li, g.xl1, g.xl0)
        elif st == "ln1x":
            stage_layernorm(px, nc, g, g.x_fm, g.xn1_bf, g.xn1_f32, "ln1x")
            for r0 in range(0, D, 128):
                px.dma("sync", g.x1_f32[r0:r0 + 128, :], g.x_fm[r0:r0 + 128, :])
            px.barrier()
        elif st == "comb0":
            stage_combine_ln(px, nc, g, g.x_fm, g.s0_f32, g.xl0,
                             g.x1_f32, g.xn1_bf, g.xn1_f32, "c0")
        elif st == "m_proj":
            stage_m_proj(px, nc, g)
        elif st == "m_conv":
            stage_m_conv(px, nc, g)
        elif st == "m_scan":
            stage_m_scan(px, nc, g)
        elif st == "m_out":
            stage_m_out(px, nc, g)
        elif st == "m_ffn":
            stage_m_ffn(px, nc, g)
        elif st == "moe_route":
            stage_moe_route(px, nc, g)
        elif st.startswith("moe_e"):
            stage_moe_expert(px, nc, g, int(st[5:]))
        elif st == "moe_shared":
            stage_moe_shared(px, nc, g)
        else:
            raise ValueError(st)

    for name in meta.get("taps", []):
        src = getattr(g, name)
        dst = eout("tap_" + name, list(src.tensor.shape), src.dtype)
        rows = int(src.tensor.shape[0])
        for r0 in range(0, rows, 128):
            rr = min(128, rows - r0)
            px.dma("sync", dst[r0:r0 + rr, :], src[r0:r0 + rr, :])
    px.barrier()
    emit_program(nc, px)
    return nc, g



ALL_STAGES = (["ln0", "l0a_proj", "l0a_attn", "l0a_out", "l0a_ffn"] +
              sum([[f"l0f{i}a", f"l0f{i}f"] for i in range(6)], []) +
              ["comb0", "m_proj", "m_conv", "m_scan", "m_out", "m_ffn",
               "moe_route"] + [f"moe_e{e}" for e in range(8)] + ["moe_shared"])


def kernel(**inputs):
    offsets = [int(v) for v in np.asarray(inputs["L0_attn_offsets"])]
    consts = build_consts(offsets)
    NO = len(offsets)
    mo = max(offsets)
    n_vm = (mo + 127) // 128
    vm = np.zeros((n_vm, 128, NO), np.float32)
    for i in range(n_vm):
        t = i * 128 + np.arange(128)
        vm[i] = np.where(t[:, None] >= np.array(offsets)[None, :], 0.0, -1e30)
    pb_row = np.ascontiguousarray(
        np.asarray(inputs["L0_attn_pos_bias"]).T.reshape(1, NO * 16))

    meta = {"stages": ALL_STAGES, "offsets": offsets, "n_vm": n_vm}
    nc, g = build_program(meta)

    base = {
        "c_ident": consts["ident"], "c_indP": consts["indP"],
        "c_indT": consts["indT"], "c_ones1": consts["ones1"],
        "c_ones_col": consts["ones_col"], "c_rot": consts["rot"],
        "c_R": consts["R"], "c_l0f_mask": consts["l0f_mask"],
        "c_selBC": consts["selBC"], "c_sel8": consts["sel8"],
        "pb_row": pb_row, "c_vmask": vm,
    }
    direct = ["L0_attn_norm1_scale", "L0_attn_norm2_scale", "L0_attn_w_qkv",
              "L0_attn_w_out", "L0_attn_w_gate", "L0_attn_b_gate",
              "L0_attn_w_ffn_gate", "L0_attn_w_ffn_up", "L0_attn_w_ffn_down",
              "L0_ffn_Wqkv", "L0_ffn_q_scale", "L0_ffn_k_scale", "L0_ffn_Wo",
              "L0_ffn_ln_g", "L0_ffn_ln_b", "L0_ffn_W1", "L0_ffn_b1",
              "L0_ffn_W2", "L0_ffn_b2", "L1_attn_in_proj_w", "L1_attn_conv_w",
              "L1_attn_conv_b", "L1_attn_x_proj_w", "L1_attn_dt_proj_w",
              "L1_attn_dt_proj_b", "L1_attn_A_log", "L1_attn_D",
              "L1_attn_out_proj_w", "L1_attn_ln_g", "L1_attn_ln_b",
              "L1_attn_ff_w1", "L1_attn_ff_b1", "L1_attn_ff_w2",
              "L1_attn_ff_b2"]
    for k in direct:
        base[k] = np.ascontiguousarray(inputs[k], dtype=np.float32)
    base["t_gate_w"] = np.ascontiguousarray(
        np.asarray(inputs["L1_ffn_gate_w"]).T, dtype=np.float32)
    base["t_shared_expert_gate_w"] = np.ascontiguousarray(
        np.asarray(inputs["L1_ffn_shared_expert_gate_w"]).T, dtype=np.float32)
    base["t_experts_gate_up"] = np.ascontiguousarray(
        np.asarray(inputs["L1_ffn_experts_gate_up"]).transpose(0, 2, 1),
        dtype=np.float32)
    base["t_experts_down"] = np.ascontiguousarray(
        np.asarray(inputs["L1_ffn_experts_down"]).transpose(0, 2, 1),
        dtype=np.float32)
    base["t_shared_gate_w"] = np.ascontiguousarray(
        np.asarray(inputs["L1_ffn_shared_gate_w"]).T, dtype=np.float32)
    base["t_shared_up_w"] = np.ascontiguousarray(
        np.asarray(inputs["L1_ffn_shared_up_w"]).T, dtype=np.float32)
    base["t_shared_down_w"] = np.ascontiguousarray(
        np.asarray(inputs["L1_ffn_shared_down_w"]).T, dtype=np.float32)

    x_full = np.asarray(inputs["L0_attn_x"])
    in_maps = []
    for c in range(8):
        mcore = dict(base)
        mcore["x_fm"] = np.ascontiguousarray(x_full[c].T, dtype=np.float32)
        in_maps.append(mcore)
    res = run_bass_kernel_spmd(nc, in_maps, core_ids=list(range(8)))
    out = np.stack([np.asarray(res.results[c]["out_fm"]).T for c in range(8)])
    return out.astype(np.float32)


# version 04 (final)
# speedup vs baseline: 1.4497x; 1.4497x over previous
import math
from contextlib import ExitStack
import numpy as np
import ml_dtypes
import concourse.bass as bass
import concourse.mybir as mybir
from concourse.ap import AP as BAP
from concourse.bass_utils import run_bass_kernel_spmd

F32 = mybir.dt.float32
BF16 = mybir.dt.bfloat16
I32 = mybir.dt.int32
AF = mybir.ActivationFunctionType
ALU = mybir.AluOpType
AX = mybir.AxisListType

T = 4096
D = 1024
TB = 512
NTB = T // TB
NKD = D // 128


QUEUES = ("sync", "tensor", "vector", "scalar", "gpsimd")
PARANOID = False
PARANOID_ENGINES = {"vector", "gpsimd", "scalar"}
DMA_SLOTS = 8
_DT_SZ = {F32: 4, BF16: 2, I32: 4}


def _ap_box(ap):
    t = ap.tensor
    name = t.name
    space = str(ap.space)
    esz = _DT_SZ.get(ap.dtype, 4)
    dims = list(ap.ap)
    off = int(ap.offset)
    if "SB" not in space and "PSUM" not in space and "Psum" not in space:
        lo = hi = 0
        for step, count in dims:
            d = int(step) * (int(count) - 1)
            if d >= 0:
                hi += d
            else:
                lo += d
        return (name, "D", 0, 0, (off + lo) * esz, (off + hi + 1) * esz)
    if "PSUM" in space or "Psum" in space:
        return (name, "S", 0, 127, 0, 1 << 30)
    tshape = list(t.shape)
    row_elems_t = 1
    for s in tshape[1:]:
        row_elems_t *= int(s)
    row_bytes = row_elems_t * _DT_SZ.get(t.dtype, 4)
    if row_bytes % esz:
        return (name, "S", 0, 127, 0, max(row_bytes, 4))
    row_elems = row_bytes // esz
    try:
        base_p = int(t.base_partition)
    except Exception:
        base_p = 0
    p_lo = p_hi = b_lo = b_hi = 0
    ok = True
    for step, count in dims:
        step = int(step)
        count = int(count)
        d = step * (count - 1)
        if step != 0 and step % row_elems == 0:
            dp = (step // row_elems) * (count - 1)
            if dp >= 0:
                p_hi += dp
            else:
                p_lo += dp
        elif abs(d) < row_elems:
            if d >= 0:
                b_hi += d
            else:
                b_lo += d
        else:
            ok = False
            break
    if not ok:
        return (name, "S", 0, 127, 0, row_bytes)
    p0 = off // row_elems
    c0 = off % row_elems
    return (name, "S", base_p + p0 + p_lo, base_p + p0 + p_hi,
            (c0 + b_lo) * esz, (c0 + b_hi) * esz + esz)


def _overlap(a, b):
    if a[0] != b[0]:
        return False
    if a[1] == "S" and (a[3] < b[2] or b[3] < a[2]):
        return False
    return not (a[5] <= b[4] or b[5] <= a[4])


class _Op:
    __slots__ = ("idx", "queue", "emit", "reads", "writes", "dma", "inc",
                 "waits", "scalar_reads", "tag", "semkey")

    def __init__(self, idx, queue, emit, reads, writes, dma, scalar_reads, tag):
        self.idx = idx
        self.queue = queue
        self.emit = emit
        self.reads = reads
        self.writes = writes
        self.dma = dma
        self.inc = False
        self.waits = []
        self.scalar_reads = scalar_reads
        self.tag = tag


class Px:

    def __init__(self, nc):
        self.nc = nc
        self.ops = []
        self.edges = []
        self._hist = {}
        self.n_bar = 0
        self._epoch_ops = 0

    def _boxes(self, aps):
        return [_ap_box(a) for a in aps if isinstance(a, BAP)]

    def _rec(self, queue, emit, reads=(), writes=(), dma=False,
             scalar_reads=(), tag=""):
        if self._epoch_ops >= 3000:
            self.barrier()
        self._epoch_ops += 1
        idx = len(self.ops)
        op = _Op(idx, queue, emit, self._boxes(reads), self._boxes(writes),
                 dma, self._boxes(scalar_reads), tag)
        dep = set()
        for box in op.reads + op.scalar_reads:
            for (j, b2, w2) in self._hist.get(box[0], ()):
                if w2 and _overlap(box, b2):
                    dep.add(j)
        for box in op.writes:
            for (j, b2, w2) in self._hist.get(box[0], ()):
                if _overlap(box, b2):
                    dep.add(j)
        for j in dep:
            p = self.ops[j]
            same = p.queue == queue and not p.dma and not dma
            if same and not PARANOID and queue not in PARANOID_ENGINES:
                continue
            self.edges.append((j, idx, same))
        for box in op.scalar_reads:
            for (j, b2, w2) in self._hist.get(box[0], ()):
                if w2 and _overlap(box, b2) and self.ops[j].queue == queue \
                        and not self.ops[j].dma:
                    self.edges.append((j, idx, True))
        for box in op.writes:
            self._hist.setdefault(box[0], []).append((idx, box, True))
        for box in op.reads + op.scalar_reads:
            self._hist.setdefault(box[0], []).append((idx, box, False))
        self.ops.append(op)
        return op

    def barrier(self):
        idx = len(self.ops)
        self.ops.append(_Op(idx, "BARRIER", None, [], [], False, [], "bar"))
        self._hist = {}
        self.n_bar += 1
        self._epoch_ops = 0

    def dma(self, q, out, in_, tag="", **kw):
        return self._rec(q, lambda e: e.dma_start(out=out, in_=in_, **kw),
                         reads=[in_], writes=[out], dma=True, tag=tag)

    def mm(self, out, lhsT, rhs, start=True, stop=True, tile_position=None,
           tag=""):
        return self._rec(
            "tensor",
            lambda e: e.matmul(out, lhsT, rhs, start=start, stop=stop,
                               tile_position=tile_position),
            reads=[lhsT, rhs], writes=[out], tag=tag)

    def transpose(self, out, in_, ident, tag=""):
        return self._rec("tensor", lambda e: e.transpose(out, in_, ident),
                         reads=[in_, ident], writes=[out], tag=tag)

    def act(self, out, in_, func, bias=0.0, scale=1.0, accum_out=None, tag=""):
        sreads = [b for b in (bias, scale) if isinstance(b, BAP)]
        writes = [out] + ([accum_out] if accum_out is not None else [])
        return self._rec(
            "scalar",
            lambda e: e.activation(out, in_, func, bias=bias, scale=scale,
                                   accum_out=accum_out),
            reads=[in_], writes=writes, scalar_reads=sreads, tag=tag)

    def tt(self, eng, out, in0, in1, op, tag=""):
        return self._rec(eng, lambda e: e.tensor_tensor(out, in0, in1, op),
                         reads=[in0, in1], writes=[out], tag=tag)

    def ts(self, eng, out, in0, s1, s2=None, op0=ALU.mult, op1=ALU.bypass,
           tag=""):
        sreads = [s for s in (s1, s2) if isinstance(s, BAP)]
        return self._rec(
            eng, lambda e: e.tensor_scalar(out, in0, s1, s2, op0, op1),
            reads=[in0], writes=[out], scalar_reads=sreads, tag=tag)

    def stt(self, eng, out, in0, scalar, in1, op0, op1, tag=""):
        sreads = [scalar] if isinstance(scalar, BAP) else []
        return self._rec(
            eng,
            lambda e: e.scalar_tensor_tensor(out, in0, scalar, in1, op0, op1),
            reads=[in0, in1], writes=[out], scalar_reads=sreads, tag=tag)

    def red(self, eng, out, in_, op, tag=""):
        return self._rec(eng, lambda e: e.tensor_reduce(out, in_, AX.X, op),
                         reads=[in_], writes=[out], tag=tag)

    def scan(self, out, d0, d1, init, op0=ALU.mult, op1=ALU.add, tag=""):
        sreads = [init] if isinstance(init, BAP) else []
        return self._rec(
            "vector",
            lambda e: e.tensor_tensor_scan(out, d0, d1, init, op0, op1),
            reads=[d0, d1], writes=[out], scalar_reads=sreads, tag=tag)

    def copy(self, eng, out, in_, tag=""):
        if eng == "scalar":
            return self._rec(eng, lambda e: e.copy(out, in_),
                             reads=[in_], writes=[out], tag=tag)
        return self._rec(eng, lambda e: e.tensor_copy(out=out, in_=in_),
                         reads=[in_], writes=[out], tag=tag)

    def memset(self, eng, ap_, v, tag=""):
        return self._rec(eng, lambda e: e.memset(ap_, v),
                         reads=[], writes=[ap_], tag=tag)

    def recip(self, out, in_, tag=""):
        return self._rec("vector", lambda e: e.reciprocal(out, in_),
                         reads=[in_], writes=[out], tag=tag)

    def iota(self, out, pattern, base=None, channel_multiplier=None, tag=""):
        return self._rec(
            "gpsimd",
            lambda e: e.iota(out, pattern, base=base,
                             channel_multiplier=channel_multiplier,
                             allow_small_or_imprecise_dtypes=True),
            reads=[], writes=[out], tag=tag)

    def finalize(self):
        last_c = {}
        forced = []
        for op in self.ops:
            if op.queue == "BARRIER":
                forced.extend(last_c.values())
                last_c = {}
            elif not op.dma:
                last_c[op.queue] = op.idx
        forced.extend(last_c.values())
        for j in forced:
            self.ops[j].inc = True
        for (j, _i, _f) in self.edges:
            self.ops[j].inc = True
        for op in self.ops:
            if op.dma:
                op.inc = True
        inc_count = {}
        dma_idx = {}
        op_inc_val = {}
        q_ops = {q: [] for q in QUEUES}
        for op in self.ops:
            if op.queue == "BARRIER":
                snap = dict(inc_count)
                for q in QUEUES:
                    q_ops[q].append(("BAR", snap))
                continue
            if op.dma:
                i = dma_idx.get(op.queue, 0)
                dma_idx[op.queue] = i + 1
                s = i % DMA_SLOTS
                j = i // DMA_SLOTS
                key = ("d", op.queue, s)
                op.semkey = key
                inc_count[key] = 16 * (j + 1)
                op_inc_val[op.idx] = (key, 16 * (j + 1))
                if j > 0:
                    op.waits.append((key, 16 * j, True))
            else:
                key = ("c", op.queue)
                if op.inc:
                    inc_count[key] = inc_count.get(key, 0) + 1
                    op_inc_val[op.idx] = (key, inc_count[key])
            q_ops[op.queue].append(("OP", op))
        for (j, i, f) in self.edges:
            key, val = op_inc_val[j]
            self.ops[i].waits.append((key, val, f))
        used_keys = set(inc_count.keys())
        sems = {}
        self._sem_guards = []
        for k in sorted(used_keys, key=str):
            gcm = self.nc.semaphore("_".join(str(p) for p in k))
            sems[k] = gcm.__enter__()
            self._sem_guards.append(gcm)
        return sems, q_ops

    def emit_queue(self, qname, eng, sems, q_ops):
        waited = {}

        def _w(key, val):
            if val <= waited.get(key, 0):
                return
            eng.wait_ge(sems[key], val)
            waited[key] = val

        for kind, payload in q_ops[qname]:
            if kind == "BAR":
                for key, val in payload.items():
                    if key == ("c", qname):
                        continue
                    _w(key, val)
                continue
            op = payload
            for w in op.waits:
                key, val = w[0], w[1]
                forced = w[2] if len(w) > 2 else False
                if key == ("c", qname) and not op.dma and not forced:
                    continue
                _w(key, val)
            inst = op.emit(eng)
            if op.inc:
                if op.dma:
                    inst.then_inc(sems[op.semkey], 16)
                else:
                    inst.then_inc(sems[("c", op.queue)], 1)


def emit_program(nc, px):
    sems, q_ops = px.finalize()
    with nc.Block() as block:
        @block.sync
        def _(e):
            px.emit_queue("sync", e, sems, q_ops)

        @block.tensor
        def _(e):
            px.emit_queue("tensor", e, sems, q_ops)

        @block.vector
        def _(e):
            px.emit_queue("vector", e, sems, q_ops)

        @block.scalar
        def _(e):
            px.emit_queue("scalar", e, sems, q_ops)

        @block.gpsimd
        def _(e):
            px.emit_queue("gpsimd", e, sems, q_ops)



class Ctx:
    pass


def make_psum_pool(nc):
    pool = []
    guards = []
    for i in range(8):
        gcm = nc.psum_tensor(f"pbank{i}", [128, TB], F32)
        pool.append(gcm.__enter__())
        guards.append(gcm)
    return pool, guards



def build_l0f_consts():
    HD, W, n = 64, 256, T
    half = np.arange(0, HD, 2, dtype=np.float64)
    inv_freq = 1.0 / 10000.0 ** (half / HD)
    t = np.arange(n, dtype=np.float64)
    freqs = t[:, None] * inv_freq[None, :]
    freqs = np.concatenate([freqs, freqs], -1)
    base_scale = (half + 0.4 * HD) / (1.4 * HD)
    power = (t - n // 2) / (W // 2)
    scale = base_scale[None, :] ** power[:, None]
    scale = np.concatenate([scale, scale], -1)
    cos = np.cos(freqs); sin = np.sin(freqs)
    rot = np.stack([
        np.tile((cos * scale).T, (2, 1)),
        np.tile((sin * scale).T, (2, 1)),
        np.tile((cos / scale).T, (2, 1)),
        np.tile((sin / scale).T, (2, 1)),
    ]).astype(np.float32)
    R = np.zeros((128, 128), np.float32)
    for blk in range(2):
        b = blk * 64
        for i in range(32):
            R[b + 32 + i, b + i] = -1.0
            R[b + i, b + 32 + i] = 1.0
    msk = np.zeros((2, 2, 128, 512), np.float32)
    p = np.arange(128)[:, None]; c = np.arange(512)[None, :]
    for qs in range(2):
        valid_n = (c >= qs * 128 + p) & (c <= qs * 128 + p + 256)
        msk[0, qs] = np.where(valid_n, 0.0, -1e30)
        valid_0 = (c >= 256) & (c <= 256 + qs * 128 + p)
        msk[1, qs] = np.where(valid_0, 0.0, -1e30)
    return rot, R, msk


def build_consts(offsets):
    c = {}
    c["ident"] = np.eye(128, dtype=np.float32)
    indP = np.zeros((8, 128, 16), np.float32)
    indT = np.zeros((8, 16, 128), np.float32)
    for pg in range(8):
        for p in range(128):
            h = 2 * pg + p // 64
            indP[pg, p, h] = 1.0
            indT[pg, h, p] = 1.0
    c["indP"] = indP
    c["indT"] = indT
    c["ones1"] = np.ones((1, 128), np.float32)
    c["ones_col"] = np.ones((128, 1), np.float32)
    rot, R, msk = build_l0f_consts()
    c["rot"] = rot
    c["R"] = R
    c["l0f_mask"] = msk
    selBC = np.zeros((96, 32 * 128), np.float32)
    for n in range(16):
        selBC[64 + n, (2 * n) * 128:(2 * n + 1) * 128] = 1.0
        selBC[80 + n, (2 * n + 1) * 128:(2 * n + 2) * 128] = 1.0
    c["selBC"] = selBC
    sel8 = np.zeros((8, 8 * 128), np.float32)
    for e in range(8):
        sel8[e, e * 128:(e + 1) * 128] = 1.0
    c["sel8"] = sel8
    return c



def stage_layernorm(px, nc, g, src_f32, dst_bf, dst_f32, name, eps=1e-5):
    with (
        nc.sbuf_tensor(name + "_x", [128, NKD * TB], F32) as xt,
        nc.sbuf_tensor(name + "_xb", [128, NKD * TB], BF16) as xbt,
        nc.sbuf_tensor(name + "_sq", [128, NKD * TB], BF16) as sqt,
        nc.sbuf_tensor(name + "_row", [1, 6 * TB], F32) as row,
        nc.sbuf_tensor(name + "_o", [128, NKD * TB], F32) as ot,
        nc.sbuf_tensor(name + "_ob", [128, NKD * TB], BF16) as otb,
    ):
        SUM, SQ, VAR, RSTD = 0, TB, 2 * TB, 3 * TB
        for tb in range(NTB):
            cs = slice(tb * TB, tb * TB + TB)
            for k in range(NKD):
                sl = slice(k * TB, (k + 1) * TB)
                px.dma("sync", xt[:, sl], src_f32[k * 128:(k + 1) * 128, cs])
                px.dma("gpsimd", xbt[:, sl], src_f32[k * 128:(k + 1) * 128, cs])
                px.act(sqt[:, sl], xbt[:, sl], AF.Square)
            ps_s = g.pool[0]
            ps_q = g.pool[1]
            for k in range(NKD):
                sl = slice(k * TB, (k + 1) * TB)
                px.mm(ps_s[0:1, :], g.ones_col_bf[:, 0:1], xbt[:, sl],
                      start=(k == 0), stop=(k == NKD - 1))
            for k in range(NKD):
                sl = slice(k * TB, (k + 1) * TB)
                px.mm(ps_q[0:1, :], g.ones_col_bf[:, 0:1], sqt[:, sl],
                      start=(k == 0), stop=(k == NKD - 1))
            px.act(row[:, SUM:SUM + TB], ps_s[0:1, :], AF.Copy, scale=1.0 / D)
            px.act(row[:, SQ:SQ + TB], ps_q[0:1, :], AF.Copy, scale=1.0 / D)
            px.tt("vector", row[:, VAR:VAR + TB], row[:, SUM:SUM + TB],
                  row[:, SUM:SUM + TB], ALU.mult)
            px.tt("vector", row[:, VAR:VAR + TB], row[:, SQ:SQ + TB],
                  row[:, VAR:VAR + TB], ALU.subtract)
            px.act(row[:, RSTD:RSTD + TB], row[:, VAR:VAR + TB], AF.Ln,
                   bias=eps)
            px.act(row[:, RSTD:RSTD + TB], row[:, RSTD:RSTD + TB], AF.Exp,
                   scale=-0.5)
            if getattr(g, "ln_debug", None) is not None:
                px.dma("sync", g.ln_debug[tb, 0][None, :], row[0:1, SUM:SUM + TB])
                px.dma("sync", g.ln_debug[tb, 1][None, :], row[0:1, SQ:SQ + TB])
                px.dma("sync", g.ln_debug[tb, 2][None, :], row[0:1, VAR:VAR + TB])
                px.dma("sync", g.ln_debug[tb, 3][None, :], row[0:1, RSTD:RSTD + TB])
            ps_m = g.pool[2]
            ps_r = g.pool[3]
            px.mm(ps_m[:, :], g.ones1_f32[:, :], row[:, SUM:SUM + TB])
            px.mm(ps_r[:, :], g.ones1_f32[:, :], row[:, RSTD:RSTD + TB])
            for k in range(NKD):
                sl = slice(k * TB, (k + 1) * TB)
                px.tt("vector", ot[:, sl], xt[:, sl], ps_m[:, :], ALU.subtract)
                px.tt("vector", ot[:, sl], ot[:, sl], ps_r[:, :], ALU.mult)
                px.copy("scalar", otb[:, sl], ot[:, sl])
                if dst_f32 is not None:
                    px.dma("sync", dst_f32[k * 128:(k + 1) * 128, cs],
                           ot[:, sl])
                px.dma("gpsimd", dst_bf[k * 128:(k + 1) * 128, cs],
                       otb[:, sl])
        px.barrier()



def kchunks(K):
    out = []
    k0 = 0
    while k0 < K:
        out.append((k0, min(128, K - k0)))
        k0 += 128
    return out


def load_w(px, sb, dram_w, K, M, tag="w"):
    for i, (k0, kk) in enumerate(kchunks(K)):
        px.dma("gpsimd", sb[:kk, i * M:i * M + M],
               dram_w[k0:k0 + kk, :], tag=tag)


def load_vec_cols(px, sb, dram_v, K, col0=0, tag="v"):
    for i, (k0, kk) in enumerate(kchunks(K)):
        px.dma("sync", sb[:kk, col0 + i:col0 + i + 1],
               dram_v[k0:k0 + kk][:, None], tag=tag)



def stage_l0a_proj(px, nc, g):
    with (
        nc.sbuf_tensor("ap_wqkv", [128, 8 * 3072], BF16) as wqkv,
        nc.sbuf_tensor("ap_wg", [128, 8 * 1024], BF16) as wg,
        nc.sbuf_tensor("ap_cols", [128, 16], F32) as cols,
        nc.sbuf_tensor("ap_xn", [128, 8 * TB], BF16) as xn,
        nc.sbuf_tensor("ap_xa", [128, 8 * TB], BF16) as xa,
        nc.sbuf_tensor("ap_sq", [128, 8 * TB], BF16) as sq,
        nc.sbuf_tensor("ap_row", [1, 2 * TB], F32) as row,
        nc.sbuf_tensor("ap_tmp", [128, TB], F32) as tmp,
        nc.sbuf_tensor("ap_ob", [128, 4 * TB], BF16) as ob,
    ):
        load_w(px, wqkv, g.w_qkv, D, 3072)
        load_w(px, wg, g.w_gate, D, D)
        load_vec_cols(px, cols, g.norm1_scale, D, 0)
        load_vec_cols(px, cols, g.b_gate, D, 8)
        for tb in range(NTB):
            cs = slice(tb * TB, (tb + 1) * TB)
            for k in range(NKD):
                sl = slice(k * TB, (k + 1) * TB)
                px.dma("sync", xn[:, sl], g.xn_bf[k * 128:(k + 1) * 128, cs])
                px.act(sq[:, sl], xn[:, sl], AF.Square)
            ps = g.pool[6]
            for k in range(NKD):
                px.mm(ps[0:1, :], g.ones_col_bf[:, 0:1],
                      sq[:, k * TB:(k + 1) * TB],
                      start=(k == 0), stop=(k == NKD - 1))
            px.act(row[:, 0:TB], ps[0:1, :], AF.Ln, bias=1e-6, scale=1.0 / D)
            px.act(row[:, 0:TB], row[:, 0:TB], AF.Exp, scale=-0.5)
            ps_r = g.pool[7]
            px.mm(ps_r[:, :], g.ones1_f32[:, :], row[:, 0:TB])
            for k in range(NKD):
                sl = slice(k * TB, (k + 1) * TB)
                px.tt("vector", tmp[:, :], xn[:, sl], ps_r[:, :], ALU.mult)
                px.ts("vector", xa[:, sl], tmp[:, :], cols[:, k:k + 1], None,
                      ALU.mult)
            for m in range(24):
                bank = g.pool[m % 4]
                for k in range(NKD):
                    px.mm(bank[:, :], wqkv[:, k * 3072 + m * 128:
                                           k * 3072 + (m + 1) * 128],
                          xa[:, k * TB:(k + 1) * TB],
                          start=(k == 0), stop=(k == NKD - 1))
                dst = (g.q_s, g.k_s, g.v_s)[m // 8]
                mm = m % 8
                o = ob[:, (m % 4) * TB:(m % 4 + 1) * TB]
                px.act(o, bank[:, :], AF.Copy)
                px.dma("gpsimd", dst[mm * 128:(mm + 1) * 128, cs], o)
            for m in range(NKD):
                bank = g.pool[4 + m % 2]
                for k in range(NKD):
                    px.mm(bank[:, :], wg[:, k * D + m * 128:
                                         k * D + (m + 1) * 128],
                          xa[:, k * TB:(k + 1) * TB],
                          start=(k == 0), stop=(k == NKD - 1))
                o = ob[:, (m % 4) * TB:(m % 4 + 1) * TB]
                px.act(o, bank[:, :], AF.Sigmoid, bias=cols[:, 8 + m:9 + m])
                px.dma("gpsimd", g.gate_s[m * 128:(m + 1) * 128, cs], o)
        px.barrier()



def stage_l0a_attn(px, nc, g):
    offs = g.offsets
    max_seg = getattr(g, "aa_max_seg", NTB)
    do_sm = getattr(g, "aa_do_sm", True)
    do_av = getattr(g, "aa_do_av", True)
    NO = len(offs)
    WIN = 2048
    assert max(offs) + TB <= WIN and min(offs) >= 0
    NOC = NO * 16
    with (
        nc.sbuf_tensor("aa_q", [128, 8 * TB], BF16) as q,
        nc.sbuf_tensor("aa_kw", [128, 8 * WIN], BF16) as kw,
        nc.sbuf_tensor("aa_vw", [128, 8 * WIN], BF16) as vw,
        nc.sbuf_tensor("aa_gt", [128, 8 * TB], BF16) as gt,
        nc.sbuf_tensor("aa_P", [128, 4 * TB], BF16) as Pb,
        nc.sbuf_tensor("aa_sc", [16, TB], BF16) as scb,
        nc.sbuf_tensor("aa_S", [128, 4 * NOC], F32) as Sb,
        nc.sbuf_tensor("aa_al", [128, 4 * NOC], BF16) as al,
        nc.sbuf_tensor("aa_m16", [128, 128], F32) as m16,
        nc.sbuf_tensor("aa_pb", [128, NOC], F32) as pbb,
        nc.sbuf_tensor("aa_pbrow", [1, NOC], F32) as pbrow,
        nc.sbuf_tensor("aa_vm", [128, g.n_vm * NO if g.n_vm else 1], F32) as vm,
        nc.sbuf_tensor("aa_aT", [16, TB], BF16) as aT,
        nc.sbuf_tensor("aa_tmp", [128, TB], F32) as tmpb,
        nc.sbuf_tensor("aa_acc", [128, 8 * TB], F32) as acc,
        nc.sbuf_tensor("aa_ob", [128, 2 * TB], BF16) as ob,
    ):
        px.dma("sync", pbrow[:, :], g.pb_row[:, :])
        if g.n_vm:
            for i in range(g.n_vm):
                px.dma("sync", vm[:, i * NO:(i + 1) * NO], g.c_vmask[i])
        px.mm(g.pool[0][:, :], g.ones1_f32[:, :], pbrow[:, 0:512])
        px.mm(g.pool[1][:, :192], g.ones1_f32[:, :], pbrow[:, 512:NOC])
        px.copy("vector", pbb[:, 0:512], g.pool[0][:, :])
        px.copy("vector", pbb[:, 512:NOC], g.pool[1][:, :192])
        for s in range(max_seg):
            w0 = s * TB - (WIN - TB)
            nz = max(0, -w0)
            for pg in range(8):
                r = slice(pg * 128, (pg + 1) * 128)
                px.dma("sync", q[:, pg * TB:(pg + 1) * TB],
                       g.q_s[r, s * TB:(s + 1) * TB])
                px.dma("sync", gt[:, pg * TB:(pg + 1) * TB],
                       g.gate_s[r, s * TB:(s + 1) * TB])
                if nz:
                    px.memset("gpsimd", kw[:, pg * WIN:pg * WIN + nz], 0)
                    px.memset("gpsimd", vw[:, pg * WIN:pg * WIN + nz], 0)
                px.dma("sync", kw[:, pg * WIN + nz:(pg + 1) * WIN],
                       g.k_s[r, w0 + nz:s * TB + TB])
                px.dma("sync", vw[:, pg * WIN + nz:(pg + 1) * WIN],
                       g.v_s[r, w0 + nz:s * TB + TB])
                px.memset("vector", acc[:, pg * TB:(pg + 1) * TB], 0)
            for oi, o in enumerate(offs):
                sh = WIN - TB - o
                ps_sc = g.pool[oi % 2]
                for pg in range(8):
                    P = Pb[:, (pg % 4) * TB:(pg % 4 + 1) * TB]
                    px.tt("vector" if pg % 2 == 0 else "gpsimd", P,
                          q[:, pg * TB:(pg + 1) * TB],
                          kw[:, pg * WIN + sh:pg * WIN + sh + TB], ALU.mult)
                    px.mm(ps_sc[0:16, :], g.indP_bf[:, pg * 16:(pg + 1) * 16],
                          P, start=(pg == 0), stop=(pg == 7))
                px.ts("vector", scb[:, :], ps_sc[0:16, :], 0.125, None,
                      ALU.mult)
                pT = g.pool[2 + oi % 2][:, :].bitcast(BF16)
                for ts in range(4 if getattr(g, "aa_do_tr", True) else 0):
                    px.transpose(pT[:, ts * 16:(ts + 1) * 16],
                                 scb[:, ts * 128:(ts + 1) * 128],
                                 g.ident_bf[0:16, 0:16])
                for ts in range(4 if getattr(g, "aa_do_ev", True) and getattr(g, "aa_do_tr", True) else 0):
                    if getattr(g, "aa_ev_contig", False):
                        dst = Sb[:, ts * NOC + oi * 16:ts * NOC + oi * 16 + 16]
                        src = pT[:, ts * 16:(ts + 1) * 16]
                    else:
                        dst = Sb[:, ts * NOC:(ts + 1) * NOC].rearrange(
                            "p (h o) -> p h o", o=NO)[:, :, oi:oi + 1]
                        src = pT[:, ts * 16:(ts + 1) * 16].unsqueeze(2)
                    px.copy("vector", dst, src)
            for ts in range(4 if do_sm else 0):
                S3 = Sb[:, ts * NOC:(ts + 1) * NOC].rearrange(
                    "p (h o) -> p h o", o=NO)
                pb3 = pbb[:, :].rearrange("p (h o) -> p h o", o=NO)
                px.tt("vector", S3, S3, pb3, ALU.add)
                gm = s * 4 + ts
                if gm < g.n_vm:
                    vm3 = vm[:, gm * NO:(gm + 1) * NO].unsqueeze(1) \
                        .broadcast_to([128, 16, NO])
                    px.tt("vector", S3, S3, vm3, ALU.add)
                mx = m16[:, ts * 16:(ts + 1) * 16]
                px.red("vector", mx, S3, ALU.max)
                mx3 = mx.unsqueeze(2).broadcast_to([128, 16, NO])
                px.tt("vector", S3, S3, mx3, ALU.subtract)
                px.act(Sb[:, ts * NOC:(ts + 1) * NOC],
                       Sb[:, ts * NOC:(ts + 1) * NOC], AF.Exp)
                sm = m16[:, 64 + ts * 16:64 + (ts + 1) * 16]
                px.red("vector", sm, S3, ALU.add)
                px.recip(sm, sm)
                sm3 = sm.unsqueeze(2).broadcast_to([128, 16, NO])
                px.tt("vector", al[:, ts * NOC:(ts + 1) * NOC].rearrange(
                    "p (h o) -> p h o", o=NO), S3, sm3, ALU.mult)
            for oi, o in enumerate(offs if do_av else []):
                sh = WIN - TB - o
                pT = g.pool[4 + oi % 2][:, :].bitcast(BF16)
                for ts in range(4):
                    src = al[:, ts * NOC:(ts + 1) * NOC].rearrange(
                        "p (h o) -> p h o", o=NO)[:, :, oi:oi + 1].squeeze(2)
                    px.transpose(pT[0:16, ts * 128:(ts + 1) * 128], src,
                                 g.ident_bf[:, :])
                    px.copy("vector", aT[:, ts * 128:(ts + 1) * 128],
                            pT[0:16, ts * 128:(ts + 1) * 128])
                for pg in range(8):
                    ps_ab = g.pool[6 + pg % 2]
                    px.mm(ps_ab[:, :], g.indT_bf[:, pg * 128:(pg + 1) * 128],
                          aT[:, :], start=True, stop=True)
                    px.tt("vector", tmpb[:, :],
                          vw[:, pg * WIN + sh:pg * WIN + sh + TB],
                          ps_ab[:, :], ALU.mult)
                    px.tt("gpsimd", acc[:, pg * TB:(pg + 1) * TB],
                          acc[:, pg * TB:(pg + 1) * TB], tmpb[:, :], ALU.add)
            for pg in range(8):
                o = ob[:, (pg % 2) * TB:(pg % 2 + 1) * TB]
                px.tt("vector", o, acc[:, pg * TB:(pg + 1) * TB],
                      gt[:, pg * TB:(pg + 1) * TB], ALU.mult)
                px.dma("gpsimd", g.attng_s[pg * 128:(pg + 1) * 128,
                                           s * TB:(s + 1) * TB], o)
        px.barrier()



def stage_l0a_out(px, nc, g):
    with (
        nc.sbuf_tensor("ao_wout", [128, 8 * D], BF16) as wout,
        nc.sbuf_tensor("ao_cols", [128, 8], F32) as cols,
        nc.sbuf_tensor("ao_ag", [128, 8 * TB], BF16) as ag,
        nc.sbuf_tensor("ao_xb", [128, 8 * TB], F32) as xb,
        nc.sbuf_tensor("ao_xn2", [128, 8 * TB], BF16) as xn2,
        nc.sbuf_tensor("ao_row", [1, TB], F32) as row,
        nc.sbuf_tensor("ao_tmp", [128, TB], F32) as tmpb,
    ):
        load_w(px, wout, g.w_out, D, D)
        load_vec_cols(px, cols, g.norm2_scale, D, 0)
        for tb in range(NTB):
            cs = slice(tb * TB, (tb + 1) * TB)
            for k in range(NKD):
                sl = slice(k * TB, (k + 1) * TB)
                px.dma("sync", ag[:, sl], g.attng_s[k * 128:(k + 1) * 128, cs])
            for m in range(NKD):
                bank = g.pool[m % 3]
                for k in range(NKD):
                    px.mm(bank[:, :],
                          wout[:, k * D + m * 128:k * D + (m + 1) * 128],
                          ag[:, k * TB:(k + 1) * TB],
                          start=(k == 0), stop=(k == NKD - 1))
                sl = slice(m * TB, (m + 1) * TB)
                px.dma("sync", xb[:, sl], g.xn_f32[m * 128:(m + 1) * 128, cs])
                px.tt("vector", xb[:, sl], xb[:, sl], bank[:, :], ALU.add)
                px.dma("gpsimd", g.xb_f32[m * 128:(m + 1) * 128, cs],
                       xb[:, sl])
            for k in range(NKD):
                sl = slice(k * TB, (k + 1) * TB)
                px.act(ag[:, sl], xb[:, sl], AF.Square)
            ps = g.pool[6]
            for k in range(NKD):
                px.mm(ps[0:1, :], g.ones_col_bf[:, 0:1],
                      ag[:, k * TB:(k + 1) * TB],
                      start=(k == 0), stop=(k == NKD - 1))
            px.act(row[:, :], ps[0:1, :], AF.Ln, bias=1e-6, scale=1.0 / D)
            px.act(row[:, :], row[:, :], AF.Exp, scale=-0.5)
            ps_r = g.pool[7]
            px.mm(ps_r[:, :], g.ones1_f32[:, :], row[:, :])
            for k in range(NKD):
                sl = slice(k * TB, (k + 1) * TB)
                px.tt("vector", tmpb[:, :], xb[:, sl], ps_r[:, :], ALU.mult)
                px.ts("vector", xn2[:, sl], tmpb[:, :], cols[:, k:k + 1],
                      None, ALU.mult)
                px.dma("gpsimd", g.xn2_bf[k * 128:(k + 1) * 128, cs],
                       xn2[:, sl])
        px.barrier()


def stage_l0a_ffn(px, nc, g):
    FFN = 2816
    NM_F = FFN // 128
    with (
        nc.sbuf_tensor("af_wgate", [128, 8 * FFN], BF16) as wgate,
        nc.sbuf_tensor("af_wup", [128, 8 * FFN], BF16) as wup,
        nc.sbuf_tensor("af_wdown", [128, NM_F * D], BF16) as wdown,
        nc.sbuf_tensor("af_xn2", [128, 8 * TB], BF16) as xn2,
        nc.sbuf_tensor("af_h", [128, NM_F * TB], BF16) as hb,
        nc.sbuf_tensor("af_gs", [128, TB], BF16) as gs,
        nc.sbuf_tensor("af_xbt", [128, TB], F32) as xbt,
        nc.sbuf_tensor("af_o", [128, TB], F32) as outb,
    ):
        load_w(px, wgate, g.w_ffn_gate, D, FFN)
        load_w(px, wup, g.w_ffn_up, D, FFN)
        load_w(px, wdown, g.w_ffn_down, FFN, D)
        for tb in range(NTB):
            cs = slice(tb * TB, (tb + 1) * TB)
            for k in range(NKD):
                sl = slice(k * TB, (k + 1) * TB)
                px.dma("sync", xn2[:, sl], g.xn2_bf[k * 128:(k + 1) * 128, cs])
            for m in range(NM_F):
                bg = g.pool[2 * (m % 2)]
                bu = g.pool[2 * (m % 2) + 1]
                for k in range(NKD):
                    px.mm(bg[:, :],
                          wgate[:, k * FFN + m * 128:k * FFN + (m + 1) * 128],
                          xn2[:, k * TB:(k + 1) * TB],
                          start=(k == 0), stop=(k == NKD - 1))
                for k in range(NKD):
                    px.mm(bu[:, :],
                          wup[:, k * FFN + m * 128:k * FFN + (m + 1) * 128],
                          xn2[:, k * TB:(k + 1) * TB],
                          start=(k == 0), stop=(k == NKD - 1))
                px.act(gs[:, :], bg[:, :], AF.Silu)
                px.tt("vector", hb[:, m * TB:(m + 1) * TB], gs[:, :],
                      bu[:, :], ALU.mult)
            for m in range(NKD):
                bank = g.pool[4 + m % 2]
                for k in range(NM_F):
                    px.mm(bank[:, :],
                          wdown[:, k * D + m * 128:k * D + (m + 1) * 128],
                          hb[:, k * TB:(k + 1) * TB],
                          start=(k == 0), stop=(k == NM_F - 1))
                px.dma("sync", xbt[:, :], g.xb_f32[m * 128:(m + 1) * 128, cs])
                px.tt("vector", outb[:, :], xbt[:, :], bank[:, :], ALU.add)
                px.dma("sync", g.s0_f32[m * 128:(m + 1) * 128, cs],
                       outb[:, :])
        px.barrier()



def _shift_load(px, g, dst_bf, src_f32, pg, tb):
    cs0 = tb * TB
    r = slice(pg * 128, (pg + 1) * 128)
    if pg < 4:
        px.dma("gpsimd", dst_bf, src_f32[r, cs0:cs0 + TB])
    elif tb == 0:
        px.memset("vector", dst_bf[:, 0:1], 0)
        px.dma("gpsimd", dst_bf[:, 1:TB], src_f32[r, 0:TB - 1])
    else:
        px.dma("gpsimd", dst_bf, src_f32[r, cs0 - 1:cs0 + TB - 1])


def stage_l0f_attn(px, nc, g, li, src, dst):
    specs = [
        ("fa_wqkv", [128, 8 * 3072], BF16), ("fa_wo", [128, 8 * D], BF16),
        ("fa_qkv", [128, 24 * TB], BF16), ("fa_sq", [128, TB], BF16),
        ("fa_xs", [128, 8 * TB], BF16), ("fa_rr", [16, 2 * TB], F32),
        ("fa_rot", [128, TB], F32), ("fa_eff", [128, 4 * TB], BF16),
        ("fa_qsks", [128, 2], F32), ("fa_qcur", [128, 8 * TB], BF16),
        ("fa_kring", [128, 2 * 8 * TB], BF16), ("fa_vtm", [128, 2 * 4 * 1024], BF16),
        ("fa_msk", [128, 4 * TB], F32), ("fa_sm", [128, 2 * TB], F32),
        ("fa_mx", [128, 4], F32), ("fa_al", [128, 2 * TB], BF16),
        ("fa_aT", [128, 4 * 256], BF16), ("fa_ao", [128, 8 * TB], BF16),
        ("fa_xl", [128, TB], F32), ("fa_o", [128, TB], F32),
        ("fa_t1", [128, TB], F32), ("fa_t2", [128, TB], F32),
    ]
    with ExitStack() as _st:
        (wqkv, wo, qkvb, sqb, xsb, rr, rotb, eff, qsks, qcur, kring, vtm,
         msk, smb, mxc, alb, aTb, aob, xlb, outb, t1, t2) = [
            _st.enter_context(nc.sbuf_tensor(f"{n}_{li}", sh, dt))
            for (n, sh, dt) in specs]
        load_w(px, wqkv, g.f_Wqkv[li], D, 3072)
        load_w(px, wo, g.f_Wo[li], D, D)
        px.dma("sync", qsks[0:64, 0:1], g.f_qscale[li][:, None])
        px.dma("sync", qsks[64:128, 0:1], g.f_qscale[li][:, None])
        px.dma("sync", qsks[0:64, 1:2], g.f_kscale[li][:, None])
        px.dma("sync", qsks[64:128, 1:2], g.f_kscale[li][:, None])
        for kind in range(2):
            for qs_ in range(2):
                px.dma("sync",
                       msk[:, (kind * 2 + qs_) * TB:(kind * 2 + qs_ + 1) * TB],
                       g.c_l0f_mask[kind, qs_])
        px.memset("vector", kring[:, :], 0)
        px.memset("gpsimd", vtm[:, :], 0)
        for tb in range(NTB):
            slot = tb % 2
            cs = slice(tb * TB, (tb + 1) * TB)
            for j in range(4):
                px.dma("sync", rotb[:, :], g.c_rot[j][:, cs])
                px.copy("vector", eff[:, j * TB:(j + 1) * TB], rotb[:, :])
            for pg in range(8):
                _shift_load(px, g, xsb[:, pg * TB:(pg + 1) * TB], src, pg, tb)
            for m in range(24):
                bank = g.pool[m % 2]
                for k in range(NKD):
                    px.mm(bank[:, :],
                          wqkv[:, k * 3072 + m * 128:k * 3072 + (m + 1) * 128],
                          xsb[:, k * TB:(k + 1) * TB],
                          start=(k == 0), stop=(k == NKD - 1))
                px.act(qkvb[:, m * TB:(m + 1) * TB], bank[:, :], AF.Copy)
                if m < 16:
                    px.act(sqb[:, :], bank[:, :], AF.Square)
                    ss = g.pool[2 + m // 8]
                    px.mm(ss[0:16, :], g.indP_bf[:, (m % 8) * 16:(m % 8 + 1) * 16],
                          sqb[:, :], start=(m % 8 == 0), stop=(m % 8 == 7))
            for j in range(2):
                px.act(rr[:, j * TB:(j + 1) * TB], g.pool[2 + j][0:16, :], AF.Ln)
                px.act(rr[:, j * TB:(j + 1) * TB], rr[:, j * TB:(j + 1) * TB],
                       AF.Exp, scale=-0.5)
                px.ts("vector", rr[:, j * TB:(j + 1) * TB],
                      rr[:, j * TB:(j + 1) * TB], 1e12, None, ALU.min)
            for m in range(16):
                pg = m % 8
                isq = m < 8
                rsl = rr[:, 0:TB] if isq else rr[:, TB:2 * TB]
                rb = g.pool[4]
                px.mm(rb[:, :], g.indT_f32[:, pg * 128:(pg + 1) * 128], rsl,
                      start=True, stop=True)
                qn = xsb[:, pg * TB:(pg + 1) * TB]
                px.tt("vector", qn, qkvb[:, m * TB:(m + 1) * TB], rb[:, :],
                      ALU.mult)
                px.ts("vector", qn, qn, qsks[:, 0:1] if isq else qsks[:, 1:2],
                      None, ALU.mult)
                rh = g.pool[5]
                px.mm(rh[:, :], g.c_R_bf[:, :], qn, start=True, stop=True)
                co = eff[:, 0 * TB:1 * TB] if isq else eff[:, 2 * TB:3 * TB]
                si = eff[:, 1 * TB:2 * TB] if isq else eff[:, 3 * TB:4 * TB]
                px.tt("gpsimd", t1[:, :], qn, co, ALU.mult)
                px.tt("vector", t2[:, :], rh[:, :], si, ALU.mult)
                dstq = qcur[:, pg * TB:(pg + 1) * TB] if isq else \
                    kring[:, (slot * 8 + pg) * TB:(slot * 8 + pg + 1) * TB]
                px.tt("vector", dstq, t1[:, :], t2[:, :], ALU.add)
            for pg in range(8):
                m = 16 + pg
                for ts_ in range(4):
                    pT = g.pool[6][:, :].bitcast(BF16)
                    px.transpose(pT[:, 0:128],
                                 qkvb[:, m * TB + ts_ * 128:m * TB + (ts_ + 1) * 128],
                                 g.ident_bf[:, :])
                    px.copy("vector", vtm[:, slot * 4096 + ts_ * 1024 + pg * 128:
                                          slot * 4096 + ts_ * 1024 + (pg + 1) * 128],
                            pT[:, 0:128])
            for w2 in range(2):
                w = tb * 2 + w2
                kind = 1 if w == 0 else 0
                for h in range(16):
                    pg = h // 2
                    qrow = slice((h % 2) * 64, (h % 2) * 64 + 64)
                    for qs_ in range(2):
                        sc = g.pool[qs_]
                        lhs = qcur[qrow, pg * TB + w2 * 256 + qs_ * 128:
                                   pg * TB + w2 * 256 + (qs_ + 1) * 128]
                        for jh in range(2):
                            j0 = (w - 1) * 256 + jh * 256
                            if j0 < 0:
                                px.memset("vector", sc[:, jh * 256:(jh + 1) * 256], 0)
                                continue
                            ksl = (j0 // 512) % 2
                            kof = j0 % 512
                            px.mm(sc[:, jh * 256:(jh + 1) * 256], lhs,
                                  kring[qrow, (ksl * 8 + pg) * TB + kof:
                                        (ksl * 8 + pg) * TB + kof + 256],
                                  start=True, stop=True)
                        mx = mxc[:, qs_:qs_ + 1]
                        px.red("vector", mx, sc[:, :], ALU.max)
                        px.stt("vector", smb[:, qs_ * TB:(qs_ + 1) * TB],
                               sc[:, :], mx, msk[:, (kind * 2 + qs_) * TB:
                                                 (kind * 2 + qs_ + 1) * TB],
                               ALU.subtract, ALU.add)
                        px.act(smb[:, qs_ * TB:(qs_ + 1) * TB],
                               smb[:, qs_ * TB:(qs_ + 1) * TB], AF.Exp,
                               scale=8.0, accum_out=mxc[:, 2 + qs_:3 + qs_])
                        px.recip(mxc[:, 2 + qs_:3 + qs_], mxc[:, 2 + qs_:3 + qs_])
                        px.ts("vector", alb[:, qs_ * TB:(qs_ + 1) * TB],
                              smb[:, qs_ * TB:(qs_ + 1) * TB],
                              mxc[:, 2 + qs_:3 + qs_], None, ALU.mult)
                        pT = g.pool[2 + qs_][:, :].bitcast(BF16)
                        for jc in range(4):
                            px.transpose(pT[:, jc * 128:(jc + 1) * 128],
                                         alb[:, qs_ * TB + jc * 128:
                                             qs_ * TB + (jc + 1) * 128],
                                         g.ident_bf[:, :])
                        for jc in range(4):
                            px.copy("vector",
                                    aTb[:, jc * 256 + qs_ * 128:
                                        jc * 256 + (qs_ + 1) * 128],
                                    pT[:, jc * 128:(jc + 1) * 128])
                    av = g.pool[4 + (h % 2)]
                    nj = 0
                    for jc in range(4):
                        j0 = (w - 1) * 256 + jc * 128
                        if j0 < 0:
                            continue
                        vsl = (j0 // 512) % 2
                        vtt = (j0 % 512) // 128
                        px.mm(av[0:64, 0:256],
                              vtm[:, vsl * 4096 + vtt * 1024 + h * 64:
                                  vsl * 4096 + vtt * 1024 + (h + 1) * 64],
                              aTb[:, jc * 256:(jc + 1) * 256],
                              start=(nj == 0), stop=(jc == 3))
                        nj += 1
                    px.act(aob[qrow, pg * TB + w2 * 256:pg * TB + w2 * 256 + 256],
                           av[0:64, 0:256], AF.Copy)
            for m in range(NKD):
                bank = g.pool[6 + m % 2]
                for k in range(NKD):
                    px.mm(bank[:, :], wo[:, k * D + m * 128:k * D + (m + 1) * 128],
                          aob[:, k * TB:(k + 1) * TB],
                          start=(k == 0), stop=(k == NKD - 1))
                px.dma("sync", xlb[:, :], src[m * 128:(m + 1) * 128, cs])
                px.tt("vector", outb[:, :], xlb[:, :], bank[:, :], ALU.add)
                px.dma("sync", dst[m * 128:(m + 1) * 128, cs], outb[:, :])
        px.barrier()


def stage_l0f_ffn(px, nc, g, li, src, dst):
    DI = 2730
    NC1 = len(kchunks(DI))
    with (
        nc.sbuf_tensor(f"ff{li}_w1", [128, 8 * 5460], BF16) as w1,
        nc.sbuf_tensor(f"ff{li}_w2", [128, NC1 * D], BF16) as w2,
        nc.sbuf_tensor(f"ff{li}_cols", [128, 16 + 2 * NC1 + 8], F32) as cols,
        nc.sbuf_tensor(f"ff{li}_xs", [128, 8 * TB], F32) as xsf,
        nc.sbuf_tensor(f"ff{li}_xsb", [128, 8 * TB], BF16) as xsb,
        nc.sbuf_tensor(f"ff{li}_xn", [128, 8 * TB], BF16) as xnb,
        nc.sbuf_tensor(f"ff{li}_h", [128, NC1 * TB], BF16) as hb,
        nc.sbuf_tensor(f"ff{li}_gg", [128, TB], BF16) as gg,
        nc.sbuf_tensor(f"ff{li}_row", [1, 4 * TB], F32) as row,
        nc.sbuf_tensor(f"ff{li}_o", [128, TB], F32) as outb,
    ):
        load_w(px, w1, g.f_W1[li], D, 5460)
        load_w(px, w2, g.f_W2[li], DI, D)
        load_vec_cols(px, cols, g.f_lng[li], D, 0)
        load_vec_cols(px, cols, g.f_lnb[li], D, 8)
        for i, (k0, kk) in enumerate(kchunks(DI)):
            px.dma("sync", cols[0:kk, 16 + i:17 + i], g.f_b1[li][k0:k0 + kk][:, None])
            px.dma("sync", cols[0:kk, 16 + NC1 + i:17 + NC1 + i],
                   g.f_b1[li][DI + k0:DI + k0 + kk][:, None])
        load_vec_cols(px, cols, g.f_b2[li], D, 16 + 2 * NC1)
        SUM, SQ, VAR, RSTD = 0, TB, 2 * TB, 3 * TB
        for tb in range(NTB):
            cs = slice(tb * TB, (tb + 1) * TB)
            for pg in range(8):
                sl = slice(pg * TB, (pg + 1) * TB)
                _shift_load(px, g, xsb[:, sl], src, pg, tb)
                cs0 = tb * TB
                r = slice(pg * 128, (pg + 1) * 128)
                if pg < 4:
                    px.dma("sync", xsf[:, sl], src[r, cs0:cs0 + TB])
                elif tb == 0:
                    px.memset("gpsimd", xsf[:, sl][:, 0:1], 0)
                    px.dma("sync", xsf[:, sl][:, 1:TB], src[r, 0:TB - 1])
                else:
                    px.dma("sync", xsf[:, sl], src[r, cs0 - 1:cs0 + TB - 1])
                px.act(xnb[:, sl], xsb[:, sl], AF.Square)
            ps_s = g.pool[6]
            ps_q = g.pool[7]
            for k in range(NKD):
                px.mm(ps_s[0:1, :], g.ones_col_bf[:, 0:1],
                      xsb[:, k * TB:(k + 1) * TB],
                      start=(k == 0), stop=(k == NKD - 1))
            for k in range(NKD):
                px.mm(ps_q[0:1, :], g.ones_col_bf[:, 0:1],
                      xnb[:, k * TB:(k + 1) * TB],
                      start=(k == 0), stop=(k == NKD - 1))
            px.act(row[:, SUM:SUM + TB], ps_s[0:1, :], AF.Copy, scale=1.0 / D)
            px.act(row[:, SQ:SQ + TB], ps_q[0:1, :], AF.Copy, scale=1.0 / D)
            px.tt("vector", row[:, VAR:VAR + TB], row[:, SUM:SUM + TB],
                  row[:, SUM:SUM + TB], ALU.mult)
            px.tt("vector", row[:, VAR:VAR + TB], row[:, SQ:SQ + TB],
                  row[:, VAR:VAR + TB], ALU.subtract)
            px.act(row[:, RSTD:RSTD + TB], row[:, VAR:VAR + TB], AF.Ln,
                   bias=1e-5)
            px.act(row[:, RSTD:RSTD + TB], row[:, RSTD:RSTD + TB], AF.Exp,
                   scale=-0.5)
            ps_m = g.pool[0]
            ps_r = g.pool[1]
            px.mm(ps_m[:, :], g.ones1_f32[:, :], row[:, SUM:SUM + TB])
            px.mm(ps_r[:, :], g.ones1_f32[:, :], row[:, RSTD:RSTD + TB])
            for k in range(NKD):
                sl = slice(k * TB, (k + 1) * TB)
                px.tt("vector", xsf[:, sl], xsf[:, sl], ps_m[:, :], ALU.subtract)
                px.tt("vector", xsf[:, sl], xsf[:, sl], ps_r[:, :], ALU.mult)
                px.ts("vector", xnb[:, sl], xsf[:, sl], cols[:, k:k + 1],
                      cols[:, 8 + k:9 + k], ALU.mult, ALU.add)
            for i, (k0, kk) in enumerate(kchunks(DI)):
                ba = g.pool[2 * (i % 2)]
                bgt = g.pool[2 * (i % 2) + 1]
                for k in range(NKD):
                    px.mm(ba[0:kk, :],
                          w1[:, k * 5460 + k0:k * 5460 + k0 + kk],
                          xnb[:, k * TB:(k + 1) * TB],
                          start=(k == 0), stop=(k == NKD - 1))
                for k in range(NKD):
                    px.mm(bgt[0:kk, :],
                          w1[:, k * 5460 + DI + k0:k * 5460 + DI + k0 + kk],
                          xnb[:, k * TB:(k + 1) * TB],
                          start=(k == 0), stop=(k == NKD - 1))
                px.act(gg[0:kk, :], bgt[0:kk, :], AF.Gelu,
                       bias=cols[0:kk, 16 + NC1 + i:17 + NC1 + i])
                px.stt("vector", hb[0:kk, i * TB:(i + 1) * TB], ba[0:kk, :],
                       cols[0:kk, 16 + i:17 + i], gg[0:kk, :],
                       ALU.add, ALU.mult)
            for m in range(NKD):
                bank = g.pool[4 + m % 2]
                for i, (k0, kk) in enumerate(kchunks(DI)):
                    px.mm(bank[:, :],
                          w2[0:kk, i * D + m * 128:i * D + (m + 1) * 128],
                          hb[0:kk, i * TB:(i + 1) * TB],
                          start=(i == 0), stop=(i == NC1 - 1))
                px.dma("sync", outb[:, :], src[m * 128:(m + 1) * 128, cs])
                px.stt("vector", outb[:, :], bank[:, :],
                       cols[:, 16 + 2 * NC1 + m:17 + 2 * NC1 + m],
                       outb[:, :], ALU.add, ALU.add)
                px.dma("sync", dst[m * 128:(m + 1) * 128, cs], outb[:, :])
        px.barrier()



def stage_combine_ln(px, nc, g, xsrc, ssrc, gsrc, x1_dst, xnb_dst, xnf_dst,
                     name):
    with (
        nc.sbuf_tensor(name + "_x", [128, NKD * TB], F32) as xt,
        nc.sbuf_tensor(name + "_s", [128, TB], F32) as st_,
        nc.sbuf_tensor(name + "_g", [128, TB], F32) as gt_,
        nc.sbuf_tensor(name + "_xb", [128, NKD * TB], BF16) as xbt,
        nc.sbuf_tensor(name + "_sq", [128, NKD * TB], BF16) as sqt,
        nc.sbuf_tensor(name + "_row", [1, 6 * TB], F32) as row,
        nc.sbuf_tensor(name + "_o", [128, TB], F32) as ot,
        nc.sbuf_tensor(name + "_ob", [128, TB], BF16) as otb,
    ):
        SUM, SQ, VAR, RSTD = 0, TB, 2 * TB, 3 * TB
        for tb in range(NTB):
            cs = slice(tb * TB, (tb + 1) * TB)
            for k in range(NKD):
                sl = slice(k * TB, (k + 1) * TB)
                r = slice(k * 128, (k + 1) * 128)
                px.dma("sync", xt[:, sl], xsrc[r, cs])
                px.dma("sync", st_[:, :], ssrc[r, cs])
                px.dma("sync", gt_[:, :], gsrc[r, cs])
                px.act(ot[:, :], gt_[:, :], AF.Silu)
                px.tt("vector", ot[:, :], ot[:, :], st_[:, :], ALU.mult)
                px.tt("vector", xt[:, sl], xt[:, sl], ot[:, :], ALU.add)
                px.dma("gpsimd", x1_dst[r, cs], xt[:, sl])
                px.copy("scalar", xbt[:, sl], xt[:, sl])
                px.act(sqt[:, sl], xbt[:, sl], AF.Square)
            ps_s = g.pool[0]
            ps_q = g.pool[1]
            for k in range(NKD):
                px.mm(ps_s[0:1, :], g.ones_col_bf[:, 0:1],
                      xbt[:, k * TB:(k + 1) * TB],
                      start=(k == 0), stop=(k == NKD - 1))
            for k in range(NKD):
                px.mm(ps_q[0:1, :], g.ones_col_bf[:, 0:1],
                      sqt[:, k * TB:(k + 1) * TB],
                      start=(k == 0), stop=(k == NKD - 1))
            px.act(row[:, SUM:SUM + TB], ps_s[0:1, :], AF.Copy, scale=1.0 / D)
            px.act(row[:, SQ:SQ + TB], ps_q[0:1, :], AF.Copy, scale=1.0 / D)
            px.tt("vector", row[:, VAR:VAR + TB], row[:, SUM:SUM + TB],
                  row[:, SUM:SUM + TB], ALU.mult)
            px.tt("vector", row[:, VAR:VAR + TB], row[:, SQ:SQ + TB],
                  row[:, VAR:VAR + TB], ALU.subtract)
            px.act(row[:, RSTD:RSTD + TB], row[:, VAR:VAR + TB], AF.Ln,
                   bias=1e-5)
            px.act(row[:, RSTD:RSTD + TB], row[:, RSTD:RSTD + TB], AF.Exp,
                   scale=-0.5)
            ps_m = g.pool[2]
            ps_r = g.pool[3]
            px.mm(ps_m[:, :], g.ones1_f32[:, :], row[:, SUM:SUM + TB])
            px.mm(ps_r[:, :], g.ones1_f32[:, :], row[:, RSTD:RSTD + TB])
            for k in range(NKD):
                sl = slice(k * TB, (k + 1) * TB)
                r = slice(k * 128, (k + 1) * 128)
                px.tt("vector", ot[:, :], xt[:, sl], ps_m[:, :], ALU.subtract)
                px.tt("vector", ot[:, :], ot[:, :], ps_r[:, :], ALU.mult)
                px.copy("scalar", otb[:, :], ot[:, :])
                if xnf_dst is not None:
                    px.dma("sync", xnf_dst[r, cs], ot[:, :])
                px.dma("gpsimd", xnb_dst[r, cs], otb[:, :])
        px.barrier()



def stage_m_proj(px, nc, g):
    with (
        nc.sbuf_tensor("mp_w", [128, 8 * 4096], BF16) as w,
        nc.sbuf_tensor("mp_x", [128, 8 * TB], BF16) as xt,
        nc.sbuf_tensor("mp_o", [128, 2 * TB], F32) as ot,
    ):
        load_w(px, w, g.m_inproj, D, 4096)
        for tb in range(NTB):
            cs = slice(tb * TB, (tb + 1) * TB)
            for k in range(NKD):
                px.dma("sync", xt[:, k * TB:(k + 1) * TB],
                       g.xn1_bf[k * 128:(k + 1) * 128, cs])
            for m in range(32):
                bank = g.pool[m % 4]
                for k in range(NKD):
                    px.mm(bank[:, :],
                          w[:, k * 4096 + m * 128:k * 4096 + (m + 1) * 128],
                          xt[:, k * TB:(k + 1) * TB],
                          start=(k == 0), stop=(k == NKD - 1))
                o = ot[:, (m % 2) * TB:(m % 2 + 1) * TB]
                px.act(o, bank[:, :], AF.Copy)
                dst = g.xc_f32 if m < 16 else g.z_f32
                mm = m % 16
                px.dma("sync", dst[mm * 128:(mm + 1) * 128, cs], o)
        px.barrier()


def stage_m_conv(px, nc, g):
    with (
        nc.sbuf_tensor("mc_w", [128, 16 * 5], F32) as wc,
        nc.sbuf_tensor("mc_x", [128, TB + 6], F32) as xt,
        nc.sbuf_tensor("mc_a", [128, TB], F32) as acc,
        nc.sbuf_tensor("mc_ob", [128, TB], BF16) as ob,
    ):
        for pg in range(16):
            r = slice(pg * 128, (pg + 1) * 128)
            px.dma("sync", wc[:, pg * 5:pg * 5 + 4], g.m_convw[r, :])
            px.dma("sync", wc[:, pg * 5 + 4:pg * 5 + 5], g.m_convb[r][:, None])
        for d in range(2):
            for pg in range(16):
                r = slice(pg * 128, (pg + 1) * 128)
                for tb in range(NTB):
                    cs0 = tb * TB
                    lo = cs0 - 3 if d == 0 else cs0
                    hi = cs0 + TB if d == 0 else cs0 + TB + 3
                    lo_c = max(0, lo)
                    hi_c = min(T, hi)
                    if lo < 0:
                        px.memset("vector", xt[:, 0:lo_c - lo], 0)
                    if hi > T:
                        px.memset("vector", xt[:, TB + 3 - (hi - hi_c):TB + 3], 0)
                    px.dma("sync", xt[:, lo_c - lo:lo_c - lo + hi_c - lo_c],
                           g.xc_f32[r, lo_c:hi_c])
                    for i in range(4):
                        off = i if d == 0 else 3 - i
                        xs_ = xt[:, off:off + TB]
                        if i == 0:
                            px.ts("vector", acc[:, :], xs_,
                                  wc[:, pg * 5:pg * 5 + 1], None, ALU.mult)
                        else:
                            px.stt("vector", acc[:, :], xs_,
                                   wc[:, pg * 5 + i:pg * 5 + i + 1], acc[:, :],
                                   ALU.mult, ALU.add)
                    px.act(acc[:, :], acc[:, :], AF.Silu,
                           bias=wc[:, pg * 5 + 4:pg * 5 + 5])
                    px.copy("vector", ob[:, :], acc[:, :])
                    px.dma("sync", g.xcs_f32[d][r, cs0:cs0 + TB], acc[:, :])
                    px.dma("gpsimd", g.xcs_bf[d][r, cs0:cs0 + TB], ob[:, :])
        px.barrier()


def stage_m_scan(px, nc, g):
    DT_RANK, DS = 64, 16
    specs = [
        ("ms_xw", [128, 16 * 96], BF16), ("ms_dtw", [64, 2048], BF16),
        ("ms_cols", [128, 16 * 2], F32), ("ms_A", [128, 256], F32),
        ("ms_selBC", [96, 32 * 128], F32), ("ms_xcb", [128, 16 * TB], BF16),
        ("ms_xcf", [128, 16 * TB], F32), ("ms_dblf", [96, TB], F32),
        ("ms_dtb", [64, TB], BF16), ("ms_delta", [128, 16 * TB], F32),
        ("ms_du", [128, 16 * TB], F32), ("ms_Bs", [128, TB], F32),
        ("ms_Cs", [128, TB], F32), ("ms_a", [128, TB], F32),
        ("ms_b", [128, TB], F32), ("ms_h", [128, TB], F32),
        ("ms_y", [128, 16 * TB], F32), ("ms_carry", [128, 256], F32),
        ("ms_z", [128, TB], F32), ("ms_t", [128, TB], F32),
        ("ms_yc", [128, TB], F32),
    ]
    with ExitStack() as _st:
        (xw, dtw, cols, Asb, selBC, xcb, xcf, dblf, dtb, dlt, du, Bs, Cs,
         ab, bb_, hbuf, yb, carry, zb, tb_, ycb) = [
            _st.enter_context(nc.sbuf_tensor(n, sh, dt)) for (n, sh, dt) in specs]
        load_w(px, xw, g.m_xprojw, 2048, 96)
        px.dma("gpsimd", dtw[:, :], g.m_dtw[:, :])
        for pg in range(16):
            r = slice(pg * 128, (pg + 1) * 128)
            px.dma("sync", cols[:, pg:pg + 1], g.m_dtb[r][:, None])
            px.dma("sync", cols[:, 16 + pg:17 + pg], g.m_D[r][:, None])
            px.dma("sync", Asb[:, pg * 16:(pg + 1) * 16], g.m_Alog[r, :])
        px.act(Asb[:, :], Asb[:, :], AF.Exp)
        px.ts("vector", Asb[:, :], Asb[:, :], -1.0, None, ALU.mult)
        px.dma("sync", selBC[:, :], g.c_selBC[:, :])
        for d in range(getattr(g, "ms_dirs", 2)):
            px.memset("vector", carry[:, :], 0)
            crange = range(NTB) if d == 0 else range(NTB - 1, -1, -1)
            crange = list(crange)[:getattr(g, "ms_chunks", NTB)]
            for c in crange:
                cs = slice(c * TB, (c + 1) * TB)
                for pg in range(16):
                    px.dma("sync", xcb[:, pg * TB:(pg + 1) * TB],
                           g.xcs_bf[d][pg * 128:(pg + 1) * 128, cs])
                    px.dma("sync", xcf[:, pg * TB:(pg + 1) * TB],
                           g.xcs_f32[d][pg * 128:(pg + 1) * 128, cs])
                bank = g.pool[0]
                for k in range(16):
                    px.mm(bank[0:96, :], xw[:, k * 96:(k + 1) * 96],
                          xcb[:, k * TB:(k + 1) * TB],
                          start=(k == 0), stop=(k == 15))
                px.act(dblf[:, :], bank[0:96, :], AF.Copy)
                px.copy("scalar", dtb[:, :], bank[0:64, :])
                if getattr(g, "ms_upto", 5) <= 1:
                    continue
                for m in range(16):
                    bank2 = g.pool[1 + m % 2]
                    px.mm(bank2[:, :], dtw[:, m * 128:(m + 1) * 128],
                          dtb[:, :], start=True, stop=True)
                    px.ts("vector", yb[:, m * TB:(m + 1) * TB], bank2[:, :],
                          cols[:, m:m + 1], None, ALU.add)
                ZW = 16 * TB
                px.act(dlt[:, :], yb[:, :ZW], AF.Abs)
                px.act(dlt[:, :], dlt[:, :], AF.Exp, scale=-1.0)
                px.act(dlt[:, :], dlt[:, :], AF.Ln, bias=1.0)
                px.ts("vector", du[:, :], yb[:, :ZW], 0.0, None, ALU.max)
                px.tt("vector", dlt[:, :], dlt[:, :], du[:, :], ALU.add)
                px.tt("gpsimd", du[:, :], dlt[:, :], xcf[:, :], ALU.mult)
                for n in range(DS):
                    bB = g.pool[4]
                    bC = g.pool[5]
                    px.mm(bB[:, :], selBC[:, (2 * n) * 128:(2 * n + 1) * 128],
                          dblf[:, :], start=True, stop=True)
                    px.mm(bC[:, :], selBC[:, (2 * n + 1) * 128:(2 * n + 2) * 128],
                          dblf[:, :], start=True, stop=True)
                    px.act(Bs[:, :], bB[:, :], AF.Copy)
                    px.act(Cs[:, :], bC[:, :], AF.Copy)
                    for gd in range(16):
                        dsl = dlt[:, gd * TB:(gd + 1) * TB]
                        px.act(ab[:, :], dsl, AF.Exp,
                               scale=Asb[:, gd * 16 + n:gd * 16 + n + 1])
                        px.tt("gpsimd", bb_[:, :], du[:, gd * TB:(gd + 1) * TB],
                              Bs[:, :], ALU.mult)
                        cc = carry[:, gd * 16 + n:gd * 16 + n + 1]
                        if getattr(g, "ms_upto", 5) <= 3:
                            continue
                        if d == 0:
                            px.scan(hbuf[:, :], ab[:, :], bb_[:, :], cc)
                            px.copy("vector", cc, hbuf[:, TB - 1:TB])
                        else:
                            ra = BAP(ab, TB - 1, [[TB, 128], [-1, TB]])
                            rb2 = BAP(bb_, TB - 1, [[TB, 128], [-1, TB]])
                            rh = BAP(hbuf, TB - 1, [[TB, 128], [-1, TB]])
                            px.scan(rh, ra, rb2, cc)
                            px.copy("vector", cc, hbuf[:, 0:1])
                        ysl = yb[:, gd * TB:(gd + 1) * TB]
                        if getattr(g, "ms_upto", 5) <= 4:
                            continue
                        if n == 0:
                            px.tt("vector", ysl, hbuf[:, :], Cs[:, :], ALU.mult)
                        else:
                            px.tt("vector", tb_[:, :], hbuf[:, :], Cs[:, :],
                                  ALU.mult)
                            px.tt("gpsimd", ysl, ysl, tb_[:, :], ALU.add)
                for gd in range(16 if getattr(g, "ms_upto", 5) >= 5 else 0):
                    r = slice(gd * 128, (gd + 1) * 128)
                    px.stt("vector", tb_[:, :], xcf[:, gd * TB:(gd + 1) * TB],
                           cols[:, 16 + gd:17 + gd], yb[:, gd * TB:(gd + 1) * TB],
                           ALU.mult, ALU.add)
                    px.dma("sync", zb[:, :], g.z_f32[r, cs])
                    px.act(zb[:, :], zb[:, :], AF.Silu)
                    px.tt("vector", ycb[:, :], tb_[:, :], zb[:, :], ALU.mult)
                    if d == 0:
                        px.dma("gpsimd", g.ycomb_f32[r, cs], ycb[:, :])
                    else:
                        px.dma("gpsimd", g.ycomb_f32[r, cs], ycb[:, :],
                               accum_op=ALU.add)
        px.barrier()


def stage_m_out(px, nc, g):
    with (
        nc.sbuf_tensor("mo_w", [128, 16 * D], BF16) as w,
        nc.sbuf_tensor("mo_y", [128, 16 * TB], BF16) as yt,
        nc.sbuf_tensor("mo_m", [128, 8 * TB], F32) as mt,
        nc.sbuf_tensor("mo_sq", [128, 8 * TB], BF16) as sqt,
        nc.sbuf_tensor("mo_row", [1, 4 * TB], F32) as row,
        nc.sbuf_tensor("mo_cols", [128, 16], F32) as cols,
        nc.sbuf_tensor("mo_ob", [128, TB], BF16) as ob,
    ):
        load_w(px, w, g.m_outproj, 2048, D)
        load_vec_cols(px, cols, g.m_lng, D, 0)
        load_vec_cols(px, cols, g.m_lnb, D, 8)
        SUM, SQ, VAR, RSTD = 0, TB, 2 * TB, 3 * TB
        for tb in range(NTB):
            cs = slice(tb * TB, (tb + 1) * TB)
            for k in range(16):
                px.dma("gpsimd", yt[:, k * TB:(k + 1) * TB],
                       g.ycomb_f32[k * 128:(k + 1) * 128, cs])
            for m in range(NKD):
                bank = g.pool[m % 4]
                for k in range(16):
                    px.mm(bank[:, :], w[:, k * D + m * 128:k * D + (m + 1) * 128],
                          yt[:, k * TB:(k + 1) * TB],
                          start=(k == 0), stop=(k == 15))
                sl = slice(m * TB, (m + 1) * TB)
                px.act(mt[:, sl], bank[:, :], AF.Copy)
                px.act(sqt[:, sl], bank[:, :], AF.Square)
            ps_s = g.pool[4]
            ps_q = g.pool[5]
            for k in range(NKD):
                px.mm(ps_s[0:1, :], g.ones_col_bf[:, 0:1],
                      sqt[:, k * TB:(k + 1) * TB],
                      start=(k == 0), stop=(k == NKD - 1))
            for k in range(NKD):
                px.copy("scalar", sqt[:, k * TB:(k + 1) * TB],
                        mt[:, k * TB:(k + 1) * TB])
            for k in range(NKD):
                px.mm(ps_q[0:1, :], g.ones_col_bf[:, 0:1],
                      sqt[:, k * TB:(k + 1) * TB],
                      start=(k == 0), stop=(k == NKD - 1))
            px.act(row[:, SUM:SUM + TB], ps_q[0:1, :], AF.Copy, scale=1.0 / D)
            px.act(row[:, SQ:SQ + TB], ps_s[0:1, :], AF.Copy, scale=1.0 / D)
            px.tt("vector", row[:, VAR:VAR + TB], row[:, SUM:SUM + TB],
                  row[:, SUM:SUM + TB], ALU.mult)
            px.tt("vector", row[:, VAR:VAR + TB], row[:, SQ:SQ + TB],
                  row[:, VAR:VAR + TB], ALU.subtract)
            px.act(row[:, RSTD:RSTD + TB], row[:, VAR:VAR + TB], AF.Ln,
                   bias=1e-5)
            px.act(row[:, RSTD:RSTD + TB], row[:, RSTD:RSTD + TB], AF.Exp,
                   scale=-0.5)
            ps_m = g.pool[6]
            ps_r = g.pool[7]
            px.mm(ps_m[:, :], g.ones1_f32[:, :], row[:, SUM:SUM + TB])
            px.mm(ps_r[:, :], g.ones1_f32[:, :], row[:, RSTD:RSTD + TB])
            for k in range(NKD):
                sl = slice(k * TB, (k + 1) * TB)
                px.tt("vector", mt[:, sl], mt[:, sl], ps_m[:, :], ALU.subtract)
                px.tt("vector", mt[:, sl], mt[:, sl], ps_r[:, :], ALU.mult)
                px.ts("vector", ob[:, :], mt[:, sl], cols[:, k:k + 1],
                      cols[:, 8 + k:9 + k], ALU.mult, ALU.add)
                px.dma("gpsimd", g.mh_bf[k * 128:(k + 1) * 128, cs], ob[:, :])
        px.barrier()


def stage_m_ffn(px, nc, g):
    with (
        nc.sbuf_tensor("mf_w1", [128, 8 * 4096], BF16) as w1,
        nc.sbuf_tensor("mf_w2", [128, 32 * D], BF16) as w2,
        nc.sbuf_tensor("mf_cols", [128, 32 + 8], F32) as cols,
        nc.sbuf_tensor("mf_x", [128, 8 * TB], BF16) as xt,
        nc.sbuf_tensor("mf_h", [128, 32 * TB], BF16) as hb,
        nc.sbuf_tensor("mf_o", [128, TB], F32) as ob,
    ):
        load_w(px, w1, g.m_ffw1, D, 4096)
        load_w(px, w2, g.m_ffw2, 4096, D)
        load_vec_cols(px, cols, g.m_ffb1, 4096, 0)
        load_vec_cols(px, cols, g.m_ffb2, D, 32)
        for tb in range(NTB):
            cs = slice(tb * TB, (tb + 1) * TB)
            for k in range(NKD):
                px.dma("sync", xt[:, k * TB:(k + 1) * TB],
                       g.mh_bf[k * 128:(k + 1) * 128, cs])
            for m in range(32):
                bank = g.pool[m % 4]
                for k in range(NKD):
                    px.mm(bank[:, :],
                          w1[:, k * 4096 + m * 128:k * 4096 + (m + 1) * 128],
                          xt[:, k * TB:(k + 1) * TB],
                          start=(k == 0), stop=(k == NKD - 1))
                px.act(hb[:, m * TB:(m + 1) * TB], bank[:, :], AF.Gelu,
                       bias=cols[:, m:m + 1])
            for m in range(NKD):
                bank = g.pool[4 + m % 2]
                for k in range(32):
                    px.mm(bank[:, :], w2[:, k * D + m * 128:k * D + (m + 1) * 128],
                          hb[:, k * TB:(k + 1) * TB],
                          start=(k == 0), stop=(k == 31))
                px.ts("vector", ob[:, :], bank[:, :], cols[:, 32 + m:33 + m],
                      None, ALU.add)
                px.dma("sync", g.s1_f32[m * 128:(m + 1) * 128, cs], ob[:, :])
        px.barrier()



def stage_moe_route(px, nc, g):
    with (
        nc.sbuf_tensor("mr_gw", [128, 8 * 8], BF16) as gw,
        nc.sbuf_tensor("mr_se", [128, 8], BF16) as se,
        nc.sbuf_tensor("mr_x", [128, 8 * TB], BF16) as xt,
        nc.sbuf_tensor("mr_lg", [8, TB], BF16) as lg,
        nc.sbuf_tensor("mr_tm", [128, 8], F32) as tm,
        nc.sbuf_tensor("mr_m1", [128, 4], F32) as m1,
        nc.sbuf_tensor("mr_sc", [128, 8], F32) as sc,
        nc.sbuf_tensor("mr_wt", [128, 8], BF16) as wtb,
        nc.sbuf_tensor("mr_wfm", [8, TB], F32) as wfm,
        nc.sbuf_tensor("mr_sw", [1, TB], F32) as swr,
    ):
        load_w(px, gw, g.e_gatew, D, 8)
        load_w(px, se, g.e_segatew, D, 1)
        for tb in range(NTB):
            cs = slice(tb * TB, (tb + 1) * TB)
            for k in range(NKD):
                px.dma("sync", xt[:, k * TB:(k + 1) * TB],
                       g.xn1_bf[k * 128:(k + 1) * 128, cs])
            blog = g.pool[0]
            for k in range(NKD):
                px.mm(blog[0:8, :], gw[:, k * 8:(k + 1) * 8],
                      xt[:, k * TB:(k + 1) * TB],
                      start=(k == 0), stop=(k == NKD - 1))
            bse = g.pool[1]
            for k in range(NKD):
                px.mm(bse[0:1, :], se[:, k:k + 1], xt[:, k * TB:(k + 1) * TB],
                      start=(k == 0), stop=(k == NKD - 1))
            px.act(swr[:, :], bse[0:1, :], AF.Sigmoid)
            px.dma("sync", g.sw_f32[0:1, cs], swr[:, :])
            px.copy("vector", lg[:, :], blog[0:8, :])
            for ts_ in range(4):
                pT = g.pool[2][:, :].bitcast(BF16)
                px.transpose(pT[:, 0:8], lg[:, ts_ * 128:(ts_ + 1) * 128],
                             g.ident_bf[0:8, 0:8])
                px.copy("vector", tm[:, 0:8], pT[:, 0:8])
                px.red("vector", m1[:, 0:1], tm[:, 0:8], ALU.max)
                px.ts("vector", tm[:, 0:8], tm[:, 0:8], m1[:, 0:1], None,
                      ALU.subtract)
                px.act(tm[:, 0:8], tm[:, 0:8], AF.Exp, accum_out=m1[:, 1:2])
                px.recip(m1[:, 1:2], m1[:, 1:2])
                px.ts("vector", sc[:, :], tm[:, 0:8], m1[:, 1:2], None,
                      ALU.mult)
                px.red("vector", m1[:, 2:3], sc[:, :], ALU.max)
                px.ts("vector", tm[:, 0:8], sc[:, :], m1[:, 2:3], None,
                      ALU.is_ge)
                px.stt("vector", tm[:, 0:8], tm[:, 0:8], -1e30, sc[:, :],
                       ALU.mult, ALU.add)
                px.red("vector", m1[:, 3:4], tm[:, 0:8], ALU.max)
                px.ts("vector", tm[:, 0:8], sc[:, :], m1[:, 3:4], None,
                      ALU.is_ge)
                px.tt("vector", sc[:, :], sc[:, :], tm[:, 0:8], ALU.mult)
                px.copy("vector", wtb[:, :], sc[:, :])
                pT2 = g.pool[3][:, :].bitcast(BF16)
                px.transpose(pT2[0:8, 0:128], wtb[:, :], g.ident_bf[:, :])
                px.copy("vector", wfm[:, ts_ * 128:(ts_ + 1) * 128],
                        pT2[0:8, 0:128])
            px.dma("sync", g.w_fm_f32[:, cs], wfm[:, :])
        px.barrier()


def stage_moe_expert(px, nc, g, e):
    I = 1408
    NCI = I // 128
    with (
        nc.sbuf_tensor(f"me{e}_wgu", [128, 8 * 2816], BF16) as wgu,
        nc.sbuf_tensor(f"me{e}_wd", [128, NCI * D], BF16) as wd,
        nc.sbuf_tensor(f"me{e}_x", [128, 8 * TB], BF16) as xt,
        nc.sbuf_tensor(f"me{e}_w", [8, TB], F32) as wfm,
        nc.sbuf_tensor(f"me{e}_h", [128, NCI * TB], BF16) as hb,
        nc.sbuf_tensor(f"me{e}_gs", [128, TB], BF16) as gs,
        nc.sbuf_tensor(f"me{e}_t", [128, TB], F32) as tmp,
        nc.sbuf_tensor(f"me{e}_o", [128, TB], F32) as ob,
    ):
        load_w(px, wgu, g.e_gup[e], D, 2816)
        load_w(px, wd, g.e_down[e], I, D)
        for tb in range(NTB):
            cs = slice(tb * TB, (tb + 1) * TB)
            for k in range(NKD):
                px.dma("sync", xt[:, k * TB:(k + 1) * TB],
                       g.xn1_bf[k * 128:(k + 1) * 128, cs])
            px.dma("sync", wfm[:, :], g.w_fm_f32[:, cs])
            web = g.pool[6]
            px.mm(web[:, :], g.c_sel8[:, e * 128:(e + 1) * 128], wfm[:, :],
                  start=True, stop=True)
            for i in range(NCI):
                bg = g.pool[2 * (i % 2)]
                bu = g.pool[2 * (i % 2) + 1]
                for k in range(NKD):
                    px.mm(bg[:, :],
                          wgu[:, k * 2816 + i * 128:k * 2816 + (i + 1) * 128],
                          xt[:, k * TB:(k + 1) * TB],
                          start=(k == 0), stop=(k == NKD - 1))
                for k in range(NKD):
                    px.mm(bu[:, :],
                          wgu[:, k * 2816 + I + i * 128:k * 2816 + I + (i + 1) * 128],
                          xt[:, k * TB:(k + 1) * TB],
                          start=(k == 0), stop=(k == NKD - 1))
                px.act(gs[:, :], bg[:, :], AF.Silu)
                px.tt("vector", tmp[:, :], gs[:, :], bu[:, :], ALU.mult)
                px.tt("vector", hb[:, i * TB:(i + 1) * TB], tmp[:, :],
                      web[:, :], ALU.mult)
            for m in range(NKD):
                bank = g.pool[4 + m % 2]
                for i in range(NCI):
                    px.mm(bank[:, :], wd[:, i * D + m * 128:i * D + (m + 1) * 128],
                          hb[:, i * TB:(i + 1) * TB],
                          start=(i == 0), stop=(i == NCI - 1))
                px.act(ob[:, :], bank[:, :], AF.Copy)
                if e == 0:
                    px.dma("gpsimd", g.moe_acc[m * 128:(m + 1) * 128, cs],
                           ob[:, :])
                else:
                    px.dma("gpsimd", g.moe_acc[m * 128:(m + 1) * 128, cs],
                           ob[:, :], accum_op=ALU.add)
        px.barrier()


def stage_moe_shared(px, nc, g):
    SI = 2816
    NCS = SI // 128
    with (
        nc.sbuf_tensor("msh_wg", [128, 8 * SI], BF16) as wgt,
        nc.sbuf_tensor("msh_wu", [128, 8 * SI], BF16) as wut,
        nc.sbuf_tensor("msh_wd", [128, NCS * D], BF16) as wdt,
        nc.sbuf_tensor("msh_x", [128, 8 * TB], BF16) as xt,
        nc.sbuf_tensor("msh_h", [128, NCS * TB], BF16) as hb,
        nc.sbuf_tensor("msh_gs", [128, TB], BF16) as gs,
        nc.sbuf_tensor("msh_sw", [1, TB], F32) as swr,
        nc.sbuf_tensor("msh_t", [128, TB], F32) as tmp,
        nc.sbuf_tensor("msh_x1", [128, TB], F32) as x1t,
        nc.sbuf_tensor("msh_s1", [128, TB], F32) as s1t,
        nc.sbuf_tensor("msh_o", [128, TB], F32) as ob,
    ):
        load_w(px, wgt, g.e_shgate, D, SI)
        load_w(px, wut, g.e_shup, D, SI)
        load_w(px, wdt, g.e_shdown, SI, D)
        for tb in range(NTB):
            cs = slice(tb * TB, (tb + 1) * TB)
            for k in range(NKD):
                px.dma("sync", xt[:, k * TB:(k + 1) * TB],
                       g.xn1_bf[k * 128:(k + 1) * 128, cs])
            px.dma("sync", swr[:, :], g.sw_f32[0:1, cs])
            bsw = g.pool[6]
            px.mm(bsw[:, :], g.ones1_f32[:, :], swr[:, :], start=True,
                  stop=True)
            for i in range(NCS):
                bg = g.pool[2 * (i % 2)]
                bu = g.pool[2 * (i % 2) + 1]
                for k in range(NKD):
                    px.mm(bg[:, :],
                          wgt[:, k * SI + i * 128:k * SI + (i + 1) * 128],
                          xt[:, k * TB:(k + 1) * TB],
                          start=(k == 0), stop=(k == NKD - 1))
                for k in range(NKD):
                    px.mm(bu[:, :],
                          wut[:, k * SI + i * 128:k * SI + (i + 1) * 128],
                          xt[:, k * TB:(k + 1) * TB],
                          start=(k == 0), stop=(k == NKD - 1))
                px.act(gs[:, :], bg[:, :], AF.Silu)
                px.tt("vector", tmp[:, :], gs[:, :], bu[:, :], ALU.mult)
                px.tt("vector", hb[:, i * TB:(i + 1) * TB], tmp[:, :],
                      bsw[:, :], ALU.mult)
            for m in range(NKD):
                bank = g.pool[4 + m % 2]
                for i in range(NCS):
                    px.mm(bank[:, :],
                          wdt[:, i * D + m * 128:i * D + (m + 1) * 128],
                          hb[:, i * TB:(i + 1) * TB],
                          start=(i == 0), stop=(i == NCS - 1))
                r = slice(m * 128, (m + 1) * 128)
                px.dma("sync", tmp[:, :], g.moe_acc[r, cs])
                px.tt("vector", tmp[:, :], tmp[:, :], bank[:, :], ALU.add)
                px.act(tmp[:, :], tmp[:, :], AF.Silu)
                px.dma("sync", s1t[:, :], g.s1_f32[r, cs])
                px.dma("sync", x1t[:, :], g.x1_f32[r, cs])
                px.tt("vector", tmp[:, :], tmp[:, :], s1t[:, :], ALU.mult)
                px.tt("vector", ob[:, :], x1t[:, :], tmp[:, :], ALU.add)
                px.dma("sync", g.out_fm[r, cs], ob[:, :])
        px.barrier()




def build_program(meta):
    nc = bass.Bass("TRN2", target_bir_lowering=False)
    g = Ctx()
    g.pool, g._pool_guards = make_psum_pool(nc)

    def ein(name, shape, dt=F32):
        return nc.dram_tensor(name, list(shape), dt, kind="ExternalInput").ap()

    def scratch(name, shape, dt=F32):
        return nc.dram_tensor(name, list(shape), dt, kind="Internal").ap()

    def eout(name, shape, dt=F32):
        return nc.dram_tensor(name, list(shape), dt, kind="ExternalOutput").ap()

    g.x_fm = ein("x_fm", [D, T])
    g.c_ident = ein("c_ident", [128, 128])
    g.c_indP = ein("c_indP", [8, 128, 16])
    g.c_indT = ein("c_indT", [8, 16, 128])
    g.c_ones1 = ein("c_ones1", [1, 128])
    g.c_ones_col = ein("c_ones_col", [128, 1])

    g._const_guards = []

    def sres(name, shape, dt):
        gcm = nc.sbuf_tensor(name, list(shape), dt)
        t = gcm.__enter__()
        g._const_guards.append(gcm)
        return t

    g.ident_bf = sres("ident_bf", [128, 128], BF16)
    g.indP_bf = sres("indP_bf", [128, 8 * 16], BF16)
    g.indT_bf = sres("indT_bf", [16, 8 * 128], BF16)
    g.indT_f32 = sres("indT_f32", [16, 8 * 128], F32)
    g.ones1_f32 = sres("ones1_f32", [1, 128], F32)
    g.ones1_bf = sres("ones1_bf", [1, 128], BF16)
    g.ones_col_bf = sres("ones_col_bf", [128, 1], BF16)

    px = Px(nc)

    ACT_CONSTS = [1e-5, 1e-6, 1.0 / D, -0.5, 0.5, -1.0, 2.0, 8.0,
                  1.0 / 64, 1e-12, -1e30]
    g.constcol = sres("constcol", [128, len(ACT_CONSTS)], F32)
    for i, v in enumerate(ACT_CONSTS):
        px.memset("vector", g.constcol[:, i:i + 1], v)
        nc.const_aps.register(F32, v, g.constcol[:, i:i + 1])
    px.dma("gpsimd", g.ident_bf[:, :], g.c_ident[:, :])
    for pg in range(8):
        px.dma("gpsimd", g.indP_bf[:, pg * 16:(pg + 1) * 16], g.c_indP[pg])
        px.dma("gpsimd", g.indT_bf[:, pg * 128:(pg + 1) * 128], g.c_indT[pg])
        px.dma("sync", g.indT_f32[:, pg * 128:(pg + 1) * 128], g.c_indT[pg])
    px.dma("sync", g.ones1_f32[:, :], g.c_ones1[:, :])
    px.dma("gpsimd", g.ones1_bf[:, :], g.c_ones1[:, :])
    px.dma("gpsimd", g.ones_col_bf[:, :], g.c_ones_col[:, :])
    px.barrier()

    stages = meta.get("stages", ["ln0"])
    g.offsets = meta.get("offsets", [])
    g.n_vm = meta.get("n_vm", 0)

    g.xn_bf = scratch("xn_bf", [D, T], BF16)
    g.xn_f32 = scratch("xn_f32", [D, T])

    if "l0a_proj" in stages or "l0a_attn" in stages or "l0a_out" in stages:
        g.q_s = scratch("q_s", [D, T], BF16)
        g.k_s = scratch("k_s", [D, T], BF16)
        g.v_s = scratch("v_s", [D, T], BF16)
        g.gate_s = scratch("gate_s", [D, T], BF16)
        g.attng_s = scratch("attng_s", [D, T], BF16)
        g.s0_f32 = scratch("s0_f32", [D, T])
        g.xb_f32 = scratch("xb_f32", [D, T])
        g.xn2_bf = scratch("xn2_bf", [D, T], BF16)
    if "l0a_proj" in stages:
        g.w_qkv = ein("L0_attn_w_qkv", [D, 3 * D])
        g.w_gate = ein("L0_attn_w_gate", [D, D])
        g.norm1_scale = ein("L0_attn_norm1_scale", [D])
        g.b_gate = ein("L0_attn_b_gate", [D])
    if "l0a_attn" in stages:
        g.pb_row = ein("pb_row", [1, len(g.offsets) * 16])
        if g.n_vm:
            g.c_vmask = ein("c_vmask", [g.n_vm, 128, len(g.offsets)])
    if "l0a_out" in stages:
        g.w_out = ein("L0_attn_w_out", [D, D])
        g.norm2_scale = ein("L0_attn_norm2_scale", [D])
    if "l0a_ffn" in stages:
        g.w_ffn_gate = ein("L0_attn_w_ffn_gate", [D, 2816])
        g.w_ffn_up = ein("L0_attn_w_ffn_up", [D, 2816])
        g.w_ffn_down = ein("L0_attn_w_ffn_down", [2816, D])

    need_l0f = any(st.startswith("l0f") for st in stages)
    if need_l0f:
        g.f_Wqkv = ein("L0_ffn_Wqkv", [6, D, 3072])
        g.f_Wo = ein("L0_ffn_Wo", [6, D, D])
        g.f_qscale = ein("L0_ffn_q_scale", [6, 64])
        g.f_kscale = ein("L0_ffn_k_scale", [6, 64])
        g.f_lng = ein("L0_ffn_ln_g", [6, D])
        g.f_lnb = ein("L0_ffn_ln_b", [6, D])
        g.f_W1 = ein("L0_ffn_W1", [6, D, 5460])
        g.f_b1 = ein("L0_ffn_b1", [6, 5460])
        g.f_W2 = ein("L0_ffn_W2", [6, 2730, D])
        g.f_b2 = ein("L0_ffn_b2", [6, D])
        g.c_rot = ein("c_rot", [4, 128, T])
        g.c_R = ein("c_R", [128, 128])
        g.c_l0f_mask = ein("c_l0f_mask", [2, 2, 128, 512])
        g.c_R_bf = sres("c_R_bf", [128, 128], BF16)
        px.dma("gpsimd", g.c_R_bf[:, :], g.c_R[:, :])
        g.xl0 = scratch("xl0_f32", [D, T])
        g.xl1 = scratch("xl1_f32", [D, T])

    need_l1 = any(st.startswith(("comb", "m_", "moe")) for st in stages)
    if need_l1:
        g.x1_f32 = scratch("x1_f32", [D, T])
        g.xn1_bf = scratch("xn1_bf", [D, T], BF16)
        g.xn1_f32 = scratch("xn1_f32", [D, T])
        g.m_inproj = ein("L1_attn_in_proj_w", [D, 4096])
        g.m_convw = ein("L1_attn_conv_w", [2048, 4])
        g.m_convb = ein("L1_attn_conv_b", [2048])
        g.m_xprojw = ein("L1_attn_x_proj_w", [2048, 96])
        g.m_dtw = ein("L1_attn_dt_proj_w", [64, 2048])
        g.m_dtb = ein("L1_attn_dt_proj_b", [2048])
        g.m_Alog = ein("L1_attn_A_log", [2048, 16])
        g.m_D = ein("L1_attn_D", [2048])
        g.m_outproj = ein("L1_attn_out_proj_w", [2048, D])
        g.m_lng = ein("L1_attn_ln_g", [D])
        g.m_lnb = ein("L1_attn_ln_b", [D])
        g.m_ffw1 = ein("L1_attn_ff_w1", [D, 4096])
        g.m_ffb1 = ein("L1_attn_ff_b1", [4096])
        g.m_ffw2 = ein("L1_attn_ff_w2", [4096, D])
        g.m_ffb2 = ein("L1_attn_ff_b2", [D])
        g.c_selBC = ein("c_selBC", [96, 32 * 128])
        g.xc_f32 = scratch("xc_f32", [2048, T])
        g.z_f32 = scratch("z_f32", [2048, T])
        g.xcs_f32 = scratch("xcs_f32", [2, 2048, T])
        g.xcs_bf = scratch("xcs_bf", [2, 2048, T], BF16)
        g.ycomb_f32 = scratch("ycomb_f32", [2048, T])
        g.mh_bf = scratch("mh_bf", [D, T], BF16)
        g.s1_f32 = scratch("s1_f32", [D, T])
        g.e_gatew = ein("t_gate_w", [D, 8])
        g.e_segatew = ein("t_shared_expert_gate_w", [D, 1])
        g.e_gup = ein("t_experts_gate_up", [8, D, 2816])
        g.e_down = ein("t_experts_down", [8, 1408, D])
        g.e_shgate = ein("t_shared_gate_w", [D, 2816])
        g.e_shup = ein("t_shared_up_w", [D, 2816])
        g.e_shdown = ein("t_shared_down_w", [2816, D])
        g.c_sel8_d = ein("c_sel8", [8, 8 * 128])
        g.c_sel8 = sres("c_sel8_sb", [8, 8 * 128], F32)
        px.dma("sync", g.c_sel8[:, :], g.c_sel8_d[:, :])
        g.w_fm_f32 = scratch("w_fm_f32", [8, T])
        g.sw_f32 = scratch("sw_f32", [1, T])
        g.moe_acc = scratch("moe_acc", [D, T])
        g.out_fm = eout("out_fm", [D, T])

    for st in stages:
        if st == "ln0":
            stage_layernorm(px, nc, g, g.x_fm, g.xn_bf, g.xn_f32, "ln0")
        elif st == "l0a_proj":
            stage_l0a_proj(px, nc, g)
        elif st == "l0a_attn":
            stage_l0a_attn(px, nc, g)
        elif st == "l0a_out":
            stage_l0a_out(px, nc, g)
        elif st == "l0a_ffn":
            stage_l0a_ffn(px, nc, g)
        elif st.startswith("l0f") and st.endswith("a"):
            li = int(st[3:-1])
            src = g.xn_f32 if li == 0 else g.xl0
            stage_l0f_attn(px, nc, g, li, src, g.xl1)
        elif st.startswith("l0f") and st.endswith("f"):
            li = int(st[3:-1])
            stage_l0f_ffn(px, nc, g, li, g.xl1, g.xl0)
        elif st == "ln1x":
            stage_layernorm(px, nc, g, g.x_fm, g.xn1_bf, g.xn1_f32, "ln1x")
            for r0 in range(0, D, 128):
                px.dma("sync", g.x1_f32[r0:r0 + 128, :], g.x_fm[r0:r0 + 128, :])
            px.barrier()
        elif st == "comb0":
            stage_combine_ln(px, nc, g, g.x_fm, g.s0_f32, g.xl0,
                             g.x1_f32, g.xn1_bf, g.xn1_f32, "c0")
        elif st == "m_proj":
            stage_m_proj(px, nc, g)
        elif st == "m_conv":
            stage_m_conv(px, nc, g)
        elif st == "m_scan":
            stage_m_scan(px, nc, g)
        elif st == "m_out":
            stage_m_out(px, nc, g)
        elif st == "m_ffn":
            stage_m_ffn(px, nc, g)
        elif st == "moe_route":
            stage_moe_route(px, nc, g)
        elif st.startswith("moe_e"):
            stage_moe_expert(px, nc, g, int(st[5:]))
        elif st == "moe_shared":
            stage_moe_shared(px, nc, g)
        else:
            raise ValueError(st)

    for name in meta.get("taps", []):
        src = getattr(g, name)
        dst = eout("tap_" + name, list(src.tensor.shape), src.dtype)
        rows = int(src.tensor.shape[0])
        for r0 in range(0, rows, 128):
            rr = min(128, rows - r0)
            px.dma("sync", dst[r0:r0 + rr, :], src[r0:r0 + rr, :])
    px.barrier()
    emit_program(nc, px)
    return nc, g



ALL_STAGES = (["ln0", "l0a_proj", "l0a_attn", "l0a_out", "l0a_ffn"] +
              sum([[f"l0f{i}a", f"l0f{i}f"] for i in range(6)], []) +
              ["comb0", "m_proj", "m_conv", "m_scan", "m_out", "m_ffn",
               "moe_route"] + [f"moe_e{e}" for e in range(8)] + ["moe_shared"])


def kernel(**inputs):
    offsets = [int(v) for v in np.asarray(inputs["L0_attn_offsets"])]
    consts = build_consts(offsets)
    NO = len(offsets)
    mo = max(offsets)
    n_vm = (mo + 127) // 128
    vm = np.zeros((n_vm, 128, NO), np.float32)
    for i in range(n_vm):
        t = i * 128 + np.arange(128)
        vm[i] = np.where(t[:, None] >= np.array(offsets)[None, :], 0.0, -1e30)
    pb_row = np.ascontiguousarray(
        np.asarray(inputs["L0_attn_pos_bias"]).T.reshape(1, NO * 16))

    meta = {"stages": ALL_STAGES, "offsets": offsets, "n_vm": n_vm}
    nc, g = build_program(meta)

    base = {
        "c_ident": consts["ident"], "c_indP": consts["indP"],
        "c_indT": consts["indT"], "c_ones1": consts["ones1"],
        "c_ones_col": consts["ones_col"], "c_rot": consts["rot"],
        "c_R": consts["R"], "c_l0f_mask": consts["l0f_mask"],
        "c_selBC": consts["selBC"], "c_sel8": consts["sel8"],
        "pb_row": pb_row, "c_vmask": vm,
    }
    direct = ["L0_attn_norm1_scale", "L0_attn_norm2_scale", "L0_attn_w_qkv",
              "L0_attn_w_out", "L0_attn_w_gate", "L0_attn_b_gate",
              "L0_attn_w_ffn_gate", "L0_attn_w_ffn_up", "L0_attn_w_ffn_down",
              "L0_ffn_Wqkv", "L0_ffn_q_scale", "L0_ffn_k_scale", "L0_ffn_Wo",
              "L0_ffn_ln_g", "L0_ffn_ln_b", "L0_ffn_W1", "L0_ffn_b1",
              "L0_ffn_W2", "L0_ffn_b2", "L1_attn_in_proj_w", "L1_attn_conv_w",
              "L1_attn_conv_b", "L1_attn_x_proj_w", "L1_attn_dt_proj_w",
              "L1_attn_dt_proj_b", "L1_attn_A_log", "L1_attn_D",
              "L1_attn_out_proj_w", "L1_attn_ln_g", "L1_attn_ln_b",
              "L1_attn_ff_w1", "L1_attn_ff_b1", "L1_attn_ff_w2",
              "L1_attn_ff_b2"]
    for k in direct:
        base[k] = np.ascontiguousarray(inputs[k], dtype=np.float32)
    base["t_gate_w"] = np.ascontiguousarray(
        np.asarray(inputs["L1_ffn_gate_w"]).T, dtype=np.float32)
    base["t_shared_expert_gate_w"] = np.ascontiguousarray(
        np.asarray(inputs["L1_ffn_shared_expert_gate_w"]).T, dtype=np.float32)
    base["t_experts_gate_up"] = np.ascontiguousarray(
        np.asarray(inputs["L1_ffn_experts_gate_up"]).transpose(0, 2, 1),
        dtype=np.float32)
    base["t_experts_down"] = np.ascontiguousarray(
        np.asarray(inputs["L1_ffn_experts_down"]).transpose(0, 2, 1),
        dtype=np.float32)
    base["t_shared_gate_w"] = np.ascontiguousarray(
        np.asarray(inputs["L1_ffn_shared_gate_w"]).T, dtype=np.float32)
    base["t_shared_up_w"] = np.ascontiguousarray(
        np.asarray(inputs["L1_ffn_shared_up_w"]).T, dtype=np.float32)
    base["t_shared_down_w"] = np.ascontiguousarray(
        np.asarray(inputs["L1_ffn_shared_down_w"]).T, dtype=np.float32)

    x_full = np.asarray(inputs["L0_attn_x"])
    in_maps = []
    for c in range(8):
        mcore = dict(base)
        mcore["x_fm"] = np.ascontiguousarray(x_full[c].T, dtype=np.float32)
        in_maps.append(mcore)
    res = run_bass_kernel_spmd(nc, in_maps, core_ids=list(range(8)))
    out = np.stack([np.asarray(res.results[c]["out_fm"]).T for c in range(8)])
    return out.astype(np.float32)


# version 05
# speedup vs baseline: 1.5783x; 1.0887x over previous
import math
from contextlib import ExitStack
import numpy as np
import ml_dtypes
import concourse.bass as bass
import concourse.mybir as mybir
from concourse.ap import AP as BAP
from concourse.bass_utils import run_bass_kernel_spmd

F32 = mybir.dt.float32
BF16 = mybir.dt.bfloat16
I32 = mybir.dt.int32
AF = mybir.ActivationFunctionType
ALU = mybir.AluOpType
AX = mybir.AxisListType

T = 4096
D = 1024
TB = 512
NTB = T // TB
NKD = D // 128


QUEUES = ("sync", "tensor", "vector", "scalar", "gpsimd")
PARANOID = False
PARANOID_ENGINES = {"vector", "gpsimd", "scalar"}
DMA_SLOTS = 8
_DT_SZ = {F32: 4, BF16: 2, I32: 4}


def _ap_box(ap):
    t = ap.tensor
    name = t.name
    space = str(ap.space)
    esz = _DT_SZ.get(ap.dtype, 4)
    dims = list(ap.ap)
    off = int(ap.offset)
    if "SB" not in space and "PSUM" not in space and "Psum" not in space:
        lo = hi = 0
        for step, count in dims:
            d = int(step) * (int(count) - 1)
            if d >= 0:
                hi += d
            else:
                lo += d
        return (name, "D", 0, 0, (off + lo) * esz, (off + hi + 1) * esz)
    if "PSUM" in space or "Psum" in space:
        return (name, "S", 0, 127, 0, 1 << 30)
    tshape = list(t.shape)
    row_elems_t = 1
    for s in tshape[1:]:
        row_elems_t *= int(s)
    row_bytes = row_elems_t * _DT_SZ.get(t.dtype, 4)
    if row_bytes % esz:
        return (name, "S", 0, 127, 0, max(row_bytes, 4))
    row_elems = row_bytes // esz
    try:
        base_p = int(t.base_partition)
    except Exception:
        base_p = 0
    p_lo = p_hi = b_lo = b_hi = 0
    ok = True
    for step, count in dims:
        step = int(step)
        count = int(count)
        d = step * (count - 1)
        if step != 0 and step % row_elems == 0:
            dp = (step // row_elems) * (count - 1)
            if dp >= 0:
                p_hi += dp
            else:
                p_lo += dp
        elif abs(d) < row_elems:
            if d >= 0:
                b_hi += d
            else:
                b_lo += d
        else:
            ok = False
            break
    if not ok:
        return (name, "S", 0, 127, 0, row_bytes)
    p0 = off // row_elems
    c0 = off % row_elems
    return (name, "S", base_p + p0 + p_lo, base_p + p0 + p_hi,
            (c0 + b_lo) * esz, (c0 + b_hi) * esz + esz)


def _overlap(a, b):
    if a[0] != b[0]:
        return False
    if a[1] == "S" and (a[3] < b[2] or b[3] < a[2]):
        return False
    return not (a[5] <= b[4] or b[5] <= a[4])


class _Op:
    __slots__ = ("idx", "queue", "emit", "reads", "writes", "dma", "inc",
                 "waits", "scalar_reads", "tag", "semkey")

    def __init__(self, idx, queue, emit, reads, writes, dma, scalar_reads, tag):
        self.idx = idx
        self.queue = queue
        self.emit = emit
        self.reads = reads
        self.writes = writes
        self.dma = dma
        self.inc = False
        self.waits = []
        self.scalar_reads = scalar_reads
        self.tag = tag


class Px:

    def __init__(self, nc):
        self.nc = nc
        self.ops = []
        self.edges = []
        self._hist = {}
        self.n_bar = 0
        self._epoch_ops = 0

    def _boxes(self, aps):
        return [_ap_box(a) for a in aps if isinstance(a, BAP)]

    def _rec(self, queue, emit, reads=(), writes=(), dma=False,
             scalar_reads=(), tag=""):
        if self._epoch_ops >= 3000:
            self.barrier()
        self._epoch_ops += 1
        idx = len(self.ops)
        op = _Op(idx, queue, emit, self._boxes(reads), self._boxes(writes),
                 dma, self._boxes(scalar_reads), tag)
        dep = set()
        for box in op.reads + op.scalar_reads:
            for (j, b2, w2) in self._hist.get(box[0], ()):
                if w2 and _overlap(box, b2):
                    dep.add(j)
        for box in op.writes:
            for (j, b2, w2) in self._hist.get(box[0], ()):
                if _overlap(box, b2):
                    dep.add(j)
        for j in dep:
            p = self.ops[j]
            same = p.queue == queue and not p.dma and not dma
            if same and not PARANOID and queue not in PARANOID_ENGINES:
                continue
            self.edges.append((j, idx, same))
        for box in op.scalar_reads:
            for (j, b2, w2) in self._hist.get(box[0], ()):
                if w2 and _overlap(box, b2) and self.ops[j].queue == queue \
                        and not self.ops[j].dma:
                    self.edges.append((j, idx, True))
        for box in op.writes:
            self._hist.setdefault(box[0], []).append((idx, box, True))
        for box in op.reads + op.scalar_reads:
            self._hist.setdefault(box[0], []).append((idx, box, False))
        self.ops.append(op)
        return op

    def barrier(self):
        idx = len(self.ops)
        self.ops.append(_Op(idx, "BARRIER", None, [], [], False, [], "bar"))
        self._hist = {}
        self.n_bar += 1
        self._epoch_ops = 0

    def dma(self, q, out, in_, tag="", **kw):
        return self._rec(q, lambda e: e.dma_start(out=out, in_=in_, **kw),
                         reads=[in_], writes=[out], dma=True, tag=tag)

    def mm(self, out, lhsT, rhs, start=True, stop=True, tile_position=None,
           tag=""):
        return self._rec(
            "tensor",
            lambda e: e.matmul(out, lhsT, rhs, start=start, stop=stop,
                               tile_position=tile_position),
            reads=[lhsT, rhs], writes=[out], tag=tag)

    def transpose(self, out, in_, ident, tag=""):
        return self._rec("tensor", lambda e: e.transpose(out, in_, ident),
                         reads=[in_, ident], writes=[out], tag=tag)

    def act(self, out, in_, func, bias=0.0, scale=1.0, accum_out=None, tag=""):
        sreads = [b for b in (bias, scale) if isinstance(b, BAP)]
        writes = [out] + ([accum_out] if accum_out is not None else [])
        return self._rec(
            "scalar",
            lambda e: e.activation(out, in_, func, bias=bias, scale=scale,
                                   accum_out=accum_out),
            reads=[in_], writes=writes, scalar_reads=sreads, tag=tag)

    def tt(self, eng, out, in0, in1, op, tag=""):
        if eng == "gpsimd":
            eng = "vector"
        return self._rec(eng, lambda e: e.tensor_tensor(out, in0, in1, op),
                         reads=[in0, in1], writes=[out], tag=tag)

    def ts(self, eng, out, in0, s1, s2=None, op0=ALU.mult, op1=ALU.bypass,
           tag=""):
        sreads = [s for s in (s1, s2) if isinstance(s, BAP)]
        return self._rec(
            eng, lambda e: e.tensor_scalar(out, in0, s1, s2, op0, op1),
            reads=[in0], writes=[out], scalar_reads=sreads, tag=tag)

    def stt(self, eng, out, in0, scalar, in1, op0, op1, tag=""):
        sreads = [scalar] if isinstance(scalar, BAP) else []
        return self._rec(
            eng,
            lambda e: e.scalar_tensor_tensor(out, in0, scalar, in1, op0, op1),
            reads=[in0, in1], writes=[out], scalar_reads=sreads, tag=tag)

    def red(self, eng, out, in_, op, tag=""):
        return self._rec(eng, lambda e: e.tensor_reduce(out, in_, AX.X, op),
                         reads=[in_], writes=[out], tag=tag)

    def scan(self, out, d0, d1, init, op0=ALU.mult, op1=ALU.add, tag=""):
        sreads = [init] if isinstance(init, BAP) else []
        return self._rec(
            "vector",
            lambda e: e.tensor_tensor_scan(out, d0, d1, init, op0, op1),
            reads=[d0, d1], writes=[out], scalar_reads=sreads, tag=tag)

    def copy(self, eng, out, in_, tag=""):
        if eng == "scalar":
            return self._rec(eng, lambda e: e.copy(out, in_),
                             reads=[in_], writes=[out], tag=tag)
        return self._rec(eng, lambda e: e.tensor_copy(out=out, in_=in_),
                         reads=[in_], writes=[out], tag=tag)

    def memset(self, eng, ap_, v, tag=""):
        return self._rec(eng, lambda e: e.memset(ap_, v),
                         reads=[], writes=[ap_], tag=tag)

    def recip(self, out, in_, tag=""):
        return self._rec("vector", lambda e: e.reciprocal(out, in_),
                         reads=[in_], writes=[out], tag=tag)

    def iota(self, out, pattern, base=None, channel_multiplier=None, tag=""):
        return self._rec(
            "gpsimd",
            lambda e: e.iota(out, pattern, base=base,
                             channel_multiplier=channel_multiplier,
                             allow_small_or_imprecise_dtypes=True),
            reads=[], writes=[out], tag=tag)

    def finalize(self):
        last_c = {}
        forced = []
        for op in self.ops:
            if op.queue == "BARRIER":
                forced.extend(last_c.values())
                last_c = {}
            elif not op.dma:
                last_c[op.queue] = op.idx
        forced.extend(last_c.values())
        for j in forced:
            self.ops[j].inc = True
        for (j, _i, _f) in self.edges:
            self.ops[j].inc = True
        for op in self.ops:
            if op.dma:
                op.inc = True
        inc_count = {}
        dma_idx = {}
        op_inc_val = {}
        q_ops = {q: [] for q in QUEUES}
        for op in self.ops:
            if op.queue == "BARRIER":
                snap = dict(inc_count)
                for q in QUEUES:
                    q_ops[q].append(("BAR", snap))
                continue
            if op.dma:
                i = dma_idx.get(op.queue, 0)
                dma_idx[op.queue] = i + 1
                s = i % DMA_SLOTS
                j = i // DMA_SLOTS
                key = ("d", op.queue, s)
                op.semkey = key
                inc_count[key] = 16 * (j + 1)
                op_inc_val[op.idx] = (key, 16 * (j + 1))
                if j > 0:
                    op.waits.append((key, 16 * j, True))
            else:
                key = ("c", op.queue)
                if op.inc:
                    inc_count[key] = inc_count.get(key, 0) + 1
                    op_inc_val[op.idx] = (key, inc_count[key])
            q_ops[op.queue].append(("OP", op))
        for (j, i, f) in self.edges:
            key, val = op_inc_val[j]
            self.ops[i].waits.append((key, val, f))
        used_keys = set(inc_count.keys())
        sems = {}
        self._sem_guards = []
        for k in sorted(used_keys, key=str):
            gcm = self.nc.semaphore("_".join(str(p) for p in k))
            sems[k] = gcm.__enter__()
            self._sem_guards.append(gcm)
        return sems, q_ops

    def emit_queue(self, qname, eng, sems, q_ops):
        waited = {}

        def _w(key, val):
            if val <= waited.get(key, 0):
                return
            eng.wait_ge(sems[key], val)
            waited[key] = val

        for kind, payload in q_ops[qname]:
            if kind == "BAR":
                for key, val in payload.items():
                    if key == ("c", qname):
                        continue
                    _w(key, val)
                continue
            op = payload
            for w in op.waits:
                key, val = w[0], w[1]
                forced = w[2] if len(w) > 2 else False
                if key == ("c", qname) and not op.dma and not forced:
                    continue
                _w(key, val)
            inst = op.emit(eng)
            if op.inc:
                if op.dma:
                    inst.then_inc(sems[op.semkey], 16)
                else:
                    inst.then_inc(sems[("c", op.queue)], 1)


def emit_program(nc, px):
    sems, q_ops = px.finalize()
    with nc.Block() as block:
        @block.sync
        def _(e):
            px.emit_queue("sync", e, sems, q_ops)

        @block.tensor
        def _(e):
            px.emit_queue("tensor", e, sems, q_ops)

        @block.vector
        def _(e):
            px.emit_queue("vector", e, sems, q_ops)

        @block.scalar
        def _(e):
            px.emit_queue("scalar", e, sems, q_ops)

        @block.gpsimd
        def _(e):
            px.emit_queue("gpsimd", e, sems, q_ops)



class Ctx:
    pass


def make_psum_pool(nc):
    pool = []
    guards = []
    for i in range(8):
        gcm = nc.psum_tensor(f"pbank{i}", [128, TB], F32)
        pool.append(gcm.__enter__())
        guards.append(gcm)
    return pool, guards



def build_l0f_consts():
    HD, W, n = 64, 256, T
    half = np.arange(0, HD, 2, dtype=np.float64)
    inv_freq = 1.0 / 10000.0 ** (half / HD)
    t = np.arange(n, dtype=np.float64)
    freqs = t[:, None] * inv_freq[None, :]
    freqs = np.concatenate([freqs, freqs], -1)
    base_scale = (half + 0.4 * HD) / (1.4 * HD)
    power = (t - n // 2) / (W // 2)
    scale = base_scale[None, :] ** power[:, None]
    scale = np.concatenate([scale, scale], -1)
    cos = np.cos(freqs); sin = np.sin(freqs)
    rot = np.stack([
        np.tile((cos * scale).T, (2, 1)),
        np.tile((sin * scale).T, (2, 1)),
        np.tile((cos / scale).T, (2, 1)),
        np.tile((sin / scale).T, (2, 1)),
    ]).astype(np.float32)
    R = np.zeros((128, 128), np.float32)
    for blk in range(2):
        b = blk * 64
        for i in range(32):
            R[b + 32 + i, b + i] = -1.0
            R[b + i, b + 32 + i] = 1.0
    msk = np.zeros((2, 2, 128, 512), np.float32)
    p = np.arange(128)[:, None]; c = np.arange(512)[None, :]
    for qs in range(2):
        valid_n = (c >= qs * 128 + p) & (c <= qs * 128 + p + 256)
        msk[0, qs] = np.where(valid_n, 0.0, -1e30)
        valid_0 = (c >= 256) & (c <= 256 + qs * 128 + p)
        msk[1, qs] = np.where(valid_0, 0.0, -1e30)
    return rot, R, msk


def build_consts(offsets):
    c = {}
    c["ident"] = np.eye(128, dtype=np.float32)
    indP = np.zeros((8, 128, 16), np.float32)
    indT = np.zeros((8, 16, 128), np.float32)
    for pg in range(8):
        for p in range(128):
            h = 2 * pg + p // 64
            indP[pg, p, h] = 1.0
            indT[pg, h, p] = 1.0
    c["indP"] = indP
    c["indT"] = indT
    c["ones1"] = np.ones((1, 128), np.float32)
    c["ones_col"] = np.ones((128, 1), np.float32)
    rot, R, msk = build_l0f_consts()
    c["rot"] = rot
    c["R"] = R
    c["l0f_mask"] = msk
    selBC = np.zeros((96, 32 * 128), np.float32)
    for n in range(16):
        selBC[64 + n, (2 * n) * 128:(2 * n + 1) * 128] = 1.0
        selBC[80 + n, (2 * n + 1) * 128:(2 * n + 2) * 128] = 1.0
    c["selBC"] = selBC
    sel8 = np.zeros((8, 8 * 128), np.float32)
    for e in range(8):
        sel8[e, e * 128:(e + 1) * 128] = 1.0
    c["sel8"] = sel8
    return c



def stage_layernorm(px, nc, g, src_f32, dst_bf, dst_f32, name, eps=1e-5):
    with (
        nc.sbuf_tensor(name + "_x", [128, NKD * TB], F32) as xt,
        nc.sbuf_tensor(name + "_xb", [128, NKD * TB], BF16) as xbt,
        nc.sbuf_tensor(name + "_sq", [128, NKD * TB], BF16) as sqt,
        nc.sbuf_tensor(name + "_row", [1, 6 * TB], F32) as row,
        nc.sbuf_tensor(name + "_o", [128, NKD * TB], F32) as ot,
        nc.sbuf_tensor(name + "_ob", [128, NKD * TB], BF16) as otb,
    ):
        SUM, SQ, VAR, RSTD = 0, TB, 2 * TB, 3 * TB
        for tb in range(NTB):
            cs = slice(tb * TB, tb * TB + TB)
            for k in range(NKD):
                sl = slice(k * TB, (k + 1) * TB)
                px.dma("sync", xt[:, sl], src_f32[k * 128:(k + 1) * 128, cs])
                px.dma("gpsimd", xbt[:, sl], src_f32[k * 128:(k + 1) * 128, cs])
                px.act(sqt[:, sl], xbt[:, sl], AF.Square)
            ps_s = g.pool[0]
            ps_q = g.pool[1]
            for k in range(NKD):
                sl = slice(k * TB, (k + 1) * TB)
                px.mm(ps_s[0:1, :], g.ones_col_bf[:, 0:1], xbt[:, sl],
                      start=(k == 0), stop=(k == NKD - 1))
            for k in range(NKD):
                sl = slice(k * TB, (k + 1) * TB)
                px.mm(ps_q[0:1, :], g.ones_col_bf[:, 0:1], sqt[:, sl],
                      start=(k == 0), stop=(k == NKD - 1))
            px.act(row[:, SUM:SUM + TB], ps_s[0:1, :], AF.Copy, scale=1.0 / D)
            px.act(row[:, SQ:SQ + TB], ps_q[0:1, :], AF.Copy, scale=1.0 / D)
            px.tt("vector", row[:, VAR:VAR + TB], row[:, SUM:SUM + TB],
                  row[:, SUM:SUM + TB], ALU.mult)
            px.tt("vector", row[:, VAR:VAR + TB], row[:, SQ:SQ + TB],
                  row[:, VAR:VAR + TB], ALU.subtract)
            px.act(row[:, RSTD:RSTD + TB], row[:, VAR:VAR + TB], AF.Ln,
                   bias=eps)
            px.act(row[:, RSTD:RSTD + TB], row[:, RSTD:RSTD + TB], AF.Exp,
                   scale=-0.5)
            if getattr(g, "ln_debug", None) is not None:
                px.dma("sync", g.ln_debug[tb, 0][None, :], row[0:1, SUM:SUM + TB])
                px.dma("sync", g.ln_debug[tb, 1][None, :], row[0:1, SQ:SQ + TB])
                px.dma("sync", g.ln_debug[tb, 2][None, :], row[0:1, VAR:VAR + TB])
                px.dma("sync", g.ln_debug[tb, 3][None, :], row[0:1, RSTD:RSTD + TB])
            ps_m = g.pool[2]
            ps_r = g.pool[3]
            px.mm(ps_m[:, :], g.ones1_f32[:, :], row[:, SUM:SUM + TB])
            px.mm(ps_r[:, :], g.ones1_f32[:, :], row[:, RSTD:RSTD + TB])
            for k in range(NKD):
                sl = slice(k * TB, (k + 1) * TB)
                px.tt("vector", ot[:, sl], xt[:, sl], ps_m[:, :], ALU.subtract)
                px.tt("vector", ot[:, sl], ot[:, sl], ps_r[:, :], ALU.mult)
                px.copy("scalar", otb[:, sl], ot[:, sl])
                if dst_f32 is not None:
                    px.dma("sync", dst_f32[k * 128:(k + 1) * 128, cs],
                           ot[:, sl])
                px.dma("gpsimd", dst_bf[k * 128:(k + 1) * 128, cs],
                       otb[:, sl])
        px.barrier()



def kchunks(K):
    out = []
    k0 = 0
    while k0 < K:
        out.append((k0, min(128, K - k0)))
        k0 += 128
    return out


def load_w(px, sb, dram_w, K, M, tag="w"):
    for i, (k0, kk) in enumerate(kchunks(K)):
        px.dma("gpsimd", sb[:kk, i * M:i * M + M],
               dram_w[k0:k0 + kk, :], tag=tag)


def load_vec_cols(px, sb, dram_v, K, col0=0, tag="v"):
    for i, (k0, kk) in enumerate(kchunks(K)):
        px.dma("sync", sb[:kk, col0 + i:col0 + i + 1],
               dram_v[k0:k0 + kk][:, None], tag=tag)



def stage_l0a_proj(px, nc, g):
    with (
        nc.sbuf_tensor("ap_wqkv", [128, 8 * 3072], BF16) as wqkv,
        nc.sbuf_tensor("ap_wg", [128, 8 * 1024], BF16) as wg,
        nc.sbuf_tensor("ap_cols", [128, 16], F32) as cols,
        nc.sbuf_tensor("ap_xn", [128, 8 * TB], BF16) as xn,
        nc.sbuf_tensor("ap_xa", [128, 8 * TB], BF16) as xa,
        nc.sbuf_tensor("ap_sq", [128, 8 * TB], BF16) as sq,
        nc.sbuf_tensor("ap_row", [1, 2 * TB], F32) as row,
        nc.sbuf_tensor("ap_tmp", [128, TB], F32) as tmp,
        nc.sbuf_tensor("ap_ob", [128, 4 * TB], BF16) as ob,
    ):
        load_w(px, wqkv, g.w_qkv, D, 3072)
        load_w(px, wg, g.w_gate, D, D)
        load_vec_cols(px, cols, g.norm1_scale, D, 0)
        load_vec_cols(px, cols, g.b_gate, D, 8)
        for tb in range(NTB):
            cs = slice(tb * TB, (tb + 1) * TB)
            for k in range(NKD):
                sl = slice(k * TB, (k + 1) * TB)
                px.dma("sync", xn[:, sl], g.xn_bf[k * 128:(k + 1) * 128, cs])
                px.act(sq[:, sl], xn[:, sl], AF.Square)
            ps = g.pool[6]
            for k in range(NKD):
                px.mm(ps[0:1, :], g.ones_col_bf[:, 0:1],
                      sq[:, k * TB:(k + 1) * TB],
                      start=(k == 0), stop=(k == NKD - 1))
            px.act(row[:, 0:TB], ps[0:1, :], AF.Ln, bias=1e-6, scale=1.0 / D)
            px.act(row[:, 0:TB], row[:, 0:TB], AF.Exp, scale=-0.5)
            ps_r = g.pool[7]
            px.mm(ps_r[:, :], g.ones1_f32[:, :], row[:, 0:TB])
            for k in range(NKD):
                sl = slice(k * TB, (k + 1) * TB)
                px.tt("vector", tmp[:, :], xn[:, sl], ps_r[:, :], ALU.mult)
                px.ts("vector", xa[:, sl], tmp[:, :], cols[:, k:k + 1], None,
                      ALU.mult)
            for m in range(24):
                bank = g.pool[m % 4]
                for k in range(NKD):
                    px.mm(bank[:, :], wqkv[:, k * 3072 + m * 128:
                                           k * 3072 + (m + 1) * 128],
                          xa[:, k * TB:(k + 1) * TB],
                          start=(k == 0), stop=(k == NKD - 1))
                dst = (g.q_s, g.k_s, g.v_s)[m // 8]
                mm = m % 8
                o = ob[:, (m % 4) * TB:(m % 4 + 1) * TB]
                px.act(o, bank[:, :], AF.Copy)
                px.dma("gpsimd", dst[mm * 128:(mm + 1) * 128, cs], o)
            for m in range(NKD):
                bank = g.pool[4 + m % 2]
                for k in range(NKD):
                    px.mm(bank[:, :], wg[:, k * D + m * 128:
                                         k * D + (m + 1) * 128],
                          xa[:, k * TB:(k + 1) * TB],
                          start=(k == 0), stop=(k == NKD - 1))
                o = ob[:, (m % 4) * TB:(m % 4 + 1) * TB]
                px.act(o, bank[:, :], AF.Sigmoid, bias=cols[:, 8 + m:9 + m])
                px.dma("gpsimd", g.gate_s[m * 128:(m + 1) * 128, cs], o)
        px.barrier()



def stage_l0a_attn(px, nc, g):
    offs = g.offsets
    max_seg = getattr(g, "aa_max_seg", NTB)
    do_sm = getattr(g, "aa_do_sm", True)
    do_av = getattr(g, "aa_do_av", True)
    NO = len(offs)
    WIN = 2048
    assert max(offs) + TB <= WIN and min(offs) >= 0
    NOC = NO * 16
    with (
        nc.sbuf_tensor("aa_q", [128, 8 * TB], BF16) as q,
        nc.sbuf_tensor("aa_kw", [128, 8 * WIN], BF16) as kw,
        nc.sbuf_tensor("aa_vw", [128, 8 * WIN], BF16) as vw,
        nc.sbuf_tensor("aa_gt", [128, 8 * TB], BF16) as gt,
        nc.sbuf_tensor("aa_P", [128, 4 * TB], BF16) as Pb,
        nc.sbuf_tensor("aa_sc", [16, TB], BF16) as scb,
        nc.sbuf_tensor("aa_S", [128, 4 * NOC], F32) as Sb,
        nc.sbuf_tensor("aa_al", [128, 4 * NOC], BF16) as al,
        nc.sbuf_tensor("aa_m16", [128, 128], F32) as m16,
        nc.sbuf_tensor("aa_pb", [128, NOC], F32) as pbb,
        nc.sbuf_tensor("aa_pbrow", [1, NOC], F32) as pbrow,
        nc.sbuf_tensor("aa_vm", [128, g.n_vm * NO if g.n_vm else 1], F32) as vm,
        nc.sbuf_tensor("aa_aT", [16, TB], BF16) as aT,
        nc.sbuf_tensor("aa_tmp", [128, TB], F32) as tmpb,
        nc.sbuf_tensor("aa_acc", [128, 8 * TB], F32) as acc,
        nc.sbuf_tensor("aa_ob", [128, 2 * TB], BF16) as ob,
    ):
        px.dma("sync", pbrow[:, :], g.pb_row[:, :])
        if g.n_vm:
            for i in range(g.n_vm):
                px.dma("sync", vm[:, i * NO:(i + 1) * NO], g.c_vmask[i])
        px.mm(g.pool[0][:, :], g.ones1_f32[:, :], pbrow[:, 0:512])
        px.mm(g.pool[1][:, :192], g.ones1_f32[:, :], pbrow[:, 512:NOC])
        px.copy("vector", pbb[:, 0:512], g.pool[0][:, :])
        px.copy("vector", pbb[:, 512:NOC], g.pool[1][:, :192])
        for s in range(max_seg):
            w0 = s * TB - (WIN - TB)
            nz = max(0, -w0)
            for pg in range(8):
                r = slice(pg * 128, (pg + 1) * 128)
                px.dma("sync", q[:, pg * TB:(pg + 1) * TB],
                       g.q_s[r, s * TB:(s + 1) * TB])
                px.dma("sync", gt[:, pg * TB:(pg + 1) * TB],
                       g.gate_s[r, s * TB:(s + 1) * TB])
                if nz:
                    px.memset("gpsimd", kw[:, pg * WIN:pg * WIN + nz], 0)
                    px.memset("gpsimd", vw[:, pg * WIN:pg * WIN + nz], 0)
                px.dma("sync", kw[:, pg * WIN + nz:(pg + 1) * WIN],
                       g.k_s[r, w0 + nz:s * TB + TB])
                px.dma("sync", vw[:, pg * WIN + nz:(pg + 1) * WIN],
                       g.v_s[r, w0 + nz:s * TB + TB])
                px.memset("vector", acc[:, pg * TB:(pg + 1) * TB], 0)
            for oi, o in enumerate(offs):
                sh = WIN - TB - o
                ps_sc = g.pool[oi % 2]
                for pg in range(8):
                    P = Pb[:, (pg % 4) * TB:(pg % 4 + 1) * TB]
                    px.tt("vector" if pg % 2 == 0 else "gpsimd", P,
                          q[:, pg * TB:(pg + 1) * TB],
                          kw[:, pg * WIN + sh:pg * WIN + sh + TB], ALU.mult)
                    px.mm(ps_sc[0:16, :], g.indP_bf[:, pg * 16:(pg + 1) * 16],
                          P, start=(pg == 0), stop=(pg == 7))
                px.ts("vector", scb[:, :], ps_sc[0:16, :], 0.125, None,
                      ALU.mult)
                pT = g.pool[2 + oi % 2][:, :].bitcast(BF16)
                for ts in range(4 if getattr(g, "aa_do_tr", True) else 0):
                    px.transpose(pT[:, ts * 16:(ts + 1) * 16],
                                 scb[:, ts * 128:(ts + 1) * 128],
                                 g.ident_bf[0:16, 0:16])
                for ts in range(4 if getattr(g, "aa_do_ev", True) and getattr(g, "aa_do_tr", True) else 0):
                    if getattr(g, "aa_ev_contig", False):
                        dst = Sb[:, ts * NOC + oi * 16:ts * NOC + oi * 16 + 16]
                        src = pT[:, ts * 16:(ts + 1) * 16]
                    else:
                        dst = Sb[:, ts * NOC:(ts + 1) * NOC].rearrange(
                            "p (h o) -> p h o", o=NO)[:, :, oi:oi + 1]
                        src = pT[:, ts * 16:(ts + 1) * 16].unsqueeze(2)
                    px.copy("vector", dst, src)
            for ts in range(4 if do_sm else 0):
                S3 = Sb[:, ts * NOC:(ts + 1) * NOC].rearrange(
                    "p (h o) -> p h o", o=NO)
                pb3 = pbb[:, :].rearrange("p (h o) -> p h o", o=NO)
                px.tt("vector", S3, S3, pb3, ALU.add)
                gm = s * 4 + ts
                if gm < g.n_vm:
                    vm3 = vm[:, gm * NO:(gm + 1) * NO].unsqueeze(1) \
                        .broadcast_to([128, 16, NO])
                    px.tt("vector", S3, S3, vm3, ALU.add)
                mx = m16[:, ts * 16:(ts + 1) * 16]
                px.red("vector", mx, S3, ALU.max)
                mx3 = mx.unsqueeze(2).broadcast_to([128, 16, NO])
                px.tt("vector", S3, S3, mx3, ALU.subtract)
                px.act(Sb[:, ts * NOC:(ts + 1) * NOC],
                       Sb[:, ts * NOC:(ts + 1) * NOC], AF.Exp)
                sm = m16[:, 64 + ts * 16:64 + (ts + 1) * 16]
                px.red("vector", sm, S3, ALU.add)
                px.recip(sm, sm)
                sm3 = sm.unsqueeze(2).broadcast_to([128, 16, NO])
                px.tt("vector", al[:, ts * NOC:(ts + 1) * NOC].rearrange(
                    "p (h o) -> p h o", o=NO), S3, sm3, ALU.mult)
            for oi, o in enumerate(offs if do_av else []):
                sh = WIN - TB - o
                pT = g.pool[4 + oi % 2][:, :].bitcast(BF16)
                for ts in range(4):
                    src = al[:, ts * NOC:(ts + 1) * NOC].rearrange(
                        "p (h o) -> p h o", o=NO)[:, :, oi:oi + 1].squeeze(2)
                    px.transpose(pT[0:16, ts * 128:(ts + 1) * 128], src,
                                 g.ident_bf[:, :])
                    px.copy("vector", aT[:, ts * 128:(ts + 1) * 128],
                            pT[0:16, ts * 128:(ts + 1) * 128])
                for pg in range(8):
                    ps_ab = g.pool[6 + pg % 2]
                    px.mm(ps_ab[:, :], g.indT_bf[:, pg * 128:(pg + 1) * 128],
                          aT[:, :], start=True, stop=True)
                    px.tt("vector", tmpb[:, :],
                          vw[:, pg * WIN + sh:pg * WIN + sh + TB],
                          ps_ab[:, :], ALU.mult)
                    px.tt("gpsimd", acc[:, pg * TB:(pg + 1) * TB],
                          acc[:, pg * TB:(pg + 1) * TB], tmpb[:, :], ALU.add)
            for pg in range(8):
                o = ob[:, (pg % 2) * TB:(pg % 2 + 1) * TB]
                px.tt("vector", o, acc[:, pg * TB:(pg + 1) * TB],
                      gt[:, pg * TB:(pg + 1) * TB], ALU.mult)
                px.dma("gpsimd", g.attng_s[pg * 128:(pg + 1) * 128,
                                           s * TB:(s + 1) * TB], o)
        px.barrier()



def stage_l0a_out(px, nc, g):
    with (
        nc.sbuf_tensor("ao_wout", [128, 8 * D], BF16) as wout,
        nc.sbuf_tensor("ao_cols", [128, 8], F32) as cols,
        nc.sbuf_tensor("ao_ag", [128, 8 * TB], BF16) as ag,
        nc.sbuf_tensor("ao_xb", [128, 8 * TB], F32) as xb,
        nc.sbuf_tensor("ao_xn2", [128, 8 * TB], BF16) as xn2,
        nc.sbuf_tensor("ao_row", [1, TB], F32) as row,
        nc.sbuf_tensor("ao_tmp", [128, TB], F32) as tmpb,
    ):
        load_w(px, wout, g.w_out, D, D)
        load_vec_cols(px, cols, g.norm2_scale, D, 0)
        for tb in range(NTB):
            cs = slice(tb * TB, (tb + 1) * TB)
            for k in range(NKD):
                sl = slice(k * TB, (k + 1) * TB)
                px.dma("sync", ag[:, sl], g.attng_s[k * 128:(k + 1) * 128, cs])
            for m in range(NKD):
                bank = g.pool[m % 3]
                for k in range(NKD):
                    px.mm(bank[:, :],
                          wout[:, k * D + m * 128:k * D + (m + 1) * 128],
                          ag[:, k * TB:(k + 1) * TB],
                          start=(k == 0), stop=(k == NKD - 1))
                sl = slice(m * TB, (m + 1) * TB)
                px.dma("sync", xb[:, sl], g.xn_f32[m * 128:(m + 1) * 128, cs])
                px.tt("vector", xb[:, sl], xb[:, sl], bank[:, :], ALU.add)
                px.dma("gpsimd", g.xb_f32[m * 128:(m + 1) * 128, cs],
                       xb[:, sl])
            for k in range(NKD):
                sl = slice(k * TB, (k + 1) * TB)
                px.act(ag[:, sl], xb[:, sl], AF.Square)
            ps = g.pool[6]
            for k in range(NKD):
                px.mm(ps[0:1, :], g.ones_col_bf[:, 0:1],
                      ag[:, k * TB:(k + 1) * TB],
                      start=(k == 0), stop=(k == NKD - 1))
            px.act(row[:, :], ps[0:1, :], AF.Ln, bias=1e-6, scale=1.0 / D)
            px.act(row[:, :], row[:, :], AF.Exp, scale=-0.5)
            ps_r = g.pool[7]
            px.mm(ps_r[:, :], g.ones1_f32[:, :], row[:, :])
            for k in range(NKD):
                sl = slice(k * TB, (k + 1) * TB)
                px.tt("vector", tmpb[:, :], xb[:, sl], ps_r[:, :], ALU.mult)
                px.ts("vector", xn2[:, sl], tmpb[:, :], cols[:, k:k + 1],
                      None, ALU.mult)
                px.dma("gpsimd", g.xn2_bf[k * 128:(k + 1) * 128, cs],
                       xn2[:, sl])
        px.barrier()


def stage_l0a_ffn(px, nc, g):
    FFN = 2816
    NM_F = FFN // 128
    with (
        nc.sbuf_tensor("af_wgate", [128, 8 * FFN], BF16) as wgate,
        nc.sbuf_tensor("af_wup", [128, 8 * FFN], BF16) as wup,
        nc.sbuf_tensor("af_wdown", [128, NM_F * D], BF16) as wdown,
        nc.sbuf_tensor("af_xn2", [128, 8 * TB], BF16) as xn2,
        nc.sbuf_tensor("af_h", [128, NM_F * TB], BF16) as hb,
        nc.sbuf_tensor("af_gs", [128, TB], BF16) as gs,
        nc.sbuf_tensor("af_xbt", [128, TB], F32) as xbt,
        nc.sbuf_tensor("af_o", [128, TB], F32) as outb,
    ):
        load_w(px, wgate, g.w_ffn_gate, D, FFN)
        load_w(px, wup, g.w_ffn_up, D, FFN)
        load_w(px, wdown, g.w_ffn_down, FFN, D)
        for tb in range(NTB):
            cs = slice(tb * TB, (tb + 1) * TB)
            for k in range(NKD):
                sl = slice(k * TB, (k + 1) * TB)
                px.dma("sync", xn2[:, sl], g.xn2_bf[k * 128:(k + 1) * 128, cs])
            for m in range(NM_F):
                bg = g.pool[2 * (m % 2)]
                bu = g.pool[2 * (m % 2) + 1]
                for k in range(NKD):
                    px.mm(bg[:, :],
                          wgate[:, k * FFN + m * 128:k * FFN + (m + 1) * 128],
                          xn2[:, k * TB:(k + 1) * TB],
                          start=(k == 0), stop=(k == NKD - 1))
                for k in range(NKD):
                    px.mm(bu[:, :],
                          wup[:, k * FFN + m * 128:k * FFN + (m + 1) * 128],
                          xn2[:, k * TB:(k + 1) * TB],
                          start=(k == 0), stop=(k == NKD - 1))
                px.act(gs[:, :], bg[:, :], AF.Silu)
                px.tt("vector", hb[:, m * TB:(m + 1) * TB], gs[:, :],
                      bu[:, :], ALU.mult)
            for m in range(NKD):
                bank = g.pool[4 + m % 2]
                for k in range(NM_F):
                    px.mm(bank[:, :],
                          wdown[:, k * D + m * 128:k * D + (m + 1) * 128],
                          hb[:, k * TB:(k + 1) * TB],
                          start=(k == 0), stop=(k == NM_F - 1))
                px.dma("sync", xbt[:, :], g.xb_f32[m * 128:(m + 1) * 128, cs])
                px.tt("vector", outb[:, :], xbt[:, :], bank[:, :], ALU.add)
                px.dma("sync", g.s0_f32[m * 128:(m + 1) * 128, cs],
                       outb[:, :])
        px.barrier()



def _shift_load(px, g, dst_bf, src_f32, pg, tb):
    cs0 = tb * TB
    r = slice(pg * 128, (pg + 1) * 128)
    if pg < 4:
        px.dma("gpsimd", dst_bf, src_f32[r, cs0:cs0 + TB])
    elif tb == 0:
        px.memset("vector", dst_bf[:, 0:1], 0)
        px.dma("gpsimd", dst_bf[:, 1:TB], src_f32[r, 0:TB - 1])
    else:
        px.dma("gpsimd", dst_bf, src_f32[r, cs0 - 1:cs0 + TB - 1])


def stage_l0f_attn(px, nc, g, li, src, dst):
    specs = [
        ("fa_wqkv", [128, 8 * 3072], BF16), ("fa_wo", [128, 8 * D], BF16),
        ("fa_qkv", [128, 24 * TB], BF16), ("fa_sq", [128, TB], BF16),
        ("fa_xs", [128, 8 * TB], BF16), ("fa_rr", [16, 2 * TB], F32),
        ("fa_rot", [128, TB], F32), ("fa_eff", [128, 4 * TB], BF16),
        ("fa_qsks", [128, 2], F32), ("fa_qcur", [128, 8 * TB], BF16),
        ("fa_kring", [128, 2 * 8 * TB], BF16), ("fa_vtm", [128, 2 * 4 * 1024], BF16),
        ("fa_msk", [128, 4 * TB], F32), ("fa_sm", [128, 2 * TB], F32),
        ("fa_mx", [128, 4], F32), ("fa_al", [128, 2 * TB], BF16),
        ("fa_aT", [128, 4 * 256], BF16), ("fa_ao", [128, 8 * TB], BF16),
        ("fa_xl", [128, TB], F32), ("fa_o", [128, TB], F32),
        ("fa_t1", [128, TB], F32), ("fa_t2", [128, TB], F32),
    ]
    with ExitStack() as _st:
        (wqkv, wo, qkvb, sqb, xsb, rr, rotb, eff, qsks, qcur, kring, vtm,
         msk, smb, mxc, alb, aTb, aob, xlb, outb, t1, t2) = [
            _st.enter_context(nc.sbuf_tensor(f"{n}_{li}", sh, dt))
            for (n, sh, dt) in specs]
        load_w(px, wqkv, g.f_Wqkv[li], D, 3072)
        load_w(px, wo, g.f_Wo[li], D, D)
        px.dma("sync", qsks[0:64, 0:1], g.f_qscale[li][:, None])
        px.dma("sync", qsks[64:128, 0:1], g.f_qscale[li][:, None])
        px.dma("sync", qsks[0:64, 1:2], g.f_kscale[li][:, None])
        px.dma("sync", qsks[64:128, 1:2], g.f_kscale[li][:, None])
        for kind in range(2):
            for qs_ in range(2):
                px.dma("sync",
                       msk[:, (kind * 2 + qs_) * TB:(kind * 2 + qs_ + 1) * TB],
                       g.c_l0f_mask[kind, qs_])
        px.memset("vector", kring[:, :], 0)
        px.memset("gpsimd", vtm[:, :], 0)
        for tb in range(NTB):
            slot = tb % 2
            cs = slice(tb * TB, (tb + 1) * TB)
            for j in range(4):
                px.dma("sync", rotb[:, :], g.c_rot[j][:, cs])
                px.copy("vector", eff[:, j * TB:(j + 1) * TB], rotb[:, :])
            for pg in range(8):
                _shift_load(px, g, xsb[:, pg * TB:(pg + 1) * TB], src, pg, tb)
            for m in range(24):
                bank = g.pool[m % 2]
                for k in range(NKD):
                    px.mm(bank[:, :],
                          wqkv[:, k * 3072 + m * 128:k * 3072 + (m + 1) * 128],
                          xsb[:, k * TB:(k + 1) * TB],
                          start=(k == 0), stop=(k == NKD - 1))
                px.act(qkvb[:, m * TB:(m + 1) * TB], bank[:, :], AF.Copy)
                if m < 16:
                    px.act(sqb[:, :], bank[:, :], AF.Square)
                    ss = g.pool[2 + m // 8]
                    px.mm(ss[0:16, :], g.indP_bf[:, (m % 8) * 16:(m % 8 + 1) * 16],
                          sqb[:, :], start=(m % 8 == 0), stop=(m % 8 == 7))
            for j in range(2):
                px.act(rr[:, j * TB:(j + 1) * TB], g.pool[2 + j][0:16, :], AF.Ln)
                px.act(rr[:, j * TB:(j + 1) * TB], rr[:, j * TB:(j + 1) * TB],
                       AF.Exp, scale=-0.5)
                px.ts("vector", rr[:, j * TB:(j + 1) * TB],
                      rr[:, j * TB:(j + 1) * TB], 1e12, None, ALU.min)
            for m in range(16):
                pg = m % 8
                isq = m < 8
                rsl = rr[:, 0:TB] if isq else rr[:, TB:2 * TB]
                rb = g.pool[4]
                px.mm(rb[:, :], g.indT_f32[:, pg * 128:(pg + 1) * 128], rsl,
                      start=True, stop=True)
                qn = xsb[:, pg * TB:(pg + 1) * TB]
                px.tt("vector", qn, qkvb[:, m * TB:(m + 1) * TB], rb[:, :],
                      ALU.mult)
                px.ts("vector", qn, qn, qsks[:, 0:1] if isq else qsks[:, 1:2],
                      None, ALU.mult)
                rh = g.pool[5]
                px.mm(rh[:, :], g.c_R_bf[:, :], qn, start=True, stop=True)
                co = eff[:, 0 * TB:1 * TB] if isq else eff[:, 2 * TB:3 * TB]
                si = eff[:, 1 * TB:2 * TB] if isq else eff[:, 3 * TB:4 * TB]
                px.tt("gpsimd", t1[:, :], qn, co, ALU.mult)
                px.tt("vector", t2[:, :], rh[:, :], si, ALU.mult)
                dstq = qcur[:, pg * TB:(pg + 1) * TB] if isq else \
                    kring[:, (slot * 8 + pg) * TB:(slot * 8 + pg + 1) * TB]
                px.tt("vector", dstq, t1[:, :], t2[:, :], ALU.add)
            for pg in range(8):
                m = 16 + pg
                for ts_ in range(4):
                    pT = g.pool[6][:, :].bitcast(BF16)
                    px.transpose(pT[:, 0:128],
                                 qkvb[:, m * TB + ts_ * 128:m * TB + (ts_ + 1) * 128],
                                 g.ident_bf[:, :])
                    px.copy("vector", vtm[:, slot * 4096 + ts_ * 1024 + pg * 128:
                                          slot * 4096 + ts_ * 1024 + (pg + 1) * 128],
                            pT[:, 0:128])
            for w2 in range(2):
                w = tb * 2 + w2
                kind = 1 if w == 0 else 0
                for h in range(16):
                    pg = h // 2
                    qrow = slice((h % 2) * 64, (h % 2) * 64 + 64)
                    for qs_ in range(2):
                        sc = g.pool[qs_]
                        lhs = qcur[qrow, pg * TB + w2 * 256 + qs_ * 128:
                                   pg * TB + w2 * 256 + (qs_ + 1) * 128]
                        for jh in range(2):
                            j0 = (w - 1) * 256 + jh * 256
                            if j0 < 0:
                                px.memset("vector", sc[:, jh * 256:(jh + 1) * 256], 0)
                                continue
                            ksl = (j0 // 512) % 2
                            kof = j0 % 512
                            px.mm(sc[:, jh * 256:(jh + 1) * 256], lhs,
                                  kring[qrow, (ksl * 8 + pg) * TB + kof:
                                        (ksl * 8 + pg) * TB + kof + 256],
                                  start=True, stop=True)
                        mx = mxc[:, qs_:qs_ + 1]
                        px.red("vector", mx, sc[:, :], ALU.max)
                        px.stt("vector", smb[:, qs_ * TB:(qs_ + 1) * TB],
                               sc[:, :], mx, msk[:, (kind * 2 + qs_) * TB:
                                                 (kind * 2 + qs_ + 1) * TB],
                               ALU.subtract, ALU.add)
                        px.act(smb[:, qs_ * TB:(qs_ + 1) * TB],
                               smb[:, qs_ * TB:(qs_ + 1) * TB], AF.Exp,
                               scale=8.0, accum_out=mxc[:, 2 + qs_:3 + qs_])
                        px.recip(mxc[:, 2 + qs_:3 + qs_], mxc[:, 2 + qs_:3 + qs_])
                        px.ts("vector", alb[:, qs_ * TB:(qs_ + 1) * TB],
                              smb[:, qs_ * TB:(qs_ + 1) * TB],
                              mxc[:, 2 + qs_:3 + qs_], None, ALU.mult)
                        pT = g.pool[2 + qs_][:, :].bitcast(BF16)
                        for jc in range(4):
                            px.transpose(pT[:, jc * 128:(jc + 1) * 128],
                                         alb[:, qs_ * TB + jc * 128:
                                             qs_ * TB + (jc + 1) * 128],
                                         g.ident_bf[:, :])
                        for jc in range(4):
                            px.copy("vector",
                                    aTb[:, jc * 256 + qs_ * 128:
                                        jc * 256 + (qs_ + 1) * 128],
                                    pT[:, jc * 128:(jc + 1) * 128])
                    av = g.pool[4 + (h % 2)]
                    nj = 0
                    for jc in range(4):
                        j0 = (w - 1) * 256 + jc * 128
                        if j0 < 0:
                            continue
                        vsl = (j0 // 512) % 2
                        vtt = (j0 % 512) // 128
                        px.mm(av[0:64, 0:256],
                              vtm[:, vsl * 4096 + vtt * 1024 + h * 64:
                                  vsl * 4096 + vtt * 1024 + (h + 1) * 64],
                              aTb[:, jc * 256:(jc + 1) * 256],
                              start=(nj == 0), stop=(jc == 3))
                        nj += 1
                    px.act(aob[qrow, pg * TB + w2 * 256:pg * TB + w2 * 256 + 256],
                           av[0:64, 0:256], AF.Copy)
            for m in range(NKD):
                bank = g.pool[6 + m % 2]
                for k in range(NKD):
                    px.mm(bank[:, :], wo[:, k * D + m * 128:k * D + (m + 1) * 128],
                          aob[:, k * TB:(k + 1) * TB],
                          start=(k == 0), stop=(k == NKD - 1))
                px.dma("sync", xlb[:, :], src[m * 128:(m + 1) * 128, cs])
                px.tt("vector", outb[:, :], xlb[:, :], bank[:, :], ALU.add)
                px.dma("sync", dst[m * 128:(m + 1) * 128, cs], outb[:, :])
        px.barrier()


def stage_l0f_ffn(px, nc, g, li, src, dst):
    DI = 2730
    NC1 = len(kchunks(DI))
    with (
        nc.sbuf_tensor(f"ff{li}_w1", [128, 8 * 5460], BF16) as w1,
        nc.sbuf_tensor(f"ff{li}_w2", [128, NC1 * D], BF16) as w2,
        nc.sbuf_tensor(f"ff{li}_cols", [128, 16 + 2 * NC1 + 8], F32) as cols,
        nc.sbuf_tensor(f"ff{li}_xs", [128, 8 * TB], F32) as xsf,
        nc.sbuf_tensor(f"ff{li}_xsb", [128, 8 * TB], BF16) as xsb,
        nc.sbuf_tensor(f"ff{li}_xn", [128, 8 * TB], BF16) as xnb,
        nc.sbuf_tensor(f"ff{li}_h", [128, NC1 * TB], BF16) as hb,
        nc.sbuf_tensor(f"ff{li}_gg", [128, TB], BF16) as gg,
        nc.sbuf_tensor(f"ff{li}_row", [1, 4 * TB], F32) as row,
        nc.sbuf_tensor(f"ff{li}_o", [128, TB], F32) as outb,
    ):
        load_w(px, w1, g.f_W1[li], D, 5460)
        load_w(px, w2, g.f_W2[li], DI, D)
        load_vec_cols(px, cols, g.f_lng[li], D, 0)
        load_vec_cols(px, cols, g.f_lnb[li], D, 8)
        for i, (k0, kk) in enumerate(kchunks(DI)):
            px.dma("sync", cols[0:kk, 16 + i:17 + i], g.f_b1[li][k0:k0 + kk][:, None])
            px.dma("sync", cols[0:kk, 16 + NC1 + i:17 + NC1 + i],
                   g.f_b1[li][DI + k0:DI + k0 + kk][:, None])
        load_vec_cols(px, cols, g.f_b2[li], D, 16 + 2 * NC1)
        SUM, SQ, VAR, RSTD = 0, TB, 2 * TB, 3 * TB
        for tb in range(NTB):
            cs = slice(tb * TB, (tb + 1) * TB)
            for pg in range(8):
                sl = slice(pg * TB, (pg + 1) * TB)
                _shift_load(px, g, xsb[:, sl], src, pg, tb)
                cs0 = tb * TB
                r = slice(pg * 128, (pg + 1) * 128)
                if pg < 4:
                    px.dma("sync", xsf[:, sl], src[r, cs0:cs0 + TB])
                elif tb == 0:
                    px.memset("gpsimd", xsf[:, sl][:, 0:1], 0)
                    px.dma("sync", xsf[:, sl][:, 1:TB], src[r, 0:TB - 1])
                else:
                    px.dma("sync", xsf[:, sl], src[r, cs0 - 1:cs0 + TB - 1])
                px.act(xnb[:, sl], xsb[:, sl], AF.Square)
            ps_s = g.pool[6]
            ps_q = g.pool[7]
            for k in range(NKD):
                px.mm(ps_s[0:1, :], g.ones_col_bf[:, 0:1],
                      xsb[:, k * TB:(k + 1) * TB],
                      start=(k == 0), stop=(k == NKD - 1))
            for k in range(NKD):
                px.mm(ps_q[0:1, :], g.ones_col_bf[:, 0:1],
                      xnb[:, k * TB:(k + 1) * TB],
                      start=(k == 0), stop=(k == NKD - 1))
            px.act(row[:, SUM:SUM + TB], ps_s[0:1, :], AF.Copy, scale=1.0 / D)
            px.act(row[:, SQ:SQ + TB], ps_q[0:1, :], AF.Copy, scale=1.0 / D)
            px.tt("vector", row[:, VAR:VAR + TB], row[:, SUM:SUM + TB],
                  row[:, SUM:SUM + TB], ALU.mult)
            px.tt("vector", row[:, VAR:VAR + TB], row[:, SQ:SQ + TB],
                  row[:, VAR:VAR + TB], ALU.subtract)
            px.act(row[:, RSTD:RSTD + TB], row[:, VAR:VAR + TB], AF.Ln,
                   bias=1e-5)
            px.act(row[:, RSTD:RSTD + TB], row[:, RSTD:RSTD + TB], AF.Exp,
                   scale=-0.5)
            ps_m = g.pool[0]
            ps_r = g.pool[1]
            px.mm(ps_m[:, :], g.ones1_f32[:, :], row[:, SUM:SUM + TB])
            px.mm(ps_r[:, :], g.ones1_f32[:, :], row[:, RSTD:RSTD + TB])
            for k in range(NKD):
                sl = slice(k * TB, (k + 1) * TB)
                px.tt("vector", xsf[:, sl], xsf[:, sl], ps_m[:, :], ALU.subtract)
                px.tt("vector", xsf[:, sl], xsf[:, sl], ps_r[:, :], ALU.mult)
                px.ts("vector", xnb[:, sl], xsf[:, sl], cols[:, k:k + 1],
                      cols[:, 8 + k:9 + k], ALU.mult, ALU.add)
            for i, (k0, kk) in enumerate(kchunks(DI)):
                ba = g.pool[2 * (i % 2)]
                bgt = g.pool[2 * (i % 2) + 1]
                for k in range(NKD):
                    px.mm(ba[0:kk, :],
                          w1[:, k * 5460 + k0:k * 5460 + k0 + kk],
                          xnb[:, k * TB:(k + 1) * TB],
                          start=(k == 0), stop=(k == NKD - 1))
                for k in range(NKD):
                    px.mm(bgt[0:kk, :],
                          w1[:, k * 5460 + DI + k0:k * 5460 + DI + k0 + kk],
                          xnb[:, k * TB:(k + 1) * TB],
                          start=(k == 0), stop=(k == NKD - 1))
                px.act(gg[0:kk, :], bgt[0:kk, :], AF.Gelu,
                       bias=cols[0:kk, 16 + NC1 + i:17 + NC1 + i])
                px.stt("vector", hb[0:kk, i * TB:(i + 1) * TB], ba[0:kk, :],
                       cols[0:kk, 16 + i:17 + i], gg[0:kk, :],
                       ALU.add, ALU.mult)
            for m in range(NKD):
                bank = g.pool[4 + m % 2]
                for i, (k0, kk) in enumerate(kchunks(DI)):
                    px.mm(bank[:, :],
                          w2[0:kk, i * D + m * 128:i * D + (m + 1) * 128],
                          hb[0:kk, i * TB:(i + 1) * TB],
                          start=(i == 0), stop=(i == NC1 - 1))
                px.dma("sync", outb[:, :], src[m * 128:(m + 1) * 128, cs])
                px.stt("vector", outb[:, :], bank[:, :],
                       cols[:, 16 + 2 * NC1 + m:17 + 2 * NC1 + m],
                       outb[:, :], ALU.add, ALU.add)
                px.dma("sync", dst[m * 128:(m + 1) * 128, cs], outb[:, :])
        px.barrier()



def stage_combine_ln(px, nc, g, xsrc, ssrc, gsrc, x1_dst, xnb_dst, xnf_dst,
                     name):
    with (
        nc.sbuf_tensor(name + "_x", [128, NKD * TB], F32) as xt,
        nc.sbuf_tensor(name + "_s", [128, TB], F32) as st_,
        nc.sbuf_tensor(name + "_g", [128, TB], F32) as gt_,
        nc.sbuf_tensor(name + "_xb", [128, NKD * TB], BF16) as xbt,
        nc.sbuf_tensor(name + "_sq", [128, NKD * TB], BF16) as sqt,
        nc.sbuf_tensor(name + "_row", [1, 6 * TB], F32) as row,
        nc.sbuf_tensor(name + "_o", [128, TB], F32) as ot,
        nc.sbuf_tensor(name + "_ob", [128, TB], BF16) as otb,
    ):
        SUM, SQ, VAR, RSTD = 0, TB, 2 * TB, 3 * TB
        for tb in range(NTB):
            cs = slice(tb * TB, (tb + 1) * TB)
            for k in range(NKD):
                sl = slice(k * TB, (k + 1) * TB)
                r = slice(k * 128, (k + 1) * 128)
                px.dma("sync", xt[:, sl], xsrc[r, cs])
                px.dma("sync", st_[:, :], ssrc[r, cs])
                px.dma("sync", gt_[:, :], gsrc[r, cs])
                px.act(ot[:, :], gt_[:, :], AF.Silu)
                px.tt("vector", ot[:, :], ot[:, :], st_[:, :], ALU.mult)
                px.tt("vector", xt[:, sl], xt[:, sl], ot[:, :], ALU.add)
                px.dma("gpsimd", x1_dst[r, cs], xt[:, sl])
                px.copy("scalar", xbt[:, sl], xt[:, sl])
                px.act(sqt[:, sl], xbt[:, sl], AF.Square)
            ps_s = g.pool[0]
            ps_q = g.pool[1]
            for k in range(NKD):
                px.mm(ps_s[0:1, :], g.ones_col_bf[:, 0:1],
                      xbt[:, k * TB:(k + 1) * TB],
                      start=(k == 0), stop=(k == NKD - 1))
            for k in range(NKD):
                px.mm(ps_q[0:1, :], g.ones_col_bf[:, 0:1],
                      sqt[:, k * TB:(k + 1) * TB],
                      start=(k == 0), stop=(k == NKD - 1))
            px.act(row[:, SUM:SUM + TB], ps_s[0:1, :], AF.Copy, scale=1.0 / D)
            px.act(row[:, SQ:SQ + TB], ps_q[0:1, :], AF.Copy, scale=1.0 / D)
            px.tt("vector", row[:, VAR:VAR + TB], row[:, SUM:SUM + TB],
                  row[:, SUM:SUM + TB], ALU.mult)
            px.tt("vector", row[:, VAR:VAR + TB], row[:, SQ:SQ + TB],
                  row[:, VAR:VAR + TB], ALU.subtract)
            px.act(row[:, RSTD:RSTD + TB], row[:, VAR:VAR + TB], AF.Ln,
                   bias=1e-5)
            px.act(row[:, RSTD:RSTD + TB], row[:, RSTD:RSTD + TB], AF.Exp,
                   scale=-0.5)
            ps_m = g.pool[2]
            ps_r = g.pool[3]
            px.mm(ps_m[:, :], g.ones1_f32[:, :], row[:, SUM:SUM + TB])
            px.mm(ps_r[:, :], g.ones1_f32[:, :], row[:, RSTD:RSTD + TB])
            for k in range(NKD):
                sl = slice(k * TB, (k + 1) * TB)
                r = slice(k * 128, (k + 1) * 128)
                px.tt("vector", ot[:, :], xt[:, sl], ps_m[:, :], ALU.subtract)
                px.tt("vector", ot[:, :], ot[:, :], ps_r[:, :], ALU.mult)
                px.copy("scalar", otb[:, :], ot[:, :])
                if xnf_dst is not None:
                    px.dma("sync", xnf_dst[r, cs], ot[:, :])
                px.dma("gpsimd", xnb_dst[r, cs], otb[:, :])
        px.barrier()



def stage_m_proj(px, nc, g):
    with (
        nc.sbuf_tensor("mp_w", [128, 8 * 4096], BF16) as w,
        nc.sbuf_tensor("mp_x", [128, 8 * TB], BF16) as xt,
        nc.sbuf_tensor("mp_o", [128, 2 * TB], F32) as ot,
    ):
        load_w(px, w, g.m_inproj, D, 4096)
        for tb in range(NTB):
            cs = slice(tb * TB, (tb + 1) * TB)
            for k in range(NKD):
                px.dma("sync", xt[:, k * TB:(k + 1) * TB],
                       g.xn1_bf[k * 128:(k + 1) * 128, cs])
            for m in range(32):
                bank = g.pool[m % 4]
                for k in range(NKD):
                    px.mm(bank[:, :],
                          w[:, k * 4096 + m * 128:k * 4096 + (m + 1) * 128],
                          xt[:, k * TB:(k + 1) * TB],
                          start=(k == 0), stop=(k == NKD - 1))
                o = ot[:, (m % 2) * TB:(m % 2 + 1) * TB]
                px.act(o, bank[:, :], AF.Copy)
                dst = g.xc_f32 if m < 16 else g.z_f32
                mm = m % 16
                px.dma("sync", dst[mm * 128:(mm + 1) * 128, cs], o)
        px.barrier()


def stage_m_conv(px, nc, g):
    with (
        nc.sbuf_tensor("mc_w", [128, 16 * 5], F32) as wc,
        nc.sbuf_tensor("mc_x", [128, TB + 6], F32) as xt,
        nc.sbuf_tensor("mc_a", [128, TB], F32) as acc,
        nc.sbuf_tensor("mc_ob", [128, TB], BF16) as ob,
    ):
        for pg in range(16):
            r = slice(pg * 128, (pg + 1) * 128)
            px.dma("sync", wc[:, pg * 5:pg * 5 + 4], g.m_convw[r, :])
            px.dma("sync", wc[:, pg * 5 + 4:pg * 5 + 5], g.m_convb[r][:, None])
        for d in range(2):
            for pg in range(16):
                r = slice(pg * 128, (pg + 1) * 128)
                for tb in range(NTB):
                    cs0 = tb * TB
                    lo = cs0 - 3 if d == 0 else cs0
                    hi = cs0 + TB if d == 0 else cs0 + TB + 3
                    lo_c = max(0, lo)
                    hi_c = min(T, hi)
                    if lo < 0:
                        px.memset("vector", xt[:, 0:lo_c - lo], 0)
                    if hi > T:
                        px.memset("vector", xt[:, TB + 3 - (hi - hi_c):TB + 3], 0)
                    px.dma("sync", xt[:, lo_c - lo:lo_c - lo + hi_c - lo_c],
                           g.xc_f32[r, lo_c:hi_c])
                    for i in range(4):
                        off = i if d == 0 else 3 - i
                        xs_ = xt[:, off:off + TB]
                        if i == 0:
                            px.ts("vector", acc[:, :], xs_,
                                  wc[:, pg * 5:pg * 5 + 1], None, ALU.mult)
                        else:
                            px.stt("vector", acc[:, :], xs_,
                                   wc[:, pg * 5 + i:pg * 5 + i + 1], acc[:, :],
                                   ALU.mult, ALU.add)
                    px.act(acc[:, :], acc[:, :], AF.Silu,
                           bias=wc[:, pg * 5 + 4:pg * 5 + 5])
                    px.copy("vector", ob[:, :], acc[:, :])
                    px.dma("sync", g.xcs_f32[d][r, cs0:cs0 + TB], acc[:, :])
                    px.dma("gpsimd", g.xcs_bf[d][r, cs0:cs0 + TB], ob[:, :])
        px.barrier()


def stage_m_scan(px, nc, g):
    DT_RANK, DS = 64, 16
    specs = [
        ("ms_xw", [128, 16 * 96], BF16), ("ms_dtw", [64, 2048], BF16),
        ("ms_cols", [128, 16 * 2], F32), ("ms_A", [128, 256], F32),
        ("ms_selBC", [96, 32 * 128], F32), ("ms_xcb", [128, 16 * TB], BF16),
        ("ms_xcf", [128, 16 * TB], F32), ("ms_dblf", [96, TB], F32),
        ("ms_dtb", [64, TB], BF16), ("ms_delta", [128, 16 * TB], F32),
        ("ms_du", [128, 16 * TB], F32), ("ms_Bs", [128, TB], F32),
        ("ms_Cs", [128, TB], F32), ("ms_a", [128, TB], F32),
        ("ms_b", [128, TB], F32), ("ms_h", [128, TB], F32),
        ("ms_y", [128, 16 * TB], F32), ("ms_carry", [128, 256], F32),
        ("ms_z", [128, TB], F32), ("ms_t", [128, TB], F32),
        ("ms_yc", [128, TB], F32),
    ]
    with ExitStack() as _st:
        (xw, dtw, cols, Asb, selBC, xcb, xcf, dblf, dtb, dlt, du, Bs, Cs,
         ab, bb_, hbuf, yb, carry, zb, tb_, ycb) = [
            _st.enter_context(nc.sbuf_tensor(n, sh, dt)) for (n, sh, dt) in specs]
        load_w(px, xw, g.m_xprojw, 2048, 96)
        px.dma("gpsimd", dtw[:, :], g.m_dtw[:, :])
        for pg in range(16):
            r = slice(pg * 128, (pg + 1) * 128)
            px.dma("sync", cols[:, pg:pg + 1], g.m_dtb[r][:, None])
            px.dma("sync", cols[:, 16 + pg:17 + pg], g.m_D[r][:, None])
            px.dma("sync", Asb[:, pg * 16:(pg + 1) * 16], g.m_Alog[r, :])
        px.act(Asb[:, :], Asb[:, :], AF.Exp)
        px.ts("vector", Asb[:, :], Asb[:, :], -1.0, None, ALU.mult)
        px.dma("sync", selBC[:, :], g.c_selBC[:, :])
        for d in range(getattr(g, "ms_dirs", 2)):
            px.memset("vector", carry[:, :], 0)
            crange = range(NTB) if d == 0 else range(NTB - 1, -1, -1)
            crange = list(crange)[:getattr(g, "ms_chunks", NTB)]
            for c in crange:
                cs = slice(c * TB, (c + 1) * TB)
                for pg in range(16):
                    px.dma("sync", xcb[:, pg * TB:(pg + 1) * TB],
                           g.xcs_bf[d][pg * 128:(pg + 1) * 128, cs])
                    px.dma("sync", xcf[:, pg * TB:(pg + 1) * TB],
                           g.xcs_f32[d][pg * 128:(pg + 1) * 128, cs])
                bank = g.pool[0]
                for k in range(16):
                    px.mm(bank[0:96, :], xw[:, k * 96:(k + 1) * 96],
                          xcb[:, k * TB:(k + 1) * TB],
                          start=(k == 0), stop=(k == 15))
                px.act(dblf[:, :], bank[0:96, :], AF.Copy)
                px.copy("scalar", dtb[:, :], bank[0:64, :])
                if getattr(g, "ms_upto", 5) <= 1:
                    continue
                for m in range(16):
                    bank2 = g.pool[1 + m % 2]
                    px.mm(bank2[:, :], dtw[:, m * 128:(m + 1) * 128],
                          dtb[:, :], start=True, stop=True)
                    px.ts("vector", yb[:, m * TB:(m + 1) * TB], bank2[:, :],
                          cols[:, m:m + 1], None, ALU.add)
                ZW = 16 * TB
                px.act(dlt[:, :], yb[:, :ZW], AF.Abs)
                px.act(dlt[:, :], dlt[:, :], AF.Exp, scale=-1.0)
                px.act(dlt[:, :], dlt[:, :], AF.Ln, bias=1.0)
                px.ts("vector", du[:, :], yb[:, :ZW], 0.0, None, ALU.max)
                px.tt("vector", dlt[:, :], dlt[:, :], du[:, :], ALU.add)
                px.tt("gpsimd", du[:, :], dlt[:, :], xcf[:, :], ALU.mult)
                for n in range(DS):
                    bB = g.pool[4]
                    bC = g.pool[5]
                    px.mm(bB[:, :], selBC[:, (2 * n) * 128:(2 * n + 1) * 128],
                          dblf[:, :], start=True, stop=True)
                    px.mm(bC[:, :], selBC[:, (2 * n + 1) * 128:(2 * n + 2) * 128],
                          dblf[:, :], start=True, stop=True)
                    px.act(Bs[:, :], bB[:, :], AF.Copy)
                    px.act(Cs[:, :], bC[:, :], AF.Copy)
                    for gd in range(16):
                        dsl = dlt[:, gd * TB:(gd + 1) * TB]
                        px.act(ab[:, :], dsl, AF.Exp,
                               scale=Asb[:, gd * 16 + n:gd * 16 + n + 1])
                        px.tt("gpsimd", bb_[:, :], du[:, gd * TB:(gd + 1) * TB],
                              Bs[:, :], ALU.mult)
                        cc = carry[:, gd * 16 + n:gd * 16 + n + 1]
                        if getattr(g, "ms_upto", 5) <= 3:
                            continue
                        if d == 0:
                            px.scan(hbuf[:, :], ab[:, :], bb_[:, :], cc)
                            px.copy("vector", cc, hbuf[:, TB - 1:TB])
                        else:
                            ra = BAP(ab, TB - 1, [[TB, 128], [-1, TB]])
                            rb2 = BAP(bb_, TB - 1, [[TB, 128], [-1, TB]])
                            rh = BAP(hbuf, TB - 1, [[TB, 128], [-1, TB]])
                            px.scan(rh, ra, rb2, cc)
                            px.copy("vector", cc, hbuf[:, 0:1])
                        ysl = yb[:, gd * TB:(gd + 1) * TB]
                        if getattr(g, "ms_upto", 5) <= 4:
                            continue
                        if n == 0:
                            px.tt("vector", ysl, hbuf[:, :], Cs[:, :], ALU.mult)
                        else:
                            px.tt("vector", tb_[:, :], hbuf[:, :], Cs[:, :],
                                  ALU.mult)
                            px.tt("gpsimd", ysl, ysl, tb_[:, :], ALU.add)
                for gd in range(16 if getattr(g, "ms_upto", 5) >= 5 else 0):
                    r = slice(gd * 128, (gd + 1) * 128)
                    px.stt("vector", tb_[:, :], xcf[:, gd * TB:(gd + 1) * TB],
                           cols[:, 16 + gd:17 + gd], yb[:, gd * TB:(gd + 1) * TB],
                           ALU.mult, ALU.add)
                    px.dma("sync", zb[:, :], g.z_f32[r, cs])
                    px.act(zb[:, :], zb[:, :], AF.Silu)
                    px.tt("vector", ycb[:, :], tb_[:, :], zb[:, :], ALU.mult)
                    if d == 0:
                        px.dma("gpsimd", g.ycomb_f32[r, cs], ycb[:, :])
                    else:
                        px.dma("gpsimd", g.ycomb_f32[r, cs], ycb[:, :],
                               accum_op=ALU.add)
        px.barrier()


def stage_m_out(px, nc, g):
    with (
        nc.sbuf_tensor("mo_w", [128, 16 * D], BF16) as w,
        nc.sbuf_tensor("mo_y", [128, 16 * TB], BF16) as yt,
        nc.sbuf_tensor("mo_m", [128, 8 * TB], F32) as mt,
        nc.sbuf_tensor("mo_sq", [128, 8 * TB], BF16) as sqt,
        nc.sbuf_tensor("mo_row", [1, 4 * TB], F32) as row,
        nc.sbuf_tensor("mo_cols", [128, 16], F32) as cols,
        nc.sbuf_tensor("mo_ob", [128, TB], BF16) as ob,
    ):
        load_w(px, w, g.m_outproj, 2048, D)
        load_vec_cols(px, cols, g.m_lng, D, 0)
        load_vec_cols(px, cols, g.m_lnb, D, 8)
        SUM, SQ, VAR, RSTD = 0, TB, 2 * TB, 3 * TB
        for tb in range(NTB):
            cs = slice(tb * TB, (tb + 1) * TB)
            for k in range(16):
                px.dma("gpsimd", yt[:, k * TB:(k + 1) * TB],
                       g.ycomb_f32[k * 128:(k + 1) * 128, cs])
            for m in range(NKD):
                bank = g.pool[m % 4]
                for k in range(16):
                    px.mm(bank[:, :], w[:, k * D + m * 128:k * D + (m + 1) * 128],
                          yt[:, k * TB:(k + 1) * TB],
                          start=(k == 0), stop=(k == 15))
                sl = slice(m * TB, (m + 1) * TB)
                px.act(mt[:, sl], bank[:, :], AF.Copy)
                px.act(sqt[:, sl], bank[:, :], AF.Square)
            ps_s = g.pool[4]
            ps_q = g.pool[5]
            for k in range(NKD):
                px.mm(ps_s[0:1, :], g.ones_col_bf[:, 0:1],
                      sqt[:, k * TB:(k + 1) * TB],
                      start=(k == 0), stop=(k == NKD - 1))
            for k in range(NKD):
                px.copy("scalar", sqt[:, k * TB:(k + 1) * TB],
                        mt[:, k * TB:(k + 1) * TB])
            for k in range(NKD):
                px.mm(ps_q[0:1, :], g.ones_col_bf[:, 0:1],
                      sqt[:, k * TB:(k + 1) * TB],
                      start=(k == 0), stop=(k == NKD - 1))
            px.act(row[:, SUM:SUM + TB], ps_q[0:1, :], AF.Copy, scale=1.0 / D)
            px.act(row[:, SQ:SQ + TB], ps_s[0:1, :], AF.Copy, scale=1.0 / D)
            px.tt("vector", row[:, VAR:VAR + TB], row[:, SUM:SUM + TB],
                  row[:, SUM:SUM + TB], ALU.mult)
            px.tt("vector", row[:, VAR:VAR + TB], row[:, SQ:SQ + TB],
                  row[:, VAR:VAR + TB], ALU.subtract)
            px.act(row[:, RSTD:RSTD + TB], row[:, VAR:VAR + TB], AF.Ln,
                   bias=1e-5)
            px.act(row[:, RSTD:RSTD + TB], row[:, RSTD:RSTD + TB], AF.Exp,
                   scale=-0.5)
            ps_m = g.pool[6]
            ps_r = g.pool[7]
            px.mm(ps_m[:, :], g.ones1_f32[:, :], row[:, SUM:SUM + TB])
            px.mm(ps_r[:, :], g.ones1_f32[:, :], row[:, RSTD:RSTD + TB])
            for k in range(NKD):
                sl = slice(k * TB, (k + 1) * TB)
                px.tt("vector", mt[:, sl], mt[:, sl], ps_m[:, :], ALU.subtract)
                px.tt("vector", mt[:, sl], mt[:, sl], ps_r[:, :], ALU.mult)
                px.ts("vector", ob[:, :], mt[:, sl], cols[:, k:k + 1],
                      cols[:, 8 + k:9 + k], ALU.mult, ALU.add)
                px.dma("gpsimd", g.mh_bf[k * 128:(k + 1) * 128, cs], ob[:, :])
        px.barrier()


def stage_m_ffn(px, nc, g):
    with (
        nc.sbuf_tensor("mf_w1", [128, 8 * 4096], BF16) as w1,
        nc.sbuf_tensor("mf_w2", [128, 32 * D], BF16) as w2,
        nc.sbuf_tensor("mf_cols", [128, 32 + 8], F32) as cols,
        nc.sbuf_tensor("mf_x", [128, 8 * TB], BF16) as xt,
        nc.sbuf_tensor("mf_h", [128, 32 * TB], BF16) as hb,
        nc.sbuf_tensor("mf_o", [128, TB], F32) as ob,
    ):
        load_w(px, w1, g.m_ffw1, D, 4096)
        load_w(px, w2, g.m_ffw2, 4096, D)
        load_vec_cols(px, cols, g.m_ffb1, 4096, 0)
        load_vec_cols(px, cols, g.m_ffb2, D, 32)
        for tb in range(NTB):
            cs = slice(tb * TB, (tb + 1) * TB)
            for k in range(NKD):
                px.dma("sync", xt[:, k * TB:(k + 1) * TB],
                       g.mh_bf[k * 128:(k + 1) * 128, cs])
            for m in range(32):
                bank = g.pool[m % 4]
                for k in range(NKD):
                    px.mm(bank[:, :],
                          w1[:, k * 4096 + m * 128:k * 4096 + (m + 1) * 128],
                          xt[:, k * TB:(k + 1) * TB],
                          start=(k == 0), stop=(k == NKD - 1))
                px.act(hb[:, m * TB:(m + 1) * TB], bank[:, :], AF.Gelu,
                       bias=cols[:, m:m + 1])
            for m in range(NKD):
                bank = g.pool[4 + m % 2]
                for k in range(32):
                    px.mm(bank[:, :], w2[:, k * D + m * 128:k * D + (m + 1) * 128],
                          hb[:, k * TB:(k + 1) * TB],
                          start=(k == 0), stop=(k == 31))
                px.ts("vector", ob[:, :], bank[:, :], cols[:, 32 + m:33 + m],
                      None, ALU.add)
                px.dma("sync", g.s1_f32[m * 128:(m + 1) * 128, cs], ob[:, :])
        px.barrier()



def stage_moe_route(px, nc, g):
    with (
        nc.sbuf_tensor("mr_gw", [128, 8 * 8], BF16) as gw,
        nc.sbuf_tensor("mr_se", [128, 8], BF16) as se,
        nc.sbuf_tensor("mr_x", [128, 8 * TB], BF16) as xt,
        nc.sbuf_tensor("mr_lg", [8, TB], BF16) as lg,
        nc.sbuf_tensor("mr_tm", [128, 8], F32) as tm,
        nc.sbuf_tensor("mr_m1", [128, 4], F32) as m1,
        nc.sbuf_tensor("mr_sc", [128, 8], F32) as sc,
        nc.sbuf_tensor("mr_wt", [128, 8], BF16) as wtb,
        nc.sbuf_tensor("mr_wfm", [8, TB], F32) as wfm,
        nc.sbuf_tensor("mr_sw", [1, TB], F32) as swr,
    ):
        load_w(px, gw, g.e_gatew, D, 8)
        load_w(px, se, g.e_segatew, D, 1)
        for tb in range(NTB):
            cs = slice(tb * TB, (tb + 1) * TB)
            for k in range(NKD):
                px.dma("sync", xt[:, k * TB:(k + 1) * TB],
                       g.xn1_bf[k * 128:(k + 1) * 128, cs])
            blog = g.pool[0]
            for k in range(NKD):
                px.mm(blog[0:8, :], gw[:, k * 8:(k + 1) * 8],
                      xt[:, k * TB:(k + 1) * TB],
                      start=(k == 0), stop=(k == NKD - 1))
            bse = g.pool[1]
            for k in range(NKD):
                px.mm(bse[0:1, :], se[:, k:k + 1], xt[:, k * TB:(k + 1) * TB],
                      start=(k == 0), stop=(k == NKD - 1))
            px.act(swr[:, :], bse[0:1, :], AF.Sigmoid)
            px.dma("sync", g.sw_f32[0:1, cs], swr[:, :])
            px.copy("vector", lg[:, :], blog[0:8, :])
            for ts_ in range(4):
                pT = g.pool[2][:, :].bitcast(BF16)
                px.transpose(pT[:, 0:8], lg[:, ts_ * 128:(ts_ + 1) * 128],
                             g.ident_bf[0:8, 0:8])
                px.copy("vector", tm[:, 0:8], pT[:, 0:8])
                px.red("vector", m1[:, 0:1], tm[:, 0:8], ALU.max)
                px.ts("vector", tm[:, 0:8], tm[:, 0:8], m1[:, 0:1], None,
                      ALU.subtract)
                px.act(tm[:, 0:8], tm[:, 0:8], AF.Exp, accum_out=m1[:, 1:2])
                px.recip(m1[:, 1:2], m1[:, 1:2])
                px.ts("vector", sc[:, :], tm[:, 0:8], m1[:, 1:2], None,
                      ALU.mult)
                px.red("vector", m1[:, 2:3], sc[:, :], ALU.max)
                px.ts("vector", tm[:, 0:8], sc[:, :], m1[:, 2:3], None,
                      ALU.is_ge)
                px.stt("vector", tm[:, 0:8], tm[:, 0:8], -1e30, sc[:, :],
                       ALU.mult, ALU.add)
                px.red("vector", m1[:, 3:4], tm[:, 0:8], ALU.max)
                px.ts("vector", tm[:, 0:8], sc[:, :], m1[:, 3:4], None,
                      ALU.is_ge)
                px.tt("vector", sc[:, :], sc[:, :], tm[:, 0:8], ALU.mult)
                px.copy("vector", wtb[:, :], sc[:, :])
                pT2 = g.pool[3][:, :].bitcast(BF16)
                px.transpose(pT2[0:8, 0:128], wtb[:, :], g.ident_bf[:, :])
                px.copy("vector", wfm[:, ts_ * 128:(ts_ + 1) * 128],
                        pT2[0:8, 0:128])
            px.dma("sync", g.w_fm_f32[:, cs], wfm[:, :])
        px.barrier()


def stage_moe_expert(px, nc, g, e):
    I = 1408
    NCI = I // 128
    with (
        nc.sbuf_tensor(f"me{e}_wgu", [128, 8 * 2816], BF16) as wgu,
        nc.sbuf_tensor(f"me{e}_wd", [128, NCI * D], BF16) as wd,
        nc.sbuf_tensor(f"me{e}_x", [128, 8 * TB], BF16) as xt,
        nc.sbuf_tensor(f"me{e}_w", [8, TB], F32) as wfm,
        nc.sbuf_tensor(f"me{e}_h", [128, NCI * TB], BF16) as hb,
        nc.sbuf_tensor(f"me{e}_gs", [128, TB], BF16) as gs,
        nc.sbuf_tensor(f"me{e}_t", [128, TB], F32) as tmp,
        nc.sbuf_tensor(f"me{e}_o", [128, TB], F32) as ob,
    ):
        load_w(px, wgu, g.e_gup[e], D, 2816)
        load_w(px, wd, g.e_down[e], I, D)
        for tb in range(NTB):
            cs = slice(tb * TB, (tb + 1) * TB)
            for k in range(NKD):
                px.dma("sync", xt[:, k * TB:(k + 1) * TB],
                       g.xn1_bf[k * 128:(k + 1) * 128, cs])
            px.dma("sync", wfm[:, :], g.w_fm_f32[:, cs])
            web = g.pool[6]
            px.mm(web[:, :], g.c_sel8[:, e * 128:(e + 1) * 128], wfm[:, :],
                  start=True, stop=True)
            for i in range(NCI):
                bg = g.pool[2 * (i % 2)]
                bu = g.pool[2 * (i % 2) + 1]
                for k in range(NKD):
                    px.mm(bg[:, :],
                          wgu[:, k * 2816 + i * 128:k * 2816 + (i + 1) * 128],
                          xt[:, k * TB:(k + 1) * TB],
                          start=(k == 0), stop=(k == NKD - 1))
                for k in range(NKD):
                    px.mm(bu[:, :],
                          wgu[:, k * 2816 + I + i * 128:k * 2816 + I + (i + 1) * 128],
                          xt[:, k * TB:(k + 1) * TB],
                          start=(k == 0), stop=(k == NKD - 1))
                px.act(gs[:, :], bg[:, :], AF.Silu)
                px.tt("vector", tmp[:, :], gs[:, :], bu[:, :], ALU.mult)
                px.tt("vector", hb[:, i * TB:(i + 1) * TB], tmp[:, :],
                      web[:, :], ALU.mult)
            for m in range(NKD):
                bank = g.pool[4 + m % 2]
                for i in range(NCI):
                    px.mm(bank[:, :], wd[:, i * D + m * 128:i * D + (m + 1) * 128],
                          hb[:, i * TB:(i + 1) * TB],
                          start=(i == 0), stop=(i == NCI - 1))
                px.act(ob[:, :], bank[:, :], AF.Copy)
                if e == 0:
                    px.dma("gpsimd", g.moe_acc[m * 128:(m + 1) * 128, cs],
                           ob[:, :])
                else:
                    px.dma("gpsimd", g.moe_acc[m * 128:(m + 1) * 128, cs],
                           ob[:, :], accum_op=ALU.add)
        px.barrier()


def stage_moe_shared(px, nc, g):
    SI = 2816
    NCS = SI // 128
    with (
        nc.sbuf_tensor("msh_wg", [128, 8 * SI], BF16) as wgt,
        nc.sbuf_tensor("msh_wu", [128, 8 * SI], BF16) as wut,
        nc.sbuf_tensor("msh_wd", [128, NCS * D], BF16) as wdt,
        nc.sbuf_tensor("msh_x", [128, 8 * TB], BF16) as xt,
        nc.sbuf_tensor("msh_h", [128, NCS * TB], BF16) as hb,
        nc.sbuf_tensor("msh_gs", [128, TB], BF16) as gs,
        nc.sbuf_tensor("msh_sw", [1, TB], F32) as swr,
        nc.sbuf_tensor("msh_t", [128, TB], F32) as tmp,
        nc.sbuf_tensor("msh_x1", [128, TB], F32) as x1t,
        nc.sbuf_tensor("msh_s1", [128, TB], F32) as s1t,
        nc.sbuf_tensor("msh_o", [128, TB], F32) as ob,
    ):
        load_w(px, wgt, g.e_shgate, D, SI)
        load_w(px, wut, g.e_shup, D, SI)
        load_w(px, wdt, g.e_shdown, SI, D)
        for tb in range(NTB):
            cs = slice(tb * TB, (tb + 1) * TB)
            for k in range(NKD):
                px.dma("sync", xt[:, k * TB:(k + 1) * TB],
                       g.xn1_bf[k * 128:(k + 1) * 128, cs])
            px.dma("sync", swr[:, :], g.sw_f32[0:1, cs])
            bsw = g.pool[6]
            px.mm(bsw[:, :], g.ones1_f32[:, :], swr[:, :], start=True,
                  stop=True)
            for i in range(NCS):
                bg = g.pool[2 * (i % 2)]
                bu = g.pool[2 * (i % 2) + 1]
                for k in range(NKD):
                    px.mm(bg[:, :],
                          wgt[:, k * SI + i * 128:k * SI + (i + 1) * 128],
                          xt[:, k * TB:(k + 1) * TB],
                          start=(k == 0), stop=(k == NKD - 1))
                for k in range(NKD):
                    px.mm(bu[:, :],
                          wut[:, k * SI + i * 128:k * SI + (i + 1) * 128],
                          xt[:, k * TB:(k + 1) * TB],
                          start=(k == 0), stop=(k == NKD - 1))
                px.act(gs[:, :], bg[:, :], AF.Silu)
                px.tt("vector", tmp[:, :], gs[:, :], bu[:, :], ALU.mult)
                px.tt("vector", hb[:, i * TB:(i + 1) * TB], tmp[:, :],
                      bsw[:, :], ALU.mult)
            for m in range(NKD):
                bank = g.pool[4 + m % 2]
                for i in range(NCS):
                    px.mm(bank[:, :],
                          wdt[:, i * D + m * 128:i * D + (m + 1) * 128],
                          hb[:, i * TB:(i + 1) * TB],
                          start=(i == 0), stop=(i == NCS - 1))
                r = slice(m * 128, (m + 1) * 128)
                px.dma("sync", tmp[:, :], g.moe_acc[r, cs])
                px.tt("vector", tmp[:, :], tmp[:, :], bank[:, :], ALU.add)
                px.act(tmp[:, :], tmp[:, :], AF.Silu)
                px.dma("sync", s1t[:, :], g.s1_f32[r, cs])
                px.dma("sync", x1t[:, :], g.x1_f32[r, cs])
                px.tt("vector", tmp[:, :], tmp[:, :], s1t[:, :], ALU.mult)
                px.tt("vector", ob[:, :], x1t[:, :], tmp[:, :], ALU.add)
                px.dma("sync", g.out_fm[r, cs], ob[:, :])
        px.barrier()




def build_program(meta):
    nc = bass.Bass("TRN2", target_bir_lowering=False)
    g = Ctx()
    g.pool, g._pool_guards = make_psum_pool(nc)

    def ein(name, shape, dt=F32):
        return nc.dram_tensor(name, list(shape), dt, kind="ExternalInput").ap()

    def scratch(name, shape, dt=F32):
        return nc.dram_tensor(name, list(shape), dt, kind="Internal").ap()

    def eout(name, shape, dt=F32):
        return nc.dram_tensor(name, list(shape), dt, kind="ExternalOutput").ap()

    g.x_fm = ein("x_fm", [D, T])
    g.c_ident = ein("c_ident", [128, 128])
    g.c_indP = ein("c_indP", [8, 128, 16])
    g.c_indT = ein("c_indT", [8, 16, 128])
    g.c_ones1 = ein("c_ones1", [1, 128])
    g.c_ones_col = ein("c_ones_col", [128, 1])

    g._const_guards = []

    def sres(name, shape, dt):
        gcm = nc.sbuf_tensor(name, list(shape), dt)
        t = gcm.__enter__()
        g._const_guards.append(gcm)
        return t

    g.ident_bf = sres("ident_bf", [128, 128], BF16)
    g.indP_bf = sres("indP_bf", [128, 8 * 16], BF16)
    g.indT_bf = sres("indT_bf", [16, 8 * 128], BF16)
    g.indT_f32 = sres("indT_f32", [16, 8 * 128], F32)
    g.ones1_f32 = sres("ones1_f32", [1, 128], F32)
    g.ones1_bf = sres("ones1_bf", [1, 128], BF16)
    g.ones_col_bf = sres("ones_col_bf", [128, 1], BF16)

    px = Px(nc)

    ACT_CONSTS = [1e-5, 1e-6, 1.0 / D, -0.5, 0.5, -1.0, 2.0, 8.0,
                  1.0 / 64, 1e-12, -1e30]
    g.constcol = sres("constcol", [128, len(ACT_CONSTS)], F32)
    for i, v in enumerate(ACT_CONSTS):
        px.memset("vector", g.constcol[:, i:i + 1], v)
        nc.const_aps.register(F32, v, g.constcol[:, i:i + 1])
    px.dma("gpsimd", g.ident_bf[:, :], g.c_ident[:, :])
    for pg in range(8):
        px.dma("gpsimd", g.indP_bf[:, pg * 16:(pg + 1) * 16], g.c_indP[pg])
        px.dma("gpsimd", g.indT_bf[:, pg * 128:(pg + 1) * 128], g.c_indT[pg])
        px.dma("sync", g.indT_f32[:, pg * 128:(pg + 1) * 128], g.c_indT[pg])
    px.dma("sync", g.ones1_f32[:, :], g.c_ones1[:, :])
    px.dma("gpsimd", g.ones1_bf[:, :], g.c_ones1[:, :])
    px.dma("gpsimd", g.ones_col_bf[:, :], g.c_ones_col[:, :])
    px.barrier()

    stages = meta.get("stages", ["ln0"])
    g.offsets = meta.get("offsets", [])
    g.n_vm = meta.get("n_vm", 0)

    g.xn_bf = scratch("xn_bf", [D, T], BF16)
    g.xn_f32 = scratch("xn_f32", [D, T])

    if "l0a_proj" in stages or "l0a_attn" in stages or "l0a_out" in stages:
        g.q_s = scratch("q_s", [D, T], BF16)
        g.k_s = scratch("k_s", [D, T], BF16)
        g.v_s = scratch("v_s", [D, T], BF16)
        g.gate_s = scratch("gate_s", [D, T], BF16)
        g.attng_s = scratch("attng_s", [D, T], BF16)
        g.s0_f32 = scratch("s0_f32", [D, T])
        g.xb_f32 = scratch("xb_f32", [D, T])
        g.xn2_bf = scratch("xn2_bf", [D, T], BF16)
    if "l0a_proj" in stages:
        g.w_qkv = ein("L0_attn_w_qkv", [D, 3 * D])
        g.w_gate = ein("L0_attn_w_gate", [D, D])
        g.norm1_scale = ein("L0_attn_norm1_scale", [D])
        g.b_gate = ein("L0_attn_b_gate", [D])
    if "l0a_attn" in stages:
        g.pb_row = ein("pb_row", [1, len(g.offsets) * 16])
        if g.n_vm:
            g.c_vmask = ein("c_vmask", [g.n_vm, 128, len(g.offsets)])
    if "l0a_out" in stages:
        g.w_out = ein("L0_attn_w_out", [D, D])
        g.norm2_scale = ein("L0_attn_norm2_scale", [D])
    if "l0a_ffn" in stages:
        g.w_ffn_gate = ein("L0_attn_w_ffn_gate", [D, 2816])
        g.w_ffn_up = ein("L0_attn_w_ffn_up", [D, 2816])
        g.w_ffn_down = ein("L0_attn_w_ffn_down", [2816, D])

    need_l0f = any(st.startswith("l0f") for st in stages)
    if need_l0f:
        g.f_Wqkv = ein("L0_ffn_Wqkv", [6, D, 3072])
        g.f_Wo = ein("L0_ffn_Wo", [6, D, D])
        g.f_qscale = ein("L0_ffn_q_scale", [6, 64])
        g.f_kscale = ein("L0_ffn_k_scale", [6, 64])
        g.f_lng = ein("L0_ffn_ln_g", [6, D])
        g.f_lnb = ein("L0_ffn_ln_b", [6, D])
        g.f_W1 = ein("L0_ffn_W1", [6, D, 5460])
        g.f_b1 = ein("L0_ffn_b1", [6, 5460])
        g.f_W2 = ein("L0_ffn_W2", [6, 2730, D])
        g.f_b2 = ein("L0_ffn_b2", [6, D])
        g.c_rot = ein("c_rot", [4, 128, T])
        g.c_R = ein("c_R", [128, 128])
        g.c_l0f_mask = ein("c_l0f_mask", [2, 2, 128, 512])
        g.c_R_bf = sres("c_R_bf", [128, 128], BF16)
        px.dma("gpsimd", g.c_R_bf[:, :], g.c_R[:, :])
        g.xl0 = scratch("xl0_f32", [D, T])
        g.xl1 = scratch("xl1_f32", [D, T])

    need_l1 = any(st.startswith(("comb", "m_", "moe")) for st in stages)
    if need_l1:
        g.x1_f32 = scratch("x1_f32", [D, T])
        g.xn1_bf = scratch("xn1_bf", [D, T], BF16)
        g.xn1_f32 = scratch("xn1_f32", [D, T])
        g.m_inproj = ein("L1_attn_in_proj_w", [D, 4096])
        g.m_convw = ein("L1_attn_conv_w", [2048, 4])
        g.m_convb = ein("L1_attn_conv_b", [2048])
        g.m_xprojw = ein("L1_attn_x_proj_w", [2048, 96])
        g.m_dtw = ein("L1_attn_dt_proj_w", [64, 2048])
        g.m_dtb = ein("L1_attn_dt_proj_b", [2048])
        g.m_Alog = ein("L1_attn_A_log", [2048, 16])
        g.m_D = ein("L1_attn_D", [2048])
        g.m_outproj = ein("L1_attn_out_proj_w", [2048, D])
        g.m_lng = ein("L1_attn_ln_g", [D])
        g.m_lnb = ein("L1_attn_ln_b", [D])
        g.m_ffw1 = ein("L1_attn_ff_w1", [D, 4096])
        g.m_ffb1 = ein("L1_attn_ff_b1", [4096])
        g.m_ffw2 = ein("L1_attn_ff_w2", [4096, D])
        g.m_ffb2 = ein("L1_attn_ff_b2", [D])
        g.c_selBC = ein("c_selBC", [96, 32 * 128])
        g.xc_f32 = scratch("xc_f32", [2048, T])
        g.z_f32 = scratch("z_f32", [2048, T])
        g.xcs_f32 = scratch("xcs_f32", [2, 2048, T])
        g.xcs_bf = scratch("xcs_bf", [2, 2048, T], BF16)
        g.ycomb_f32 = scratch("ycomb_f32", [2048, T])
        g.mh_bf = scratch("mh_bf", [D, T], BF16)
        g.s1_f32 = scratch("s1_f32", [D, T])
        g.e_gatew = ein("t_gate_w", [D, 8])
        g.e_segatew = ein("t_shared_expert_gate_w", [D, 1])
        g.e_gup = ein("t_experts_gate_up", [8, D, 2816])
        g.e_down = ein("t_experts_down", [8, 1408, D])
        g.e_shgate = ein("t_shared_gate_w", [D, 2816])
        g.e_shup = ein("t_shared_up_w", [D, 2816])
        g.e_shdown = ein("t_shared_down_w", [2816, D])
        g.c_sel8_d = ein("c_sel8", [8, 8 * 128])
        g.c_sel8 = sres("c_sel8_sb", [8, 8 * 128], F32)
        px.dma("sync", g.c_sel8[:, :], g.c_sel8_d[:, :])
        g.w_fm_f32 = scratch("w_fm_f32", [8, T])
        g.sw_f32 = scratch("sw_f32", [1, T])
        g.moe_acc = scratch("moe_acc", [D, T])
        g.out_fm = eout("out_fm", [D, T])

    for st in stages:
        if st == "ln0":
            stage_layernorm(px, nc, g, g.x_fm, g.xn_bf, g.xn_f32, "ln0")
        elif st == "l0a_proj":
            stage_l0a_proj(px, nc, g)
        elif st == "l0a_attn":
            stage_l0a_attn(px, nc, g)
        elif st == "l0a_out":
            stage_l0a_out(px, nc, g)
        elif st == "l0a_ffn":
            stage_l0a_ffn(px, nc, g)
        elif st.startswith("l0f") and st.endswith("a"):
            li = int(st[3:-1])
            src = g.xn_f32 if li == 0 else g.xl0
            stage_l0f_attn(px, nc, g, li, src, g.xl1)
        elif st.startswith("l0f") and st.endswith("f"):
            li = int(st[3:-1])
            stage_l0f_ffn(px, nc, g, li, g.xl1, g.xl0)
        elif st == "ln1x":
            stage_layernorm(px, nc, g, g.x_fm, g.xn1_bf, g.xn1_f32, "ln1x")
            for r0 in range(0, D, 128):
                px.dma("sync", g.x1_f32[r0:r0 + 128, :], g.x_fm[r0:r0 + 128, :])
            px.barrier()
        elif st == "comb0":
            stage_combine_ln(px, nc, g, g.x_fm, g.s0_f32, g.xl0,
                             g.x1_f32, g.xn1_bf, g.xn1_f32, "c0")
        elif st == "m_proj":
            stage_m_proj(px, nc, g)
        elif st == "m_conv":
            stage_m_conv(px, nc, g)
        elif st == "m_scan":
            stage_m_scan(px, nc, g)
        elif st == "m_out":
            stage_m_out(px, nc, g)
        elif st == "m_ffn":
            stage_m_ffn(px, nc, g)
        elif st == "moe_route":
            stage_moe_route(px, nc, g)
        elif st.startswith("moe_e"):
            stage_moe_expert(px, nc, g, int(st[5:]))
        elif st == "moe_shared":
            stage_moe_shared(px, nc, g)
        else:
            raise ValueError(st)

    for name in meta.get("taps", []):
        src = getattr(g, name)
        dst = eout("tap_" + name, list(src.tensor.shape), src.dtype)
        rows = int(src.tensor.shape[0])
        for r0 in range(0, rows, 128):
            rr = min(128, rows - r0)
            px.dma("sync", dst[r0:r0 + rr, :], src[r0:r0 + rr, :])
    px.barrier()
    emit_program(nc, px)
    return nc, g



ALL_STAGES = (["ln0", "l0a_proj", "l0a_attn", "l0a_out", "l0a_ffn"] +
              sum([[f"l0f{i}a", f"l0f{i}f"] for i in range(6)], []) +
              ["comb0", "m_proj", "m_conv", "m_scan", "m_out", "m_ffn",
               "moe_route"] + [f"moe_e{e}" for e in range(8)] + ["moe_shared"])


def kernel(**inputs):
    offsets = [int(v) for v in np.asarray(inputs["L0_attn_offsets"])]
    consts = build_consts(offsets)
    NO = len(offsets)
    mo = max(offsets)
    n_vm = (mo + 127) // 128
    vm = np.zeros((n_vm, 128, NO), np.float32)
    for i in range(n_vm):
        t = i * 128 + np.arange(128)
        vm[i] = np.where(t[:, None] >= np.array(offsets)[None, :], 0.0, -1e30)
    pb_row = np.ascontiguousarray(
        np.asarray(inputs["L0_attn_pos_bias"]).T.reshape(1, NO * 16))

    meta = {"stages": ALL_STAGES, "offsets": offsets, "n_vm": n_vm}
    nc, g = build_program(meta)

    base = {
        "c_ident": consts["ident"], "c_indP": consts["indP"],
        "c_indT": consts["indT"], "c_ones1": consts["ones1"],
        "c_ones_col": consts["ones_col"], "c_rot": consts["rot"],
        "c_R": consts["R"], "c_l0f_mask": consts["l0f_mask"],
        "c_selBC": consts["selBC"], "c_sel8": consts["sel8"],
        "pb_row": pb_row, "c_vmask": vm,
    }
    direct = ["L0_attn_norm1_scale", "L0_attn_norm2_scale", "L0_attn_w_qkv",
              "L0_attn_w_out", "L0_attn_w_gate", "L0_attn_b_gate",
              "L0_attn_w_ffn_gate", "L0_attn_w_ffn_up", "L0_attn_w_ffn_down",
              "L0_ffn_Wqkv", "L0_ffn_q_scale", "L0_ffn_k_scale", "L0_ffn_Wo",
              "L0_ffn_ln_g", "L0_ffn_ln_b", "L0_ffn_W1", "L0_ffn_b1",
              "L0_ffn_W2", "L0_ffn_b2", "L1_attn_in_proj_w", "L1_attn_conv_w",
              "L1_attn_conv_b", "L1_attn_x_proj_w", "L1_attn_dt_proj_w",
              "L1_attn_dt_proj_b", "L1_attn_A_log", "L1_attn_D",
              "L1_attn_out_proj_w", "L1_attn_ln_g", "L1_attn_ln_b",
              "L1_attn_ff_w1", "L1_attn_ff_b1", "L1_attn_ff_w2",
              "L1_attn_ff_b2"]
    for k in direct:
        base[k] = np.ascontiguousarray(inputs[k], dtype=np.float32)
    base["t_gate_w"] = np.ascontiguousarray(
        np.asarray(inputs["L1_ffn_gate_w"]).T, dtype=np.float32)
    base["t_shared_expert_gate_w"] = np.ascontiguousarray(
        np.asarray(inputs["L1_ffn_shared_expert_gate_w"]).T, dtype=np.float32)
    base["t_experts_gate_up"] = np.ascontiguousarray(
        np.asarray(inputs["L1_ffn_experts_gate_up"]).transpose(0, 2, 1),
        dtype=np.float32)
    base["t_experts_down"] = np.ascontiguousarray(
        np.asarray(inputs["L1_ffn_experts_down"]).transpose(0, 2, 1),
        dtype=np.float32)
    base["t_shared_gate_w"] = np.ascontiguousarray(
        np.asarray(inputs["L1_ffn_shared_gate_w"]).T, dtype=np.float32)
    base["t_shared_up_w"] = np.ascontiguousarray(
        np.asarray(inputs["L1_ffn_shared_up_w"]).T, dtype=np.float32)
    base["t_shared_down_w"] = np.ascontiguousarray(
        np.asarray(inputs["L1_ffn_shared_down_w"]).T, dtype=np.float32)

    x_full = np.asarray(inputs["L0_attn_x"])
    in_maps = []
    for c in range(8):
        mcore = dict(base)
        mcore["x_fm"] = np.ascontiguousarray(x_full[c].T, dtype=np.float32)
        in_maps.append(mcore)
    res = run_bass_kernel_spmd(nc, in_maps, core_ids=list(range(8)))
    out = np.stack([np.asarray(res.results[c]["out_fm"]).T for c in range(8)])
    return out.astype(np.float32)
